# version 16
# baseline (speedup 1.0000x reference)
import contextlib
import numpy as np

import concourse.bass as bass
import concourse.mybir as mybir
from concourse.bass_utils import run_bass_kernel_spmd

F32 = mybir.dt.float32
ALU = mybir.AluOpType
ACTF = mybir.ActivationFunctionType
AX = mybir.AxisListType

D = 1024
NST = 16
NEXP = 8
MOE_H = 2048
NSLOT = 64
DQ = 256
DH = 512
FCH = 4096
GRU = 3072
DT = D // 128

ENGINES = ("tensor", "vector", "scalar", "sync", "gpsimd")


class _Op:
    __slots__ = ("eng", "fn", "reads", "writes", "is_dma", "seq", "waits",
                 "need_drain", "inc", "force_inc", "lane")

    def __init__(self, eng, fn, reads, writes, is_dma, force_inc=False):
        self.eng, self.fn = eng, fn
        self.reads, self.writes = tuple(reads), tuple(writes)
        self.is_dma = is_dma
        self.seq = -1
        self.waits = []
        self.need_drain = False
        self.inc = False
        self.force_inc = force_inc
        self.lane = -1


class Plan:
    def __init__(self):
        self.per_eng = {e: [] for e in ENGINES}
        self.ops = []
        self.last_write = {}
        self.readers = {}

    def op(self, eng, fn, reads=(), writes=(), is_dma=False, force_inc=False):
        o = _Op(eng, fn, reads, writes, is_dma, force_inc)
        o.seq = len(self.per_eng[eng])
        self.per_eng[eng].append(o)
        self.ops.append(o)
        deps = []
        for r in o.reads:
            w = self.last_write.get(r)
            if w is not None:
                deps.append(w)
        for wn in o.writes:
            w = self.last_write.get(wn)
            if w is not None:
                deps.append(w)
            deps.extend(self.readers.get(wn, ()))
        best = {}
        for d in deps:
            if d is o:
                continue
            k = (d.eng, True, d.seq) if d.is_dma else (d.eng, False)
            if k not in best or d.seq > best[k].seq:
                best[k] = d
        o.waits = list(best.values())
        for r in o.reads:
            self.readers.setdefault(r, []).append(o)
        for wn in o.writes:
            self.last_write[wn] = o
            self.readers[wn] = []
        return o

    def dma(self, eng, fn, reads=(), writes=()):
        return self.op(eng, fn, reads, writes, is_dma=True)

    NLANES = 8

    def resolve(self):
        for o in self.ops:
            if o.is_dma or o.force_inc:
                o.inc = True
            real = []
            for d in o.waits:
                if d.eng == o.eng and not d.is_dma and not d.force_inc:
                    if d.eng != "tensor" or o.is_dma:
                        o.need_drain = True
                else:
                    d.inc = True
                    real.append(d)
            o.waits = real
        self.targets = {}
        self.self_wait = {}
        for e in ENGINES:
            cc = 0
            lane_use = [0] * self.NLANES
            dma_idx = 0
            for o in self.per_eng[e]:
                if not o.inc:
                    continue
                if o.is_dma:
                    lane = dma_idx % self.NLANES
                    dma_idx += 1
                    if lane_use[lane] > 0:
                        self.self_wait[(e, o.seq)] = ((e, "dma", lane),
                                                      16 * lane_use[lane])
                    lane_use[lane] += 1
                    self.targets[(e, True, o.seq)] = ((e, "dma", lane),
                                                      16 * lane_use[lane])
                    o.lane = lane
                else:
                    cc += 1
                    self.targets[(e, False, o.seq)] = ((e, "cmp"), cc)

    def make_runner(self, sems):
        waited = {}

        def run_engine(ename, eng):
            for o in self.per_eng[ename]:
                need = {}
                for d in o.waits:
                    sem_key, tgt = self.targets[(d.eng, d.is_dma, d.seq)]
                    if need.get(sem_key, 0) < tgt:
                        need[sem_key] = tgt
                for sem_key, tgt in need.items():
                    wk = (ename, sem_key)
                    if waited.get(wk, 0) < tgt:
                        eng.wait_ge(sems[sem_key], tgt)
                        waited[wk] = tgt
                sw = self.self_wait.get((ename, o.seq))
                if sw is not None and o.is_dma:
                    sem_key, tgt = sw
                    wk = (ename, sem_key)
                    if waited.get(wk, 0) < tgt:
                        eng.wait_ge(sems[sem_key], tgt)
                        waited[wk] = tgt
                if o.need_drain:
                    eng.drain()
                inst = o.fn(eng)
                if o.inc:
                    if o.is_dma:
                        inst.then_inc(sems[(ename, "dma", o.lane)], 16)
                    else:
                        inst.then_inc(sems[(ename, "cmp")], 1)
        return run_engine


def build_program(T, dbg=False):
    assert T % 512 == 0 and T >= 512
    NC = T // 512
    NT = T // 128

    nc = bass.Bass("TRN2", target_bir_lowering=False, debug=False)
    P = Plan()

    def din(name, shape):
        return nc.dram_tensor(name, shape, F32, kind="ExternalInput").ap()

    def dscr(name, shape):
        return nc.dram_tensor(name, shape, F32,
                              kind="ExternalOutput" if dbg else "Internal").ap()

    I = {}
    for name, shape in [
        ("x0T", [D, T]),
        ("WdT", [D, D]), ("WoT0", [D, D]), ("rwT", [D, NEXP]),
        ("moe_w1", [NEXP, D, MOE_H]), ("moe_w2", [NEXP, MOE_H, D]),
        ("WkT", [D, D]), ("WvT", [D, D]), ("WrT", [D, D]), ("WoT1", [D, D]),
        ("WfcT", [D, FCH]), ("WmlpT", [FCH, D]),
        ("slot0", [NSLOT, D]), ("w1a", [D, DH]), ("w1b", [D, DH]),
        ("w2s", [DH, NSLOT]), ("WihT", [D, GRU]), ("WhhT", [D, GRU]),
        ("wq_s", [D, DQ]), ("wk_s", [D, DQ]), ("wv_s", [D, D]),
        ("wvp_s", [D, D]), ("wo_s", [D, D]),
        ("bd_t", [128, DT]), ("bo_t", [128, DT]), ("a_log_t", [128, DT * NST]),
        ("bprm_t", [128, DT * NST]),
        ("td_t", [128, DT]), ("tf_t", [128, DT]), ("ms_t", [128, DT]),
        ("mls_t", [128, DT]),
        ("b1_t", [128, DH // 128]), ("b2_c", [NSLOT, 1]), ("tau_c", [1, 1]),
        ("bih_t", [128, GRU // 128]), ("bhh_t", [128, GRU // 128]),
        ("bq_t", [128, DQ // 128]), ("bk_t", [128, DQ // 128]),
        ("bv_t", [128, DT]), ("lng_t", [128, DT]), ("lnb_t", [128, DT]),
        ("bvp_t", [128, DT]), ("bo1_t", [128, DT]),
        ("ident", [128, 128]), ("onesk", [128, 128]), ("sel8", [NEXP, 128 * NEXP]),
        ("ramp0", [128, T]), ("ramp256", [128, 256]),
        ("m8", [128, 8]), ("im8", [128, 8]),
        ("mb8c", [NSLOT, 8]),
    ]:
        I[name] = din(name, shape)

    outT = nc.dram_tensor("outT", [D, T], F32, kind="ExternalOutput").ap()

    S = {}
    for name in ["xn0", "delta", "Cc", "y0", "s0", "g0", "x1", "xn1", "kv",
                 "rr", "hh", "xr", "s1", "g1", "sv"]:
        S[name] = dscr("d_" + name, [D, T])
    S["xn1N"] = dscr("d_xn1N", [T, D])
    S["xn0N"] = dscr("d_xn0N", [T, D])

    def dscr(name, shape, _d=dscr):
        return nc.dram_tensor(name, shape, F32, kind="Internal").ap()
    CC1 = 128 * DT * NST * 2
    S["cc1i"] = dscr("d_cc1i", [1, CC1])
    S["cc1o"] = dscr("d_cc1o", [8, CC1])
    S["c2ai"] = dscr("d_c2ai", [1, 128 * DT])
    S["c2ao"] = dscr("d_c2ao", [8, 128 * DT])
    CC2 = NSLOT * (D + 1)
    S["c2bi"] = dscr("d_c2bi", [1, CC2])
    S["c2bo"] = dscr("d_c2bo", [8, CC2])

    with contextlib.ExitStack() as stk:
        def sb(name, cols, rows=128, dtype=F32):
            return stk.enter_context(nc.sbuf_tensor(name, [rows, cols], dtype))

        rh = [sb("rh0", DT * 512), sb("rh1", DT * 512)]
        wsl = [sb("wsl0", 4096), sb("wsl1", 4096)]
        ob = sb("ob", DT * 512)
        hb = sb("hb", 4096)
        sc = [sb(f"sc{i}", T) for i in range(4)]
        ya = sb("ya", max(2 * T, DT * 512))
        sm = sb("sm", 256)
        sl = sb("sl", 4096)
        alp = sb("alp", NT * NSLOT)
        lgc = sb("lgc", 512, rows=NSLOT)
        b2sb = sb("b2sb", 1, rows=NSLOT)
        ss64 = sb("ss64", 16, rows=NSLOT)
        vecs = sb("vecs", 160)
        consts = sb("consts", 16)
        apost = sb("apost", DT * NST)
        napost = sb("napost", DT * NST)
        bprmt = sb("bprmt", DT * NST)
        identt = sb("identt", 128)
        oneskt = sb("oneskt", 128)
        sel8t = sb("sel8t", 128 * NEXP, rows=NEXP)
        m8t = sb("m8t", 8)
        im8t = sb("im8t", 8)
        mb8ct = sb("mb8ct", 8, rows=NSLOT)
        cc1s = sb("cc1s", DT * NST * 2)
        carry1 = sb("carry1", DT * NST)
        e1t = sb("e1t", DT)
        ne1t = sb("ne1t", DT)
        eut = sb("eut", DT)
        ewTt = sb("ewTt", DT)
        rwkvS = sb("rwkvS", DT)
        ewc = sb("ewc", DT)
        hprev = sb("hprev", DT)
        carry2 = sb("carry2", DT)
        sistage = sb("sistage", D + 1, rows=NSLOT)
        sigstg = sb("sigstg", D + 1, rows=NSLOT)
        gat8 = sb("gat8", T, rows=NEXP)
        rampt = sb("rampt", T)
        posm = sb("posm", NT * NEXP)
        gtm = sb("gtm", NT * NEXP)
        r256t = sb("r256t", 256)
        tiv = sb("tiv", 4)
        bh1 = sb("bh1", DH // 128)
        mski = sb("mski", NSLOT, dtype=mybir.dt.int32)
        gbt = sb("gbt", 512)

        def psumt(name):
            return stk.enter_context(nc.psum_tensor(name, [128, 512], F32))

        pm = [psumt(f"pm{i}") for i in range(4)]
        pb = [psumt(f"pb{i}") for i in range(2)]
        ptr = [psumt(f"ptr{i}") for i in range(2)]

        sems = {}
        for e in ENGINES:
            sems[(e, "cmp")] = stk.enter_context(nc.semaphore(f"s_{e}"))
        for e in ("sync", "gpsimd"):
            for ln_ in range(Plan.NLANES):
                sems[(e, "dma", ln_)] = stk.enter_context(
                    nc.semaphore(f"m_{e}{ln_}"))

        def LD(dst, src, rd, wr, eng="sync"):
            P.dma(eng, lambda E, dd=dst, ss=src: E.dma_start(out=dd, in_=ss), rd, wr)

        def STR(dst, src, rd, wr, eng="gpsimd"):
            P.dma(eng, lambda E, dd=dst, ss=src: E.dma_start(out=dd, in_=ss), rd, wr)

        def OP(eng, fn, rd=(), wr=()):
            P.op(eng, fn, rd, wr)

        def chunk_of(dram, c, width=512):
            return dram.rearrange("(k p) t -> p k t", p=128)[:, :, c * width:(c + 1) * width]

        def rhv(slot, k, width=512):
            return rh[slot][:, k * width:(k + 1) * width]

        def cname(name, c, width=512):
            c0 = c * width // 512
            c1 = (c * width + width + 511) // 512
            return [f"{name}@{i}" for i in range(c0, c1)]

        def allc(name):
            return [f"{name}@{i}" for i in range(NC)]

        def load_chunk(slot, dram, name, c, width=512):
            LD(rh[slot][:, :DT * width].rearrange("p (k t) -> p k t", k=DT),
               chunk_of(dram, c, width), cname(name, c, width), [f"rh{slot}"])

        def store_chunk(dram, name, c, src_sb, src_name, width=512):
            STR(chunk_of(dram, c, width),
                src_sb[:, :DT * width].rearrange("p (k t) -> p k t", k=DT),
                [src_name], cname(name, c, width))

        def MM(out_ap, bname, lhs_ap, lhs_names, rhs_ap, rhs_names, start, stop):
            OP("tensor",
               lambda E, o=out_ap, l=lhs_ap, r=rhs_ap, s1=start, s2=stop:
                   E.matmul(o, l, r, start=s1, stop=s2),
               list(lhs_names) + list(rhs_names), [bname])

        def mm_accum(bank_ap, bname, lhs_get, lhs_names, rhs_get, rhs_names, nK):
            for k in range(nK):
                MM(bank_ap, bname, lhs_get(k), lhs_names, rhs_get(k), rhs_names,
                   k == 0, k == nK - 1)

        def load_wblk(ws, Wd, Wname, nKt, c0, c1, k0=0):
            mcols = c1 - c0
            LD(wsl[ws][:, :nKt * mcols].rearrange("p (k m) -> p k m", k=nKt),
               Wd.rearrange("(k p) m -> p k m", p=128)[:, k0:k0 + nKt, c0:c1],
               [Wname], [f"wsl{ws}"])

        def big_mm(Wd, Wname, K, M, in_dram, in_name, evac, extra_per_chunk=None):
            nKt = K // 128
            nMb = (M + 511) // 512
            for c in range(NC):
                rslot = c % 2
                load_chunk(rslot, in_dram, in_name, c)
                if extra_per_chunk is not None:
                    extra_per_chunk(c)
                for mb in range(nMb):
                    mcols = min(512, M - mb * 512)
                    ws = mb % 2
                    load_wblk(ws, Wd, Wname, nKt, mb * 512, mb * 512 + mcols)
                    for mt in range(mcols // 128):
                        bank = pm[mt % 4]
                        bname = f"pm{mt % 4}"
                        mm_accum(bank[:, :], bname,
                                 lambda k, ws=ws, mt=mt, mc=mcols:
                                     wsl[ws][:, k * mc + mt * 128:k * mc + mt * 128 + 128],
                                 [f"wsl{ws}"],
                                 lambda k, rs=rslot: rhv(rs, k), [f"rh{rslot}"], nKt)
                        evac(mb * 4 + mt, c, bank, bname)

        LD(identt[:], I["ident"][:, :], ["i_ident"], ["identt"])
        LD(oneskt[:], I["onesk"][:, :], ["i_onesk"], ["oneskt"])
        LD(sel8t[:], I["sel8"][:, :], ["i_sel8"], ["sel8t"])
        LD(m8t[:], I["m8"][:, :], ["i_m8"], ["m8t"])
        LD(im8t[:], I["im8"][:, :], ["i_im8"], ["im8t"])
        LD(mb8ct[:], I["mb8c"][:, :], ["i_mb8c"], ["mb8ct"])
        LD(b2sb[:], I["b2_c"][:, :], ["i_b2"], ["b2sb"])
        LD(rampt[:], I["ramp0"][:, :], ["i_ramp0"], ["rampt"])
        LD(r256t[:], I["ramp256"][:, :], ["i_ramp256"], ["r256t"])
        VC = {}
        col = 0
        for nm_, wd in [("bd", DT), ("bo", DT), ("td", DT), ("tf", DT), ("ms", DT),
                        ("mls", DT), ("b1", DH // 128), ("bih", GRU // 128),
                        ("bhh", GRU // 128), ("bq", DQ // 128), ("bk", DQ // 128),
                        ("bv", DT), ("lng", DT), ("lnb", DT), ("bvp", DT), ("bo1", DT)]:
            VC[nm_] = col
            LD(vecs[:, col:col + wd], I[nm_ + "_t"][:, :], [f"i_{nm_}"], ["vecs"])
            col += wd

        def vc(nm_, idx=0):
            c0 = VC[nm_] + idx
            return vecs[:, c0:c0 + 1]

        CN = {"eps5": 0, "eps6": 1, "invD": 2, "one": 3, "neg30": 4, "scq": 5,
              "invS": 6, "eps8": 7, "mhalf": 8, "mone": 9, "negT": 10, "half": 11,
              "nscq": 12}
        for nm_, v in [("eps5", 1e-5), ("eps6", 1e-6), ("invD", 1.0 / D),
                       ("one", 1.0), ("neg30", -1e30), ("scq", DQ ** -0.5),
                       ("invS", 1.0 / NSLOT), ("eps8", 1e-8), ("mhalf", -0.5),
                       ("mone", -1.0), ("negT", float(-T)), ("half", 0.5),
                       ("nscq", -(DQ ** -0.5))]:
            OP("vector", lambda E, v=v, j=CN[nm_]: E.memset(consts[:, j:j + 1], v),
               (), ["consts"])

        def cn(nm_, rows=128):
            j = CN[nm_]
            return consts[0:rows, j:j + 1]

        nc.const_aps.register(F32, -0.5, consts[:, CN["mhalf"]:CN["mhalf"] + 1])
        nc.const_aps.register(F32, -1.0, consts[:, CN["mone"]:CN["mone"] + 1])
        nc.const_aps.register(F32, float(-T), consts[:, CN["negT"]:CN["negT"] + 1])

        LD(sm[:, 0:DT * NST], I["a_log_t"][:, :], ["i_alog"], ["sm"])
        OP("scalar", lambda E: E.activation(apost[:], sm[:, 0:DT * NST], ACTF.Abs),
           ["sm"], ["apost"])
        OP("scalar", lambda E: E.activation(apost[:], apost[:], ACTF.Exp, scale=-1.0),
           ["apost"], ["apost"])
        OP("scalar", lambda E: E.activation(apost[:], apost[:], ACTF.Ln,
                                            bias=cn("one")), ["apost", "consts"],
           ["apost"])
        OP("vector", lambda E: E.scalar_tensor_tensor(
            apost[:], sm[:, 0:DT * NST], 0.0, apost[:], ALU.max, ALU.add),
            ["sm", "apost"], ["apost"])
        OP("vector", lambda E: E.tensor_scalar(apost[:], apost[:], 1e-4, None, ALU.add),
           ["apost"], ["apost"])
        OP("vector", lambda E: E.tensor_scalar(napost[:], apost[:], -1.0, None, ALU.mult),
           ["apost"], ["napost"])
        LD(bprmt[:], I["bprm_t"][:, :], ["i_bprm"], ["bprmt"])

        LD(tiv[0:1, 0:1], I["tau_c"][:, :], ["i_tau"], ["tiv"])
        OP("scalar", lambda E: E.activation(tiv[0:1, 1:2], tiv[0:1, 0:1], ACTF.Abs),
           ["tiv"], ["tiv"])
        OP("vector", lambda E: E.tensor_scalar(tiv[0:1, 1:2], tiv[0:1, 1:2], 0.1,
                                               None, ALU.add), ["tiv"], ["tiv"])
        OP("vector", lambda E: E.reciprocal(tiv[0:1, 2:3], tiv[0:1, 1:2]),
           ["tiv"], ["tiv"])
        OP("tensor", lambda E: E.matmul(ptr[0][:, 0:1], oneskt[0:1, :], tiv[0:1, 2:3],
                                        start=True, stop=True),
           ["oneskt", "tiv"], ["ptr0"])
        OP("vector", lambda E: E.tensor_copy(tiv[:, 3:4], ptr[0][:, 0:1]),
           ["ptr0"], ["tiv"])

        OP("scalar", lambda E: E.activation(e1t[:, :], vecs[:, VC["td"]:VC["td"] + DT],
                                            ACTF.Exp), ["vecs"], ["e1t"])
        OP("vector", lambda E: E.tensor_scalar(ne1t[:, :], e1t[:, :], -1.0, None,
                                               ALU.mult), ["e1t"], ["ne1t"])
        OP("scalar", lambda E: E.activation(eut[:, :], vecs[:, VC["tf"]:VC["tf"] + DT],
                                            ACTF.Exp), ["vecs"], ["eut"])
        OP("scalar", lambda E: E.activation(ewTt[:, :], e1t[:, :], ACTF.Exp,
                                            scale=float(-T)), ["e1t"], ["ewTt"])

        def stage_norm(src, srcn, dst, dstn, kind):
            for c in range(NC):
                cs = slice(c * 512, (c + 1) * 512)
                slot = c % 2
                load_chunk(slot, src, srcn, c)
                if kind == "ln":
                    mm_accum(pb[0][:, :], "pb0", lambda k: oneskt[:], ["oneskt"],
                             lambda k, s=slot: rhv(s, k), [f"rh{slot}"], DT)
                    OP("vector", lambda E, cs=cs: E.tensor_scalar(
                        sc[1][:, cs], pb[0][:, :], cn("invD"), None, ALU.mult),
                        ["pb0", "consts"], ["sc1"])
                for k in range(DT):
                    half = (k % 2) * 512 if T >= 1024 else 0
                    OP("vector", lambda E, k=k, s=slot, h=half: E.tensor_mul(
                        sc[0][:, h:h + 512], rhv(s, k), rhv(s, k)),
                        [f"rh{slot}"], ["sc0"])
                    OP("tensor", lambda E, k=k, h=half: E.matmul(
                        pb[1][:, :], oneskt[:], sc[0][:, h:h + 512],
                        start=(k == 0), stop=(k == DT - 1)),
                        ["sc0", "oneskt"], ["pb1"])
                OP("vector", lambda E, cs=cs: E.tensor_scalar(
                    sc[2][:, cs], pb[1][:, :], cn("invD"), None, ALU.mult),
                    ["pb1", "consts"], ["sc2"])
                if kind == "ln":
                    OP("vector", lambda E, cs=cs: E.tensor_mul(
                        sc[3][:, cs], sc[1][:, cs], sc[1][:, cs]), ["sc1"], ["sc3"])
                    OP("vector", lambda E, cs=cs: E.tensor_sub(
                        sc[2][:, cs], sc[2][:, cs], sc[3][:, cs]),
                        ["sc2", "sc3"], ["sc2"])
                OP("scalar", lambda E, cs=cs: E.activation(
                    sc[2][:, cs], sc[2][:, cs], ACTF.Ln,
                    bias=cn("eps5") if kind == "ln" else cn("eps6")),
                    ["sc2", "consts"], ["sc2"])
                OP("scalar", lambda E, cs=cs: E.activation(
                    sc[2][:, cs], sc[2][:, cs], ACTF.Exp, scale=-0.5),
                    ["sc2"], ["sc2"])
                for k in range(DT):
                    if kind == "ln":
                        OP("vector", lambda E, k=k, s=slot, cs=cs: E.tensor_sub(
                            ob[:, k * 512:(k + 1) * 512], rhv(s, k), sc[1][:, cs]),
                            [f"rh{slot}", "sc1"], ["ob"])
                        OP("vector", lambda E, k=k, cs=cs: E.tensor_mul(
                            ob[:, k * 512:(k + 1) * 512], ob[:, k * 512:(k + 1) * 512],
                            sc[2][:, cs]), ["ob", "sc2"], ["ob"])
                    else:
                        OP("vector", lambda E, k=k, s=slot, cs=cs: E.tensor_mul(
                            ob[:, k * 512:(k + 1) * 512], rhv(s, k), sc[2][:, cs]),
                            [f"rh{slot}", "sc2"], ["ob"])
                store_chunk(dst, dstn, c, ob, "ob")

        stage_norm(I["x0T"], "i_x0T", S["xn0"], "d_xn0", "ln")

        for tt in range(NT):
            load_chunk(0, S["xn0"], "d_xn0", tt, width=128)
            for k in range(DT):
                OP("tensor", lambda E, k=k: E.transpose(
                    ptr[k % 2][:, 0:128], rh[0][:, k * 128:(k + 1) * 128], identt[:]),
                    ["rh0", "identt"], [f"ptr{k % 2}"])
                OP("vector", lambda E, k=k: E.tensor_copy(
                    ob[:, k * 128:(k + 1) * 128], ptr[k % 2][:, 0:128]),
                    [f"ptr{k % 2}"], ["ob"])
            STR(S["xn0N"][tt * 128:(tt + 1) * 128, :], ob[:, 0:D], ["ob"],
                [f"d_xn0N@{tt}"])

        def evac_delta(mt, c, bank, bname):
            half = (mt % 2) * 512 if T >= 1024 else 0
            o_t = ob[:, mt * 512:(mt + 1) * 512]
            s_t = sc[0][:, half:half + 512]
            OP("scalar", lambda E, bank=bank, o_t=o_t: E.activation(
                o_t, bank[:, :], ACTF.Identity, bias=vc("bd", mt)),
                [bname, "vecs"], ["ob"])
            OP("scalar", lambda E, o_t=o_t, s_t=s_t: E.activation(
                s_t, o_t, ACTF.Abs), ["ob"], ["sc0"])
            OP("scalar", lambda E, s_t=s_t: E.activation(
                s_t, s_t, ACTF.Exp, scale=-1.0), ["sc0"], ["sc0"])
            OP("scalar", lambda E, s_t=s_t: E.activation(
                s_t, s_t, ACTF.Ln, bias=cn("one")), ["sc0", "consts"], ["sc0"])
            OP("vector", lambda E, o_t=o_t, s_t=s_t: E.scalar_tensor_tensor(
                o_t, o_t, 0.0, s_t, ALU.max, ALU.add), ["ob", "sc0"], ["ob"])
            if mt == DT - 1:
                store_chunk(S["delta"], "d_delta", c, ob, "ob")
        big_mm(I["WdT"], "i_WdT", D, D, S["xn0"], "d_xn0", evac_delta)

        side = []

        def SIDE(fn):
            side.append(fn)

        def s3_load(et):
            LD(sc[0][:, :], S["delta"][et * 128:(et + 1) * 128, :], allc("d_delta"),
               ["sc0"], eng="gpsimd")
            LD(sc[1][:, :], S["xn0"][et * 128:(et + 1) * 128, :], allc("d_xn0"),
               ["sc1"], eng="gpsimd")
            OP("gpsimd", lambda E: E.tensor_mul(sc[1][:, :], sc[0][:, :], sc[1][:, :]),
               ["sc0", "sc1"], ["sc1"])
            OP("vector", lambda E: E.tensor_tensor_scan(
                sc[2][:, :], cn("one").broadcast_to([128, T]), sc[0][:, :], 0.0,
                ALU.mult, ALU.add), ["sc0", "consts"], ["sc2"])
            STR(S["Cc"][et * 128:(et + 1) * 128, :], sc[2][:, :], ["sc2"],
                [f"d_Cc@r{et}"])
            for n in range(NST):
                ci = et * NST + n
                OP("scalar", lambda E, ci=ci: E.activation(
                    cc1s[:, ci:ci + 1], sc[2][:, T - 1:T], ACTF.Exp,
                    scale=napost[:, ci:ci + 1]), ["sc2", "napost"], ["cc1s"])

        def s3_n(et, n):
            ci = et * NST + n
            OP("scalar", lambda E, ci=ci: E.activation(
                sc[3][:, :], sc[0][:, :], ACTF.Exp, scale=napost[:, ci:ci + 1]),
                ["sc0", "napost"], ["sc3"])
            OP("gpsimd", lambda E, ci=ci: E.tensor_tensor(
                ya[:, T:2 * T], sc[1][:, :],
                bprmt[:, ci:ci + 1].broadcast_to([128, T]), ALU.mult),
                ["sc1", "bprmt"], ["ya"])
            OP("vector", lambda E: E.tensor_tensor_scan(
                ya[:, T:2 * T], sc[3][:, :], ya[:, T:2 * T], 0.0, ALU.mult,
                ALU.add), ["sc3", "ya"], ["ya"])
            OP("gpsimd", lambda E, ci=ci: E.tensor_copy(
                cc1s[:, DT * NST + ci:DT * NST + ci + 1], ya[:, 2 * T - 1:2 * T]),
                ["ya"], ["cc1s"])
            if n == 0:
                OP("gpsimd", lambda E: E.tensor_copy(ya[:, 0:T], ya[:, T:2 * T]),
                   ["ya"], ["ya"])
            else:
                OP("gpsimd", lambda E: E.tensor_add(
                    ya[:, 0:T], ya[:, 0:T], ya[:, T:2 * T]), ["ya", "ya"], ["ya"])

        def s3_store(et):
            STR(S["y0"][et * 128:(et + 1) * 128, :], ya[:, 0:T], ["ya"],
                [f"d_y0@r{et}"])

        for et in range(DT):
            SIDE(lambda et=et: s3_load(et))
            for n in range(NST):
                SIDE(lambda et=et, n=n: s3_n(et, n))
            SIDE(lambda et=et: s3_store(et))

        def s4_trigger():
            STR(S["cc1i"][0, :].rearrange("(p c) -> p c", p=128), cc1s[:, :],
                ["cc1s"], ["d_cc1i"])
            P.op("gpsimd", lambda E: E.collective_compute(
                "AllGather", ALU.bypass, replica_groups=[list(range(8))],
                ins=[S["cc1i"].opt()], outs=[S["cc1o"].opt()]),
                ["d_cc1i"], ["d_cc1o"], force_inc=True)

        def s4_unit():
            NB = DT * NST
            OP("vector", lambda E: E.memset(carry1[:, :], 0.0), (), ["carry1"])
            for j in range(8):
                LD(cc1s[:, :], S["cc1o"][j, :].rearrange("(p c) -> p c", p=128),
                   ["d_cc1o"], ["cc1s"], eng="gpsimd")
                OP("vector", lambda E, j=j: E.tensor_scalar(
                    sm[:, 0:NB], cc1s[:, 0:NB], m8t[:, j:j + 1],
                    im8t[:, j:j + 1], ALU.mult, ALU.add),
                    ["cc1s", "m8t", "im8t"], ["sm"])
                OP("vector", lambda E: E.tensor_mul(carry1[:, :], carry1[:, :],
                                                    sm[:, 0:NB]), ["carry1", "sm"],
                   ["carry1"])
                OP("vector", lambda E, j=j: E.tensor_scalar(
                    sm[:, 0:NB], cc1s[:, NB:2 * NB],
                    m8t[:, j:j + 1], None, ALU.mult), ["cc1s", "m8t"], ["sm"])
                OP("vector", lambda E: E.tensor_add(carry1[:, :], carry1[:, :],
                                                    sm[:, 0:NB]), ["carry1", "sm"],
                   ["carry1"])

        SIDE(s4_trigger)
        for _ in range(64):
            SIDE(None)
        SIDE(s4_unit)

        def s5_load(et):
            LD(sc[0][:, :], S["Cc"][et * 128:(et + 1) * 128, :], [f"d_Cc@r{et}"],
               ["sc0"], eng="gpsimd")
            LD(ya[:, 0:T], S["y0"][et * 128:(et + 1) * 128, :], [f"d_y0@r{et}"],
               ["ya"], eng="gpsimd")

        def s5_n(et, n):
            ci = et * NST + n
            OP("scalar", lambda E, ci=ci: E.activation(
                sc[3][:, :], sc[0][:, :], ACTF.Exp, scale=napost[:, ci:ci + 1]),
                ["sc0", "napost"], ["sc3"])
            OP("vector", lambda E, ci=ci: E.scalar_tensor_tensor(
                ya[:, 0:T], sc[3][:, :], carry1[:, ci:ci + 1], ya[:, 0:T],
                ALU.mult, ALU.add), ["sc3", "carry1", "ya"], ["ya"])

        def s5_store(et):
            STR(S["y0"][et * 128:(et + 1) * 128, :], ya[:, 0:T], ["ya"],
                [f"d_y0@r{et}"])

        for et in range(DT):
            SIDE(lambda et=et: s5_load(et))
            for n in range(NST):
                SIDE(lambda et=et, n=n: s5_n(et, n))
            SIDE(lambda et=et: s5_store(et))

        LD(wsl[0][:, 0:DT * NEXP].rearrange("p (k m) -> p k m", k=DT),
           I["rwT"].rearrange("(k p) m -> p k m", p=128), ["i_rwT"], ["wsl0"])
        for c in range(NC):
            slot = c % 2
            load_chunk(slot, S["xn0"], "d_xn0", c)
            mm_accum(pb[0][0:NEXP, :], "pb0",
                     lambda k: wsl[0][:, k * NEXP:(k + 1) * NEXP], ["wsl0"],
                     lambda k, s=slot: rhv(s, k), [f"rh{slot}"], DT)
            OP("vector", lambda E: E.tensor_copy(hb[0:NEXP, 0:512], pb[0][0:NEXP, :]),
               ["pb0"], ["hb"])
            for tti in range(4):
                tt = c * 4 + tti
                OP("tensor", lambda E, tti=tti: E.transpose(
                    ptr[0][:, 0:NEXP], hb[0:NEXP, tti * 128:(tti + 1) * 128],
                    identt[0:NEXP, 0:NEXP]), ["hb", "identt"], ["ptr0"])
                OP("vector", lambda E: E.tensor_copy(sm[:, 0:NEXP], ptr[0][:, 0:NEXP]),
                   ["ptr0"], ["sm"])
                OP("vector", lambda E: E.reduce_max(sm[:, 16:17], sm[:, 0:NEXP],
                                                    axis=AX.X), ["sm"], ["sm"])
                OP("vector", lambda E: E.tensor_copy(sm[:, 8:8 + NEXP], sm[:, 0:NEXP]),
                   ["sm"], ["sm"])
                OP("vector", lambda E: E.tensor_scalar(
                    mski[:, 0:NEXP], sm[:, 0:NEXP], sm[:, 16:17], None,
                    ALU.is_ge), ["sm"], ["mski"])
                OP("vector", lambda E: E.copy_predicated(
                    sm[:, 8:8 + NEXP], mski[:, 0:NEXP],
                    cn("neg30").broadcast_to([128, NEXP])),
                    ["sm", "mski", "consts"], ["sm"])
                OP("vector", lambda E: E.reduce_max(sm[:, 17:18], sm[:, 8:8 + NEXP],
                                                    axis=AX.X), ["sm"], ["sm"])
                OP("vector", lambda E: E.tensor_scalar(
                    sm[:, 18:19], sm[:, 16:17], -1.0, None, ALU.mult), ["sm"], ["sm"])
                OP("scalar", lambda E: E.activation(
                    sm[:, 24:24 + NEXP], sm[:, 0:NEXP], ACTF.Exp, bias=sm[:, 18:19]),
                    ["sm"], ["sm"])
                OP("scalar", lambda E: E.activation(
                    sm[:, 19:20], sm[:, 17:18], ACTF.Exp, bias=sm[:, 18:19]),
                    ["sm"], ["sm"])
                OP("vector", lambda E: E.tensor_scalar(
                    sm[:, 19:20], sm[:, 19:20], 1.0, None, ALU.add), ["sm"], ["sm"])
                OP("vector", lambda E: E.reciprocal(sm[:, 20:21], sm[:, 19:20]),
                   ["sm"], ["sm"])
                OP("vector", lambda E: E.tensor_scalar(
                    sm[:, 32:32 + NEXP], sm[:, 0:NEXP], sm[:, 17:18], None,
                    ALU.is_ge), ["sm"], ["sm"])
                OP("vector", lambda E: E.tensor_scalar(
                    sm[:, 24:24 + NEXP], sm[:, 24:24 + NEXP], sm[:, 20:21], None,
                    ALU.mult), ["sm"], ["sm"])
                OP("vector", lambda E: E.tensor_mul(
                    sm[:, 24:24 + NEXP], sm[:, 24:24 + NEXP], sm[:, 32:32 + NEXP]),
                    ["sm"], ["sm"])
                OP("vector", lambda E, tt=tt: E.tensor_copy(
                    gtm[:, tt * NEXP:(tt + 1) * NEXP], sm[:, 24:24 + NEXP]),
                    ["sm"], ["gtm"])
                OP("tensor", lambda E: E.transpose(
                    ptr[1][0:NEXP, 0:128], sm[:, 24:24 + NEXP], identt[:]),
                    ["sm", "identt"], ["ptr1"])
                OP("vector", lambda E, tt=tt: E.tensor_copy(
                    gat8[0:NEXP, tt * 128:(tt + 1) * 128], ptr[1][0:NEXP, 0:128]),
                    ["ptr1"], ["gat8"])
            OP("vector", lambda E, c=c: E.tensor_scalar(
                hb[0:NEXP, 512:1024], gat8[0:NEXP, c * 512:(c + 1) * 512], 0.0,
                None, ALU.is_gt), ["gat8"], ["hb"])
            OP("vector", lambda E: E.tensor_tensor_scan(
                hb[0:NEXP, 1024:1536], cn("one", NEXP).broadcast_to([NEXP, 512]),
                hb[0:NEXP, 512:1024], 0.0, ALU.mult, ALU.add),
                ["hb", "consts"], ["hb"])
            OP("vector", lambda E: E.tensor_mul(
                hb[0:NEXP, 1024:1536], hb[0:NEXP, 1024:1536],
                hb[0:NEXP, 512:1024]), ["hb"], ["hb"])
            OP("vector", lambda E: E.tensor_scalar(
                hb[0:NEXP, 1024:1536], hb[0:NEXP, 1024:1536], -1.0, None,
                ALU.add), ["hb"], ["hb"])
            for tti in range(4):
                tt = c * 4 + tti
                OP("tensor", lambda E, tti=tti: E.transpose(
                    ptr[0][:, 0:NEXP], hb[0:NEXP, 1024 + tti * 128:1024 + (tti + 1) * 128],
                    identt[0:NEXP, 0:NEXP]), ["hb", "identt"], ["ptr0"])
                OP("vector", lambda E, tt=tt: E.tensor_copy(
                    posm[:, tt * NEXP:(tt + 1) * NEXP], ptr[0][:, 0:NEXP]),
                    ["ptr0"], ["posm"])

        n_points = NC * NEXP * 24
        n_side = len(side)
        side_i = [0]
        point_i = [0]

        def pop_side(k):
            point_i[0] += k
            want = min(n_side, (point_i[0] * n_side * 10) // (n_points * 9))
            while side_i[0] < want:
                u = side[side_i[0]]
                if u is not None:
                    r = u()
                side_i[0] += 1

        def drain_side():
            while side_i[0] < len(side):
                u = side[side_i[0]]
                if u is not None:
                    u()
                side_i[0] += 1

        for c in range(NC):
            for tti in range(4):
                tt = c * 4 + tti
                LD(rh[0][:, tti * 1024:(tti + 1) * 1024], S["xn0N"][tt * 128:(tt + 1) * 128, :],
                   [f"d_xn0N@{tt}"], ["rh0"])
            for e in range(NEXP):
                for tti in range(4):
                    tt = c * 4 + tti
                    OP("vector", lambda E, tti=tti, tt=tt, e=e: E.tensor_scalar(
                        hb[:, tti * 256:(tti + 1) * 256], r256t[:, :],
                        posm[:, tt * NEXP + e:tt * NEXP + e + 1], None,
                        ALU.is_equal), ["r256t", "posm"], ["hb"])
                for dtile in range(DT):
                    bank, bname = pm[dtile % 2], f"pm{dtile % 2}"
                    for tti in range(4):
                        MM(bank[:, 0:256], bname,
                           rh[0][:, tti * 1024 + dtile * 128:tti * 1024 + dtile * 128 + 128],
                           ["rh0"], hb[:, tti * 256:(tti + 1) * 256], ["hb"],
                           tti == 0, tti == 3)
                    OP("scalar", lambda E, dtile=dtile, bank=bank, bname=bname:
                       E.copy(rh[1][:, dtile * 256:(dtile + 1) * 256],
                              bank[:, 0:256]), [bname], ["rh1"])
                pop_side(2)
                for half in range(2):
                    for hmb in range(2):
                        ws = hmb
                        h0 = half * 1024 + hmb * 512
                        load_wblk(ws, I["moe_w1"][e], "i_moe_w1", DT, h0, h0 + 512)
                        for mt in range(4):
                            hm = half * 8 + hmb * 4 + mt
                            bank, bname = pm[hm % 4], f"pm{hm % 4}"
                            mm_accum(bank[:, 0:256], bname,
                                     lambda k, ws=ws, mt=mt:
                                         wsl[ws][:, k * 512 + mt * 128:k * 512 + mt * 128 + 128],
                                     [f"wsl{ws}"],
                                     lambda k: rh[1][:, k * 256:(k + 1) * 256],
                                     ["rh1"], DT)
                            OP("scalar", lambda E, hm=hm, bank=bank, bname=bname:
                               E.activation(hb[:, hm * 256:(hm + 1) * 256],
                                            bank[:, 0:256], ACTF.Relu),
                               [bname], ["hb"])
                            OP("scalar", lambda E, hm=hm: E.activation(
                                hb[:, hm * 256:(hm + 1) * 256],
                                hb[:, hm * 256:(hm + 1) * 256], ACTF.Square),
                                ["hb"], ["hb"])
                            pop_side(1)
                for ht in range(16):
                    ws = ht % 2
                    LD(wsl[ws][:, 0:1024], I["moe_w2"][e][ht * 128:(ht + 1) * 128, :],
                       ["i_moe_w2"], [f"wsl{ws}"])
                    for jh in range(2):
                        for dh in range(2):
                            bank, bname = pm[jh * 2 + dh], f"pm{jh * 2 + dh}"
                            MM(bank[:, :], bname,
                               hb[:, ht * 256 + jh * 128:ht * 256 + jh * 128 + 128],
                               ["hb"], wsl[ws][:, dh * 512:(dh + 1) * 512],
                               [f"wsl{ws}"], ht == 0, ht == 15)
                    if ht % 4 == 3:
                        pop_side(1)
                for jh in range(2):
                    for dh in range(2):
                        bank, bname = pm[jh * 2 + dh], f"pm{jh * 2 + dh}"
                        OP("scalar", lambda E, jh=jh, dh=dh, bank=bank, bname=bname:
                           E.copy(sl[:, jh * 1024 + dh * 512:jh * 1024 + dh * 512 + 512],
                                  bank[:, :]), [bname], ["sl"])
                for tti in range(4):
                    tt = c * 4 + tti
                    OP("vector", lambda E, tti=tti, tt=tt, e=e: E.tensor_scalar(
                        hb[:, tti * 256:(tti + 1) * 256], r256t[:, :],
                        posm[:, tt * NEXP + e:tt * NEXP + e + 1], None,
                        ALU.is_equal), ["r256t", "posm"], ["hb"])
                    OP("vector", lambda E, tti=tti, tt=tt, e=e: E.tensor_scalar(
                        hb[:, tti * 256:(tti + 1) * 256],
                        hb[:, tti * 256:(tti + 1) * 256],
                        gtm[:, tt * NEXP + e:tt * NEXP + e + 1], None,
                        ALU.mult), ["hb", "gtm"], ["hb"])
                    for jh in range(2):
                        OP("tensor", lambda E, tti=tti, jh=jh: E.transpose(
                            ptr[jh][:, 0:128],
                            hb[:, tti * 256 + jh * 128:tti * 256 + jh * 128 + 128],
                            identt[:]), ["hb", "identt"], [f"ptr{jh}"])
                        OP("vector", lambda E, tti=tti, jh=jh: E.tensor_copy(
                            hb[:, 2048 + (tti * 2 + jh) * 128:2048 + (tti * 2 + jh) * 128 + 128],
                            ptr[jh][:, 0:128]), [f"ptr{jh}"], ["hb"])
                for tti in range(4):
                    for dh in range(2):
                        bank, bname = pb[dh], f"pb{dh}"
                        for jh in range(2):
                            MM(bank[:, :], bname,
                               hb[:, 2048 + (tti * 2 + jh) * 128:2048 + (tti * 2 + jh) * 128 + 128],
                               ["hb"], sl[:, jh * 1024 + dh * 512:jh * 1024 + dh * 512 + 512],
                               ["sl"], jh == 0, jh == 1)
                        if e == 0:
                            OP("vector", lambda E, tti=tti, dh=dh, bank=bank,
                               bname=bname: E.tensor_copy(
                                   ob[:, tti * 1024 + dh * 512:tti * 1024 + dh * 512 + 512],
                                   bank[:, :]), [bname], ["ob"])
                        else:
                            OP("vector", lambda E, tti=tti, dh=dh, bank=bank,
                               bname=bname: E.tensor_add(
                                   ob[:, tti * 1024 + dh * 512:tti * 1024 + dh * 512 + 512],
                                   ob[:, tti * 1024 + dh * 512:tti * 1024 + dh * 512 + 512],
                                   bank[:, :]), ["ob", bname], ["ob"])
                pop_side(2)
            OP("scalar", lambda E: E.activation(ob[:, 0:2048], ob[:, 0:2048],
                                                ACTF.Silu), ["ob"], ["ob"])
            OP("scalar", lambda E: E.activation(ob[:, 2048:4096], ob[:, 2048:4096],
                                                ACTF.Silu), ["ob"], ["ob"])
            for tti in range(4):
                for dtile in range(DT):
                    pk = (tti * DT + dtile) % 2
                    OP("tensor", lambda E, tti=tti, dtile=dtile, pk=pk: E.transpose(
                        ptr[pk][:, 0:128],
                        ob[:, tti * 1024 + dtile * 128:tti * 1024 + dtile * 128 + 128],
                        identt[:]), ["ob", "identt"], [f"ptr{pk}"])
                    OP("vector", lambda E, tti=tti, dtile=dtile, pk=pk: E.tensor_copy(
                        hb[:, dtile * 512 + tti * 128:dtile * 512 + tti * 128 + 128],
                        ptr[pk][:, 0:128]), [f"ptr{pk}"], ["hb"])
            store_chunk(S["g0"], "d_g0", c, hb, "hb")
        drain_side()

        for c in range(NC):
            rslot = c % 2
            oslot = 1 - rslot
            cs = slice(c * 512, (c + 1) * 512)
            LD(rh[rslot][:, :DT * 512].rearrange("p (k t) -> p k t", k=DT),
               chunk_of(S["y0"], c), [f"d_y0@r{i}" for i in range(DT)],
               [f"rh{rslot}"])
            LD(hb[:, 0:DT * 512].rearrange("p (k t) -> p k t", k=DT),
               chunk_of(S["g0"], c), cname("d_g0", c), ["hb"])
            LD(rh[oslot][:, :DT * 512].rearrange("p (k t) -> p k t", k=DT),
               chunk_of(I["x0T"], c), ["i_x0T"], [f"rh{oslot}"])
            for mb in range(2):
                ws = mb % 2
                load_wblk(ws, I["WoT0"], "i_WoT0", DT, mb * 512, (mb + 1) * 512)
                for mt in range(4):
                    m_g = mb * 4 + mt
                    bank, bname = pm[mt % 2], f"pm{mt % 2}"
                    mm_accum(bank[:, :], bname,
                             lambda k, ws=ws, mt=mt: wsl[ws][:, k * 512 + mt * 128:k * 512 + mt * 128 + 128],
                             [f"wsl{ws}"],
                             lambda k, rs=rslot: rhv(rs, k), [f"rh{rslot}"], DT)
                    OP("scalar", lambda E, m_g=m_g, bank=bank, bname=bname: E.activation(
                        ob[:, m_g * 512:(m_g + 1) * 512], bank[:, :], ACTF.Identity,
                        bias=vc("bo", m_g)), [bname, "vecs"], ["ob"])
            for k in range(DT):
                OP("vector", lambda E, k=k: E.tensor_mul(
                    ob[:, k * 512:(k + 1) * 512], ob[:, k * 512:(k + 1) * 512],
                    hb[:, k * 512:(k + 1) * 512]), ["ob", "hb"], ["ob"])
                OP("vector", lambda E, k=k, s=oslot: E.tensor_add(
                    ob[:, k * 512:(k + 1) * 512], ob[:, k * 512:(k + 1) * 512],
                    rhv(s, k)), ["ob", f"rh{oslot}"], ["ob"])
            store_chunk(S["x1"], "d_x1", c, ob, "ob")
            mm_accum(pb[0][:, :], "pb0", lambda k: oneskt[:], ["oneskt"],
                     lambda k: ob[:, k * 512:(k + 1) * 512], ["ob"], DT)
            OP("vector", lambda E, cs=cs: E.tensor_scalar(
                sc[1][:, cs], pb[0][:, :], cn("invD"), None, ALU.mult),
                ["pb0", "consts"], ["sc1"])
            for k in range(DT):
                half = (k % 2) * 512 if T >= 1024 else 0
                OP("vector", lambda E, k=k, h=half: E.tensor_mul(
                    sc[0][:, h:h + 512], ob[:, k * 512:(k + 1) * 512],
                    ob[:, k * 512:(k + 1) * 512]), ["ob"], ["sc0"])
                OP("tensor", lambda E, k=k, h=half: E.matmul(
                    pb[1][:, :], oneskt[:], sc[0][:, h:h + 512],
                    start=(k == 0), stop=(k == DT - 1)), ["sc0", "oneskt"], ["pb1"])
            OP("vector", lambda E, cs=cs: E.tensor_scalar(
                sc[2][:, cs], pb[1][:, :], cn("invD"), None, ALU.mult),
                ["pb1", "consts"], ["sc2"])
            OP("vector", lambda E, cs=cs: E.tensor_mul(
                sc[3][:, cs], sc[1][:, cs], sc[1][:, cs]), ["sc1"], ["sc3"])
            OP("vector", lambda E, cs=cs: E.tensor_sub(
                sc[2][:, cs], sc[2][:, cs], sc[3][:, cs]), ["sc2", "sc3"], ["sc2"])
            OP("scalar", lambda E, cs=cs: E.activation(
                sc[2][:, cs], sc[2][:, cs], ACTF.Ln, bias=cn("eps5")),
                ["sc2", "consts"], ["sc2"])
            OP("scalar", lambda E, cs=cs: E.activation(
                sc[2][:, cs], sc[2][:, cs], ACTF.Exp, scale=-0.5), ["sc2"], ["sc2"])
            for k in range(DT):
                OP("vector", lambda E, k=k, cs=cs: E.tensor_sub(
                    hb[:, k * 512:(k + 1) * 512], ob[:, k * 512:(k + 1) * 512],
                    sc[1][:, cs]), ["ob", "sc1"], ["hb"])
                OP("vector", lambda E, k=k, cs=cs: E.tensor_mul(
                    hb[:, k * 512:(k + 1) * 512], hb[:, k * 512:(k + 1) * 512],
                    sc[2][:, cs]), ["hb", "sc2"], ["hb"])
            store_chunk(S["xn1"], "d_xn1", c, hb, "hb")
            for tti in range(4):
                tt = c * 4 + tti
                for k in range(DT):
                    OP("tensor", lambda E, k=k, tti=tti: E.transpose(
                        ptr[k % 2][:, 0:128],
                        hb[:, k * 512 + tti * 128:k * 512 + tti * 128 + 128],
                        identt[:]), ["hb", "identt"], [f"ptr{k % 2}"])
                    OP("vector", lambda E, k=k, s=oslot: E.tensor_copy(
                        rh[s][:, k * 128:(k + 1) * 128], ptr[k % 2][:, 0:128]),
                        [f"ptr{k % 2}"], [f"rh{oslot}"])
                STR(S["xn1N"][tt * 128:(tt + 1) * 128, :], rh[oslot][:, 0:D],
                    [f"rh{oslot}"], [f"d_xn1N@{tt}"])

        stage_norm(S["xn1"], "d_xn1", S["xr"], "d_xr", "rms")
        OP("vector", lambda E: E.memset(rwkvS[:, :], 0.0), (), ["rwkvS"])
        OP("scalar", lambda E: E.activation(ewc[:, :], e1t[:, :], ACTF.Exp,
                                            scale=-1.0), ["e1t"], ["ewc"])
        for c in range(NC):
            rslot = c % 2
            load_chunk(rslot, S["xr"], "d_xr", c)
            for mb in range(2):
                load_wblk(0, I["WkT"], "i_WkT", DT, mb * 512, (mb + 1) * 512)
                for mt in range(4):
                    m_g = mb * 4 + mt
                    bank, bname = pm[mt % 2], f"pm{mt % 2}"
                    mm_accum(bank[:, :], bname,
                             lambda k, mt=mt: wsl[0][:, k * 512 + mt * 128:k * 512 + mt * 128 + 128],
                             ["wsl0"],
                             lambda k, rs=rslot: rhv(rs, k), [f"rh{rslot}"], DT)
                    OP("vector", lambda E, m_g=m_g, bank=bank, bname=bname:
                       E.tensor_copy(hb[:, m_g * 512:(m_g + 1) * 512], bank[:, :]),
                       [bname], ["hb"])
                load_wblk(1, I["WvT"], "i_WvT", DT, mb * 512, (mb + 1) * 512)
                for mt in range(4):
                    m_g = mb * 4 + mt
                    bank, bname = pm[2 + mt % 2], f"pm{2 + mt % 2}"
                    mm_accum(bank[:, :], bname,
                             lambda k, mt=mt: wsl[1][:, k * 512 + mt * 128:k * 512 + mt * 128 + 128],
                             ["wsl1"],
                             lambda k, rs=rslot: rhv(rs, k), [f"rh{rslot}"], DT)
                    OP("vector", lambda E, m_g=m_g, bank=bank, bname=bname:
                       E.tensor_mul(hb[:, m_g * 512:(m_g + 1) * 512],
                                    hb[:, m_g * 512:(m_g + 1) * 512], bank[:, :]),
                       ["hb", bname], ["hb"])
            store_chunk(S["kv"], "d_kv", c, hb, "hb")
            for et in range(DT):
                half = (et % 2) * 512 if T >= 1024 else 0
                OP("vector", lambda E, et=et, h=half: E.tensor_tensor_scan(
                    sc[0][:, h:h + 512], ewc[:, et:et + 1].broadcast_to([128, 512]),
                    hb[:, et * 512:(et + 1) * 512], rwkvS[:, et:et + 1],
                    ALU.mult, ALU.add), ["ewc", "hb", "rwkvS"], ["sc0"])
                OP("gpsimd", lambda E, et=et, h=half: E.tensor_copy(
                    rwkvS[:, et:et + 1], sc[0][:, h + 511:h + 512]),
                    ["sc0"], ["rwkvS"])
                STR(S["hh"][et * 128:(et + 1) * 128, c * 512:(c + 1) * 512],
                    sc[0][:, half:half + 512], ["sc0"], [f"d_hh@{c}"])
            for mb in range(2):
                load_wblk(0, I["WrT"], "i_WrT", DT, mb * 512, (mb + 1) * 512)
                for mt in range(4):
                    m_g = mb * 4 + mt
                    bank, bname = pm[mt % 2], f"pm{mt % 2}"
                    mm_accum(bank[:, :], bname,
                             lambda k, mt=mt: wsl[0][:, k * 512 + mt * 128:k * 512 + mt * 128 + 128],
                             ["wsl0"],
                             lambda k, rs=rslot: rhv(rs, k), [f"rh{rslot}"], DT)
                    OP("scalar", lambda E, m_g=m_g, bank=bank, bname=bname:
                       E.activation(ob[:, m_g * 512:(m_g + 1) * 512], bank[:, :],
                                    ACTF.Sigmoid), [bname], ["ob"])
            store_chunk(S["rr"], "d_rr", c, ob, "ob")

        STR(S["c2ai"][0, :].rearrange("(p c) -> p c", p=128), rwkvS[:, :],
            ["rwkvS"], ["d_c2ai"])
        P.op("gpsimd", lambda E: E.collective_compute(
            "AllGather", ALU.bypass, replica_groups=[list(range(8))],
            ins=[S["c2ai"].opt()], outs=[S["c2ao"].opt()]),
            ["d_c2ai"], ["d_c2ao"], force_inc=True)

        LD(sl[0:NSLOT, 0:D], I["slot0"][:, :], ["i_slot0"], ["sl"])
        for k in range(DT):
            OP("tensor", lambda E, k=k: E.transpose(
                ptr[k % 2][:, 0:NSLOT], sl[0:NSLOT, k * 128:(k + 1) * 128],
                identt[0:NSLOT, 0:NSLOT]), ["sl", "identt"], [f"ptr{k % 2}"])
            OP("vector", lambda E, k=k: E.tensor_copy(
                sl[:, D + k * NSLOT:D + (k + 1) * NSLOT], ptr[k % 2][:, 0:NSLOT]),
                [f"ptr{k % 2}"], ["sl"])
        for k in range(DT):
            OP("vector", lambda E, k=k: E.tensor_reduce(
                sm[:, 64 + k:64 + k + 1], sl[:, D + k * NSLOT:D + (k + 1) * NSLOT],
                axis=AX.X, op=ALU.add), ["sl"], ["sm"])
        OP("vector", lambda E: E.tensor_scalar(
            sm[:, 64:64 + DT], sm[:, 64:64 + DT], cn("invS"), None, ALU.mult),
            ["sm", "consts"], ["sm"])
        LD(wsl[0][:, 0:DT * DH].rearrange("p (k m) -> p k m", k=DT),
           I["w1b"].rearrange("(k p) m -> p k m", p=128), ["i_w1b"], ["wsl0"])
        for mt in range(DH // 128):
            mm_accum(pb[0][:, 0:1], "pb0",
                     lambda k, mt=mt: wsl[0][:, k * DH + mt * 128:k * DH + mt * 128 + 128],
                     ["wsl0"],
                     lambda k: sm[:, 64 + k:64 + k + 1], ["sm"], DT)
            OP("vector", lambda E, mt=mt: E.tensor_scalar(
                bh1[:, mt:mt + 1], pb[0][:, 0:1], vc("b1", mt), None, ALU.add),
                ["pb0", "vecs"], ["bh1"])
        LD(wsl[0][:, 0:DT * DH].rearrange("p (k m) -> p k m", k=DT),
           I["w1a"].rearrange("(k p) m -> p k m", p=128), ["i_w1a"], ["wsl0"])
        LD(wsl[1][:, 0:4 * NSLOT].rearrange("p (k m) -> p k m", k=4),
           I["w2s"].rearrange("(k p) m -> p k m", p=128), ["i_w2s"], ["wsl1"])
        si_defer = []
        for c in range(NC):
            if c == 1:
                OP("vector", lambda E: E.memset(carry2[:, :], 0.0), (), ["carry2"])
                for j in range(8):
                    LD(sm[:, 48:48 + DT], S["c2ao"][j, :].rearrange(
                        "(p c) -> p c", p=128), ["d_c2ao"], ["sm"])
                    OP("vector", lambda E, j=j: E.tensor_scalar(
                        sm[:, 56:56 + DT], ewTt[:, :], m8t[:, j:j + 1], im8t[:, j:j + 1],
                        ALU.mult, ALU.add), ["ewTt", "m8t", "im8t"], ["sm"])
                    OP("vector", lambda E: E.tensor_mul(carry2[:, :], carry2[:, :],
                                                        sm[:, 56:56 + DT]),
                       ["carry2", "sm"], ["carry2"])
                    OP("vector", lambda E, j=j: E.tensor_scalar(
                        sm[:, 48:48 + DT], sm[:, 48:48 + DT], m8t[:, j:j + 1], None,
                        ALU.mult), ["sm", "m8t"], ["sm"])
                    OP("vector", lambda E: E.tensor_add(carry2[:, :], carry2[:, :],
                                                        sm[:, 48:48 + DT]),
                       ["carry2", "sm"], ["carry2"])

            rslot = c % 2
            load_chunk(rslot, S["xn1"], "d_xn1", c)
            for mt in range(4):
                bank, bname = pm[mt % 2], f"pm{mt % 2}"
                mm_accum(bank[:, :], bname,
                         lambda k, mt=mt: wsl[0][:, k * DH + mt * 128:k * DH + mt * 128 + 128],
                         ["wsl0"],
                         lambda k, rs=rslot: rhv(rs, k), [f"rh{rslot}"], DT)
                OP("scalar", lambda E, mt=mt, bank=bank, bname=bname: E.activation(
                    hb[:, mt * 512:(mt + 1) * 512], bank[:, :], ACTF.Gelu,
                    bias=bh1[:, mt:mt + 1]), [bname, "bh1"], ["hb"])
            for u in si_defer:
                u()
            si_defer.clear()
            mm_accum(pm[2][0:NSLOT, :], "pm2",
                     lambda k: wsl[1][:, k * NSLOT:(k + 1) * NSLOT], ["wsl1"],
                     lambda k: hb[:, k * 512:(k + 1) * 512], ["hb"], 4)
            OP("vector", lambda E: E.tensor_scalar(
                lgc[:, :], pm[2][0:NSLOT, :], b2sb[:, 0:1], None, ALU.add),
                ["pm2", "b2sb"], ["lgc"])
            for tti in range(4):
                tt = c * 4 + tti
                OP("tensor", lambda E, tti=tti: E.transpose(
                    ptr[0][:, 0:NSLOT], lgc[:, tti * 128:(tti + 1) * 128],
                    identt[0:NSLOT, 0:NSLOT]), ["lgc", "identt"], ["ptr0"])
                OP("vector", lambda E: E.tensor_copy(sm[:, 128:128 + NSLOT],
                                                     ptr[0][:, 0:NSLOT]),
                   ["ptr0"], ["sm"])
                OP("vector", lambda E: E.tensor_copy(sm[:, 192:192 + NSLOT],
                                                     sm[:, 128:128 + NSLOT]),
                   ["sm"], ["sm"])
                OP("vector", lambda E: E.reduce_max(sm[:, 112:113],
                                                    sm[:, 192:192 + NSLOT],
                                                    axis=AX.X), ["sm"], ["sm"])
                OP("vector", lambda E: E.tensor_copy(sm[:, 113:114], sm[:, 112:113]),
                   ["sm"], ["sm"])
                for it in range(7):
                    OP("vector", lambda E: E.tensor_scalar(
                        mski[:, 0:NSLOT], sm[:, 192:192 + NSLOT], sm[:, 113:114],
                        None, ALU.is_ge), ["sm"], ["mski"])
                    OP("vector", lambda E: E.copy_predicated(
                        sm[:, 192:192 + NSLOT], mski[:, 0:NSLOT],
                        cn("neg30").broadcast_to([128, NSLOT])),
                        ["sm", "mski", "consts"], ["sm"])
                    OP("vector", lambda E: E.reduce_max(
                        sm[:, 113:114], sm[:, 192:192 + NSLOT], axis=AX.X),
                        ["sm"], ["sm"])
                OP("vector", lambda E: E.tensor_scalar(
                    sm[:, 0:NSLOT], sm[:, 128:128 + NSLOT], sm[:, 113:114], None,
                    ALU.is_ge), ["sm"], ["sm"])
                OP("vector", lambda E: E.tensor_mul(sm[:, 114:115], sm[:, 112:113],
                                                    tiv[:, 3:4]), ["sm", "tiv"],
                   ["sm"])
                OP("vector", lambda E: E.tensor_scalar(
                    sm[:, 114:115], sm[:, 114:115], -1.0, None, ALU.mult),
                    ["sm"], ["sm"])
                OP("scalar", lambda E: E.activation(
                    sm[:, 128:128 + NSLOT], sm[:, 128:128 + NSLOT], ACTF.Exp,
                    scale=tiv[:, 3:4], bias=sm[:, 114:115]), ["sm", "tiv"], ["sm"])
                OP("vector", lambda E: E.tensor_mul(
                    sm[:, 128:128 + NSLOT], sm[:, 128:128 + NSLOT], sm[:, 0:NSLOT]),
                    ["sm"], ["sm"])
                OP("vector", lambda E: E.reduce_sum(
                    sm[:, 115:116], sm[:, 128:128 + NSLOT], axis=AX.X), ["sm"], ["sm"])
                OP("vector", lambda E: E.reciprocal(sm[:, 116:117], sm[:, 115:116]),
                   ["sm"], ["sm"])
                OP("vector", lambda E, tt=tt: E.tensor_scalar(
                    alp[:, tt * NSLOT:(tt + 1) * NSLOT], sm[:, 128:128 + NSLOT],
                    sm[:, 116:117], None, ALU.mult), ["sm"], ["alp"])
                def si_unit(tt=tt):
                    LD(ya[:, 0:D], S["xn1N"][tt * 128:(tt + 1) * 128, :],
                       [f"d_xn1N@{tt}"], ["ya"])
                    MM(pb[0][0:NSLOT, :], "pb0", alp[:, tt * NSLOT:(tt + 1) * NSLOT],
                       ["alp"], ya[:, 0:512], ["ya"], tt == 0, tt == NT - 1)
                    MM(pb[1][0:NSLOT, :], "pb1", alp[:, tt * NSLOT:(tt + 1) * NSLOT],
                       ["alp"], ya[:, 512:1024], ["ya"], tt == 0, tt == NT - 1)
                    MM(pm[3][0:NSLOT, 0:1], "pm3", alp[:, tt * NSLOT:(tt + 1) * NSLOT],
                       ["alp"], oneskt[:, 0:1], ["oneskt"], tt == 0, tt == NT - 1)
                si_defer.append(si_unit)
        for u in si_defer:
            u()
        si_defer.clear()
        OP("vector", lambda E: E.tensor_copy(sistage[:, 0:512], pb[0][0:NSLOT, :]),
           ["pb0"], ["sistage"])
        OP("vector", lambda E: E.tensor_copy(sistage[:, 512:1024], pb[1][0:NSLOT, :]),
           ["pb1"], ["sistage"])
        OP("vector", lambda E: E.tensor_copy(sistage[:, 1024:1025],
                                             pm[3][0:NSLOT, 0:1]), ["pm3"], ["sistage"])

        STR(S["c2bi"][0, :].rearrange("(p c) -> p c", p=NSLOT), sistage[:, :],
            ["sistage"], ["d_c2bi"])
        P.op("gpsimd", lambda E: E.collective_compute(
            "AllGather", ALU.bypass, replica_groups=[list(range(8))],
            ins=[S["c2bi"].opt()], outs=[S["c2bo"].opt()]),
            ["d_c2bi"], ["d_c2bo"], force_inc=True)

        OP("vector", lambda E: E.memset(hprev[:, :], 0.0), (), ["hprev"])

        def wkv_chunk(c):
            cs = slice(c * 512, (c + 1) * 512)
            oslot = c % 2
            rwb = hb if c % 2 == 0 else sl
            rwn = "hb" if c % 2 == 0 else "sl"
            LD(rh[oslot][:, :DT * 512].rearrange("p (k t) -> p k t", k=DT),
               chunk_of(S["hh"], c), cname("d_hh", c), [f"rh{oslot}"])
            LD(rwb[:, 0:DT * 512].rearrange("p (k t) -> p k t", k=DT),
               chunk_of(S["kv"], c), cname("d_kv", c), [rwn])
            LD(ya[:, 0:DT * 512].rearrange("p (k t) -> p k t", k=DT),
               chunk_of(S["rr"], c), cname("d_rr", c), ["ya"])
            units = []
            for et in range(DT):
                def u(et=et, cs=cs, oslot=oslot, rwb=rwb, rwn=rwn):
                    kvt = rwb[:, et * 512:(et + 1) * 512]
                    hht = rh[oslot][:, et * 512:(et + 1) * 512]
                    rrt = ya[:, et * 512:(et + 1) * 512]
                    pslot = (et % 2) * 512 if T >= 1024 else 0
                    pt = sc[0][:, pslot:pslot + 512]
                    et_ = sc[3][:, pslot:pslot + 512]
                    OP("scalar", lambda E, pt=pt: E.activation(
                        pt, rampt[:, cs], ACTF.Exp,
                        scale=ne1t[:, et:et + 1]), ["rampt", "ne1t"], ["sc0"])
                    OP("vector", lambda E, pt=pt: E.tensor_scalar(
                        pt, pt, carry2[:, et:et + 1], None, ALU.mult),
                        ["sc0", "carry2"], ["sc0"])
                    OP("vector", lambda E, pslot=pslot: E.tensor_add(
                        sc[0][:, pslot + 1:pslot + 512],
                        sc[0][:, pslot + 1:pslot + 512],
                        hht[:, 0:511]), ["sc0", f"rh{oslot}"], ["sc0"])
                    OP("vector", lambda E, pslot=pslot: E.tensor_add(
                        sc[0][:, pslot:pslot + 1], sc[0][:, pslot:pslot + 1],
                        hprev[:, et:et + 1]), ["sc0", "hprev"], ["sc0"])
                    OP("gpsimd", lambda E: E.tensor_copy(
                        hprev[:, et:et + 1], hht[:, 511:512]),
                        [f"rh{oslot}"], ["hprev"])
                    OP("gpsimd", lambda E, et_=et_: E.tensor_tensor(
                        et_, kvt,
                        eut[:, et:et + 1].broadcast_to([128, 512]), ALU.mult),
                        [rwn, "eut"], ["sc3"])
                    OP("vector", lambda E, pt=pt, et_=et_: E.tensor_add(
                        pt, pt, et_), ["sc0", "sc3"], ["sc0"])
                    OP("gpsimd", lambda E, pt=pt: E.tensor_mul(
                        kvt, rrt, pt),
                        ["ya", "sc0", rwn], [rwn])
                units.append(u)
            return units

        def mixer_chunk(c):
            cs = slice(c * 512, (c + 1) * 512)
            rslot = c % 2
            rwb = hb if c % 2 == 0 else sl
            rwn = "hb" if c % 2 == 0 else "sl"
            units = []
            def u_load(rslot=rslot, c=c):
                LD(rh[rslot][:, :DT * 512].rearrange("p (k t) -> p k t", k=DT),
                   chunk_of(S["xn1"], c), cname("d_xn1", c), [f"rh{rslot}"])
            units.append(u_load)
            for mb in range(2):
                def u_w(mb=mb):
                    load_wblk(mb % 2, I["WoT1"], "i_WoT1", DT, mb * 512,
                              (mb + 1) * 512)
                units.append(u_w)
                for mt in range(4):
                    def u_mm(mb=mb, mt=mt, rslot=rslot, rwb=rwb, rwn=rwn):
                        m_g = mb * 4 + mt
                        bank, bname = pm[mt % 2], f"pm{mt % 2}"
                        mm_accum(bank[:, :], bname,
                                 lambda k, ws=mb % 2, mt=mt:
                                     wsl[ws][:, k * 512 + mt * 128:k * 512 + mt * 128 + 128],
                                 [f"wsl{mb % 2}"],
                                 lambda k: rwb[:, k * 512:(k + 1) * 512], [rwn], DT)
                        OP("vector", lambda E, m_g=m_g, bank=bank, bname=bname,
                           rs=rslot: E.scalar_tensor_tensor(
                               ob[:, m_g * 512:(m_g + 1) * 512], bank[:, :],
                               vc("ms", m_g), rhv(rs, m_g), ALU.mult, ALU.add),
                           [bname, "vecs", f"rh{rslot}"], ["ob"])
                    units.append(u_mm)
            def u_tail(c=c, cs=cs, rwb=rwb, rwn=rwn):
                store_chunk(S["hh"], "d_hh", c, ob, "ob")
                for k in range(DT):
                    half = (k % 2) * 512 if T >= 1024 else 0
                    OP("vector", lambda E, k=k, h=half: E.tensor_mul(
                        sc[1][:, h:h + 512], ob[:, k * 512:(k + 1) * 512],
                        ob[:, k * 512:(k + 1) * 512]), ["ob"], ["sc1"])
                    OP("tensor", lambda E, k=k, h=half: E.matmul(
                        pb[1][:, :], oneskt[:], sc[1][:, h:h + 512],
                        start=(k == 0), stop=(k == DT - 1)), ["sc1", "oneskt"],
                        ["pb1"])
                OP("vector", lambda E, cs=cs: E.tensor_scalar(
                    sc[2][:, cs], pb[1][:, :], cn("invD"), None, ALU.mult),
                    ["pb1", "consts"], ["sc2"])
                OP("scalar", lambda E, cs=cs: E.activation(
                    sc[2][:, cs], sc[2][:, cs], ACTF.Ln, bias=cn("eps6")),
                    ["sc2", "consts"], ["sc2"])
                OP("scalar", lambda E, cs=cs: E.activation(
                    sc[2][:, cs], sc[2][:, cs], ACTF.Exp, scale=-0.5),
                    ["sc2"], ["sc2"])
                for k in range(DT):
                    OP("vector", lambda E, k=k, cs=cs, rwb=rwb, rwn=rwn:
                       E.tensor_mul(rwb[:, k * 512:(k + 1) * 512],
                                    ob[:, k * 512:(k + 1) * 512], sc[2][:, cs]),
                       ["ob", "sc2"], [rwn])
                store_chunk(S["xr"], "d_xr", c, rwb, rwn)
            units.append(u_tail)
            return units

        prev_mix = None
        for c in range(NC):
            w_units = wkv_chunk(c)
            m_units = list(prev_mix) if prev_mix else []
            n = max(len(w_units), len(m_units))
            for i in range(n):
                if i < len(w_units):
                    w_units[i]()
                if i < len(m_units):
                    m_units[i]()
            prev_mix = mixer_chunk(c)
        for u in prev_mix:
            u()

        for c in range(NC):
            load_chunk(0, S["xr"], "d_xr", c)
            load_chunk(1, S["hh"], "d_hh", c)
            for q in range(4):
                for fb in range(2):
                    ws = fb
                    f0 = q * 1024 + fb * 512
                    load_wblk(ws, I["WfcT"], "i_WfcT", DT, f0, f0 + 512)
                    for mt in range(4):
                        hm = fb * 4 + mt
                        bank, bname = pm[mt % 2], f"pm{mt % 2}"
                        mm_accum(bank[:, :], bname,
                                 lambda k, ws=ws, mt=mt:
                                     wsl[ws][:, k * 512 + mt * 128:k * 512 + mt * 128 + 128],
                                 [f"wsl{ws}"],
                                 lambda k: rhv(0, k), ["rh0"], DT)
                        OP("scalar", lambda E, hm=hm, bank=bank, bname=bname:
                           E.activation(hb[:, hm * 512:(hm + 1) * 512], bank[:, :],
                                        ACTF.Relu), [bname], ["hb"])
                        OP("scalar", lambda E, hm=hm: E.activation(
                            hb[:, hm * 512:(hm + 1) * 512],
                            hb[:, hm * 512:(hm + 1) * 512], ACTF.Square),
                            ["hb"], ["hb"])
                        hs = (hm % 2) * 512 if T >= 1024 else 0
                        OP("scalar", lambda E, hm=hm, bank=bank, bname=bname,
                           hs=hs: E.activation(sc[0][:, hs:hs + 512], bank[:, :],
                                               ACTF.Relu, scale=-1.0),
                           [bname], ["sc0"])
                        OP("scalar", lambda E, hs=hs: E.activation(
                            sc[0][:, hs:hs + 512], sc[0][:, hs:hs + 512],
                            ACTF.Square), ["sc0"], ["sc0"])
                        OP("vector", lambda E, hm=hm, hs=hs: E.scalar_tensor_tensor(
                            hb[:, hm * 512:(hm + 1) * 512], sc[0][:, hs:hs + 512],
                            0.25, hb[:, hm * 512:(hm + 1) * 512], ALU.mult,
                            ALU.add), ["sc0", "hb"], ["hb"])
                for mb in range(2):
                    ws = mb
                    LD(wsl[ws][:, 0:8 * 512].rearrange("p (k m) -> p k m", k=8),
                       I["WmlpT"].rearrange("(k p) m -> p k m", p=128)
                       [:, q * 8:q * 8 + 8, mb * 512:(mb + 1) * 512],
                       ["i_WmlpT"], [f"wsl{ws}"])
                    for mt in range(4):
                        m_g = mb * 4 + mt
                        bank, bname = pm[2 + (m_g % 2)], f"pm{2 + (m_g % 2)}"
                        mm_accum(bank[:, :], bname,
                                 lambda k, ws=ws, mt=mt:
                                     wsl[ws][:, k * 512 + mt * 128:k * 512 + mt * 128 + 128],
                                 [f"wsl{ws}"],
                                 lambda k: hb[:, k * 512:(k + 1) * 512], ["hb"], 8)
                        if q == 0:
                            OP("vector", lambda E, m_g=m_g, bank=bank, bname=bname:
                               E.tensor_copy(ob[:, m_g * 512:(m_g + 1) * 512],
                                             bank[:, :]), [bname], ["ob"])
                        else:
                            OP("vector", lambda E, m_g=m_g, bank=bank, bname=bname:
                               E.tensor_add(ob[:, m_g * 512:(m_g + 1) * 512],
                                            ob[:, m_g * 512:(m_g + 1) * 512],
                                            bank[:, :]), ["ob", bname], ["ob"])
            for mt in range(DT):
                OP("vector", lambda E, mt=mt: E.scalar_tensor_tensor(
                    ob[:, mt * 512:(mt + 1) * 512], ob[:, mt * 512:(mt + 1) * 512],
                    vc("mls", mt), rhv(1, mt), ALU.mult, ALU.add),
                    ["ob", "vecs", "rh1"], ["ob"])
            store_chunk(S["s1"], "d_s1", c, ob, "ob")

        OP("vector", lambda E: E.memset(sistage[:, :], 0.0), ("d_c2bi",), ["sistage"])
        for j in range(8):
            LD(sigstg[:, :], S["c2bo"][j, :]
               .rearrange("(p c) -> p c", p=NSLOT), ["d_c2bo"], ["sigstg"])
            OP("vector", lambda E, j=j: E.scalar_tensor_tensor(
                sistage[:, :], sigstg[:, :], mb8ct[:, j:j + 1],
                sistage[:, :], ALU.mult, ALU.add),
                ["sigstg", "mb8ct", "sistage"], ["sistage"])

        LD(sl[0:NSLOT, 0:D], I["slot0"][:, :], ["i_slot0"], ["sl"])
        for k in range(DT):
            OP("tensor", lambda E, k=k: E.transpose(
                ptr[k % 2][:, 0:NSLOT], sl[0:NSLOT, k * 128:(k + 1) * 128],
                identt[0:NSLOT, 0:NSLOT]), ["sl", "identt"], [f"ptr{k % 2}"])
            OP("vector", lambda E, k=k: E.tensor_copy(
                sl[:, D + k * NSLOT:D + (k + 1) * NSLOT], ptr[k % 2][:, 0:NSLOT]),
                [f"ptr{k % 2}"], ["sl"])
        OP("vector", lambda E: E.tensor_scalar(
            ss64[:, 0:1], sistage[:, 1024:1025], cn("eps8", NSLOT), None, ALU.add),
            ["sistage", "consts"], ["ss64"])
        OP("vector", lambda E: E.reciprocal(ss64[:, 1:2], ss64[:, 0:1]),
           ["ss64"], ["ss64"])
        OP("vector", lambda E: E.tensor_scalar(
            sl[0:NSLOT, 0:D], sistage[:, 0:D], ss64[:, 1:2], None, ALU.mult),
            ["sistage", "ss64"], ["sl"])
        for k in range(DT):
            OP("tensor", lambda E, k=k: E.transpose(
                ptr[k % 2][:, 0:NSLOT], sl[0:NSLOT, k * 128:(k + 1) * 128],
                identt[0:NSLOT, 0:NSLOT]), ["sl", "identt"], [f"ptr{k % 2}"])
            OP("vector", lambda E, k=k: E.tensor_copy(
                sl[:, 1536 + k * NSLOT:1536 + (k + 1) * NSLOT], ptr[k % 2][:, 0:NSLOT]),
                [f"ptr{k % 2}"], ["sl"])
        for which, Wn, rbase, bias_nm, hb0 in [
            (0, "WihT", 1536, "bih", 0), (1, "WhhT", 1024, "bhh", 1536)]:
            for mb in range(6):
                ws = mb % 2
                load_wblk(ws, I[Wn], "i_" + Wn, DT, mb * 512, (mb + 1) * 512)
                for mt in range(4):
                    m_g = mb * 4 + mt
                    bank, bname = pm[mt % 2], f"pm{mt % 2}"
                    mm_accum(bank[:, 0:NSLOT], bname,
                             lambda k, ws=ws, mt=mt:
                                 wsl[ws][:, k * 512 + mt * 128:k * 512 + mt * 128 + 128],
                             [f"wsl{ws}"],
                             lambda k, rb=rbase: sl[:, rb + k * NSLOT:rb + (k + 1) * NSLOT],
                             ["sl"], DT)
                    OP("vector", lambda E, m_g=m_g, bank=bank, bname=bname,
                       bias_nm=bias_nm, hb0=hb0: E.tensor_scalar(
                           hb[:, hb0 + m_g * NSLOT:hb0 + (m_g + 1) * NSLOT],
                           bank[:, 0:NSLOT], vc(bias_nm, m_g), None, ALU.add),
                       [bname, "vecs"], ["hb"])
        OP("vector", lambda E: E.tensor_add(hb[:, 0:512], hb[:, 0:512],
                                            hb[:, 1536:2048]), ["hb"], ["hb"])
        OP("scalar", lambda E: E.activation(hb[:, 0:512], hb[:, 0:512], ACTF.Sigmoid),
           ["hb"], ["hb"])
        OP("vector", lambda E: E.tensor_add(hb[:, 512:1024], hb[:, 512:1024],
                                            hb[:, 2048:2560]), ["hb"], ["hb"])
        OP("scalar", lambda E: E.activation(hb[:, 512:1024], hb[:, 512:1024],
                                            ACTF.Sigmoid), ["hb"], ["hb"])
        OP("vector", lambda E: E.tensor_mul(hb[:, 2560:3072], hb[:, 0:512],
                                            hb[:, 2560:3072]), ["hb"], ["hb"])
        OP("vector", lambda E: E.tensor_add(hb[:, 1024:1536], hb[:, 1024:1536],
                                            hb[:, 2560:3072]), ["hb"], ["hb"])
        OP("scalar", lambda E: E.activation(hb[:, 1024:1536], hb[:, 1024:1536],
                                            ACTF.Tanh), ["hb"], ["hb"])
        OP("vector", lambda E: E.tensor_sub(hb[:, 1536:2048], sl[:, 1024:1536],
                                            hb[:, 1024:1536]), ["sl", "hb"], ["hb"])
        OP("vector", lambda E: E.tensor_mul(hb[:, 1536:2048], hb[:, 512:1024],
                                            hb[:, 1536:2048]), ["hb"], ["hb"])
        OP("vector", lambda E: E.tensor_add(sl[:, 2048:2560], hb[:, 1024:1536],
                                            hb[:, 1536:2048]), ["hb"], ["sl"])
        for Wn, M, bias_nm, dst0 in [("wq_s", DQ, "bq", 2560),
                                     ("wk_s", DQ, "bk", 2688),
                                     ("wv_s", D, "bv", 2816)]:
            for mb in range((M + 511) // 512):
                mcols = min(512, M - mb * 512)
                ws = mb % 2
                load_wblk(ws, I[Wn], "i_" + Wn, DT, mb * 512, mb * 512 + mcols)
                for mt in range(mcols // 128):
                    m_g = mb * 4 + mt
                    bank, bname = pm[mt % 2], f"pm{mt % 2}"
                    mm_accum(bank[:, 0:NSLOT], bname,
                             lambda k, ws=ws, mt=mt, mc=mcols:
                                 wsl[ws][:, k * mc + mt * 128:k * mc + mt * 128 + 128],
                             [f"wsl{ws}"],
                             lambda k: sl[:, 2048 + k * NSLOT:2048 + (k + 1) * NSLOT],
                             ["sl"], DT)
                    OP("vector", lambda E, m_g=m_g, bank=bank, bname=bname,
                       bias_nm=bias_nm, dst0=dst0: E.tensor_scalar(
                           sl[:, dst0 + m_g * NSLOT:dst0 + (m_g + 1) * NSLOT],
                           bank[:, 0:NSLOT], vc(bias_nm, m_g), None, ALU.add),
                       [bname, "vecs"], ["sl"])
        for k in range(2):
            MM(ptr[0][0:NSLOT, 0:NSLOT], "ptr0",
               sl[:, 2560 + k * NSLOT:2560 + (k + 1) * NSLOT], ["sl"],
               sl[:, 2688 + k * NSLOT:2688 + (k + 1) * NSLOT], ["sl"],
               k == 0, k == 1)
        OP("vector", lambda E: E.tensor_copy(sl[0:NSLOT, 3328:3392],
                                             ptr[0][0:NSLOT, 0:NSLOT]),
           ["ptr0"], ["sl"])
        OP("vector", lambda E: E.reduce_max(ss64[:, 2:3], sl[0:NSLOT, 3328:3392],
                                            axis=AX.X), ["sl"], ["ss64"])
        OP("vector", lambda E: E.tensor_mul(ss64[:, 3:4], ss64[:, 2:3],
                                            cn("nscq", NSLOT)),
           ["ss64", "consts"], ["ss64"])
        OP("scalar", lambda E: E.activation(
            sl[0:NSLOT, 3328:3392], sl[0:NSLOT, 3328:3392], ACTF.Exp,
            scale=cn("scq", NSLOT), bias=ss64[:, 3:4]),
            ["sl", "consts", "ss64"], ["sl"])
        OP("vector", lambda E: E.reduce_sum(ss64[:, 4:5], sl[0:NSLOT, 3328:3392],
                                            axis=AX.X), ["sl"], ["ss64"])
        OP("vector", lambda E: E.reciprocal(ss64[:, 5:6], ss64[:, 4:5]),
           ["ss64"], ["ss64"])
        OP("vector", lambda E: E.tensor_scalar(
            sl[0:NSLOT, 3328:3392], sl[0:NSLOT, 3328:3392], ss64[:, 5:6], None,
            ALU.mult), ["sl", "ss64"], ["sl"])
        OP("tensor", lambda E: E.transpose(
            ptr[1][0:NSLOT, 0:NSLOT], sl[0:NSLOT, 3328:3392],
            identt[0:NSLOT, 0:NSLOT]), ["sl", "identt"], ["ptr1"])
        OP("vector", lambda E: E.tensor_copy(sl[0:NSLOT, 3392:3456],
                                             ptr[1][0:NSLOT, 0:NSLOT]),
           ["ptr1"], ["sl"])
        for k in range(DT):
            OP("tensor", lambda E, k=k: E.transpose(
                ptr[k % 2][0:NSLOT, 0:128], sl[:, 2816 + k * NSLOT:2816 + (k + 1) * NSLOT],
                identt[:]), ["sl", "identt"], [f"ptr{k % 2}"])
            OP("vector", lambda E, k=k: E.tensor_copy(
                hb[0:NSLOT, k * 128:(k + 1) * 128], ptr[k % 2][0:NSLOT, 0:128]),
                [f"ptr{k % 2}"], ["hb"])
        for m in range(DT):
            MM(pm[0][:, 0:NSLOT], "pm0", hb[0:NSLOT, m * 128:(m + 1) * 128], ["hb"],
               sl[0:NSLOT, 3392:3456], ["sl"], True, True)
            OP("vector", lambda E, m=m: E.tensor_add(
                hb[:, 1024 + m * NSLOT:1024 + (m + 1) * NSLOT],
                sl[:, 2048 + m * NSLOT:2048 + (m + 1) * NSLOT], pm[0][:, 0:NSLOT]),
                ["sl", "pm0"], ["hb"])
        for k in range(DT):
            OP("tensor", lambda E, k=k: E.matmul(
                pb[0][:, 0:NSLOT], oneskt[:],
                hb[:, 1024 + k * NSLOT:1024 + (k + 1) * NSLOT],
                start=(k == 0), stop=(k == DT - 1)), ["hb", "oneskt"], ["pb0"])
        for k in range(DT):
            OP("vector", lambda E, k=k: E.tensor_mul(
                hb[:, 3584 + k % 2 * NSLOT:3584 + (k % 2 + 1) * NSLOT],
                hb[:, 1024 + k * NSLOT:1024 + (k + 1) * NSLOT],
                hb[:, 1024 + k * NSLOT:1024 + (k + 1) * NSLOT]), ["hb"], ["hb"])
            OP("tensor", lambda E, k=k: E.matmul(
                pb[1][:, 0:NSLOT], oneskt[:],
                hb[:, 3584 + k % 2 * NSLOT:3584 + (k % 2 + 1) * NSLOT],
                start=(k == 0), stop=(k == DT - 1)), ["hb", "oneskt"], ["pb1"])
        OP("vector", lambda E: E.tensor_scalar(
            hb[:, 3712:3776], pb[0][:, 0:NSLOT], cn("invD"), None, ALU.mult),
            ["pb0", "consts"], ["hb"])
        OP("vector", lambda E: E.tensor_scalar(
            hb[:, 3776:3840], pb[1][:, 0:NSLOT], cn("invD"), None, ALU.mult),
            ["pb1", "consts"], ["hb"])
        OP("vector", lambda E: E.tensor_mul(hb[:, 3840:3904], hb[:, 3712:3776],
                                            hb[:, 3712:3776]), ["hb"], ["hb"])
        OP("vector", lambda E: E.tensor_sub(hb[:, 3776:3840], hb[:, 3776:3840],
                                            hb[:, 3840:3904]),
           ["hb", "hb"], ["hb"])
        OP("scalar", lambda E: E.activation(hb[:, 3776:3840], hb[:, 3776:3840],
                                            ACTF.Ln, bias=cn("eps5")),
           ["hb", "consts"], ["hb"])
        OP("scalar", lambda E: E.activation(hb[:, 3776:3840], hb[:, 3776:3840],
                                            ACTF.Exp, scale=-0.5), ["hb"], ["hb"])
        for k in range(DT):
            OP("vector", lambda E, k=k: E.tensor_sub(
                hb[:, 1024 + k * NSLOT:1024 + (k + 1) * NSLOT],
                hb[:, 1024 + k * NSLOT:1024 + (k + 1) * NSLOT], hb[:, 3712:3776]),
                ["hb", "hb"], ["hb"])
            OP("vector", lambda E, k=k: E.tensor_mul(
                hb[:, 1024 + k * NSLOT:1024 + (k + 1) * NSLOT],
                hb[:, 1024 + k * NSLOT:1024 + (k + 1) * NSLOT], hb[:, 3776:3840]),
                ["hb", "hb"], ["hb"])
            OP("vector", lambda E, k=k: E.tensor_scalar(
                hb[:, 1024 + k * NSLOT:1024 + (k + 1) * NSLOT],
                hb[:, 1024 + k * NSLOT:1024 + (k + 1) * NSLOT],
                vc("lng", k), vc("lnb", k), ALU.mult, ALU.add), ["hb", "vecs"], ["hb"])
        for mb in range(2):
            ws = mb
            load_wblk(ws, I["wvp_s"], "i_wvp_s", DT, mb * 512, (mb + 1) * 512)
            for mt in range(4):
                m_g = mb * 4 + mt
                bank, bname = pm[mt % 2], f"pm{mt % 2}"
                mm_accum(bank[:, 0:NSLOT], bname,
                         lambda k, ws=ws, mt=mt:
                             wsl[ws][:, k * 512 + mt * 128:k * 512 + mt * 128 + 128],
                         [f"wsl{ws}"],
                         lambda k: hb[:, 1024 + k * NSLOT:1024 + (k + 1) * NSLOT],
                         ["hb"], DT)
                OP("vector", lambda E, m_g=m_g, bank=bank, bname=bname:
                   E.tensor_scalar(
                       sl[:, 2560 + m_g * NSLOT:2560 + (m_g + 1) * NSLOT],
                       bank[:, 0:NSLOT], vc("bvp", m_g), None, ALU.add),
                   [bname, "vecs"], ["sl"])
        for k in range(DT):
            OP("tensor", lambda E, k=k: E.transpose(
                ptr[k % 2][0:NSLOT, 0:128], sl[:, 2560 + k * NSLOT:2560 + (k + 1) * NSLOT],
                identt[:]), ["sl", "identt"], [f"ptr{k % 2}"])
            OP("vector", lambda E, k=k: E.tensor_copy(
                hb[0:NSLOT, 1536 + k * 128:1536 + (k + 1) * 128],
                ptr[k % 2][0:NSLOT, 0:128]), [f"ptr{k % 2}"], ["hb"])
        for tt in range(NT):
            OP("tensor", lambda E, tt=tt: E.transpose(
                ptr[0][0:NSLOT, 0:128], alp[:, tt * NSLOT:(tt + 1) * NSLOT],
                identt[:]), ["alp", "identt"], ["ptr0"])
            OP("vector", lambda E: E.tensor_copy(sl[0:NSLOT, 3456:3584],
                                                 ptr[0][0:NSLOT, 0:128]),
               ["ptr0"], ["sl"])
            for m in range(DT):
                MM(pm[1][:, 0:128], "pm1", hb[0:NSLOT, 1536 + m * 128:1536 + (m + 1) * 128],
                   ["hb"], sl[0:NSLOT, 3456:3584], ["sl"], True, True)
                OP("vector", lambda E, m=m: E.tensor_copy(
                    ob[:, m * 128:(m + 1) * 128], pm[1][:, 0:128]), ["pm1"], ["ob"])
            store_chunk(S["sv"], "d_sv", tt, ob, "ob", width=128)
        def evac_g1(mt, c, bank, bname):
            OP("scalar", lambda E, mt=mt, bank=bank, bname=bname: E.activation(
                ob[:, mt * 512:(mt + 1) * 512], bank[:, :], ACTF.Silu,
                bias=vc("bo1", mt)), [bname, "vecs"], ["ob"])
            if mt == DT - 1:
                store_chunk(S["g1"], "d_g1", c, ob, "ob")
        big_mm(I["wo_s"], "i_wo_s", D, D, S["sv"], "d_sv", evac_g1)

        for c in range(NC):
            load_chunk(0, S["s1"], "d_s1", c)
            load_chunk(1, S["g1"], "d_g1", c)
            LD(hb[:, 0:DT * 512].rearrange("p (k t) -> p k t", k=DT),
               chunk_of(S["x1"], c), cname("d_x1", c), ["hb"])
            for k in range(DT):
                OP("vector", lambda E, k=k: E.tensor_mul(
                    ob[:, k * 512:(k + 1) * 512], rhv(0, k), rhv(1, k)),
                    ["rh0", "rh1"], ["ob"])
                OP("vector", lambda E, k=k: E.tensor_add(
                    ob[:, k * 512:(k + 1) * 512], ob[:, k * 512:(k + 1) * 512],
                    hb[:, k * 512:(k + 1) * 512]), ["ob", "hb"], ["ob"])
            store_chunk(outT, "o_outT", c, ob, "ob")
        OP("vector", lambda E: E.engine_nop(), ["o_outT"], ())

        P.resolve()
        run = P.make_runner(sems)

        with nc.Block() as block:
            @block.tensor
            def _(eng):
                run("tensor", eng)

            @block.vector
            def _(eng):
                run("vector", eng)

            @block.scalar
            def _(eng):
                run("scalar", eng)

            @block.sync
            def _(eng):
                run("sync", eng)

            @block.gpsimd
            def _(eng):
                run("gpsimd", eng)

    return nc


def _vec_tiles(v, ntiles):
    return np.ascontiguousarray(v.reshape(ntiles, 128).T.astype(np.float32))


def _mat_tiles(v, ntiles, n2):
    return np.ascontiguousarray(
        v.reshape(ntiles, 128, n2).transpose(1, 0, 2).reshape(128, ntiles * n2)
        .astype(np.float32))


_CACHED = {}


def kernel(**inputs):
    B, Tfull, _ = inputs["L0_attn_x"].shape
    ncores = 8
    per_batch = ncores // B
    T = Tfull // per_batch

    key = T
    if key not in _CACHED:
        _CACHED[key] = build_program(T)
    nc = _CACHED[key]

    f32 = np.float32
    g = {k: np.asarray(v) for k, v in inputs.items()}

    common = {}
    common["WdT"] = np.ascontiguousarray(g["L0_attn_Wd"].T.astype(f32))
    common["WoT0"] = np.ascontiguousarray(g["L0_attn_Wo"].T.astype(f32))
    common["rwT"] = np.ascontiguousarray(g["L0_ffn_router_w"].T.astype(f32))
    common["moe_w1"] = np.ascontiguousarray(g["L0_ffn_w1"].astype(f32))
    common["moe_w2"] = np.ascontiguousarray(g["L0_ffn_w2"].astype(f32))
    common["WkT"] = np.ascontiguousarray(g["L1_attn_Wk"].T.astype(f32))
    common["WvT"] = np.ascontiguousarray(g["L1_attn_Wv"].T.astype(f32))
    common["WrT"] = np.ascontiguousarray(g["L1_attn_Wr"].T.astype(f32))
    common["WoT1"] = np.ascontiguousarray(g["L1_attn_Wo"].T.astype(f32))
    common["WfcT"] = np.ascontiguousarray(g["L1_attn_Wfc"].T.astype(f32))
    common["WmlpT"] = np.ascontiguousarray(g["L1_attn_Wmlp"].T.astype(f32))
    common["slot0"] = np.ascontiguousarray(g["L1_ffn_slot_init"].astype(f32))
    common["w1a"] = np.ascontiguousarray(g["L1_ffn_w1"][:D].astype(f32))
    common["w1b"] = np.ascontiguousarray(g["L1_ffn_w1"][D:].astype(f32))
    common["w2s"] = np.ascontiguousarray(g["L1_ffn_w2"].astype(f32))
    common["WihT"] = np.ascontiguousarray(g["L1_ffn_Wih"].T.astype(f32))
    common["WhhT"] = np.ascontiguousarray(g["L1_ffn_Whh"].T.astype(f32))
    common["wq_s"] = np.ascontiguousarray(g["L1_ffn_wq"].astype(f32))
    common["wk_s"] = np.ascontiguousarray(g["L1_ffn_wk"].astype(f32))
    common["wv_s"] = np.ascontiguousarray(g["L1_ffn_wv"].astype(f32))
    common["wvp_s"] = np.ascontiguousarray(g["L1_ffn_wvp"].astype(f32))
    common["wo_s"] = np.ascontiguousarray(g["L1_ffn_wo"].astype(f32))

    common["bd_t"] = _vec_tiles(g["L0_attn_bd"], DT)
    common["bo_t"] = _vec_tiles(g["L0_attn_bo"], DT)
    common["a_log_t"] = _mat_tiles(g["L0_attn_a_log"], DT, NST)
    common["bprm_t"] = _mat_tiles(g["L0_attn_b_param"], DT, NST)
    common["td_t"] = _vec_tiles(g["L1_attn_time_decay"], DT)
    common["tf_t"] = _vec_tiles(g["L1_attn_time_first"], DT)
    common["ms_t"] = _vec_tiles(g["L1_attn_mixer_scale"], DT)
    common["mls_t"] = _vec_tiles(g["L1_attn_mlp_scale"], DT)
    common["b1_t"] = _vec_tiles(g["L1_ffn_b1"], DH // 128)
    common["b2_c"] = np.ascontiguousarray(
        g["L1_ffn_b2"].reshape(NSLOT, 1).astype(f32))
    common["tau_c"] = np.ascontiguousarray(
        g["L1_ffn_tau"].reshape(1, 1).astype(f32))
    common["bih_t"] = _vec_tiles(g["L1_ffn_bih"], GRU // 128)
    common["bhh_t"] = _vec_tiles(g["L1_ffn_bhh"], GRU // 128)
    common["bq_t"] = _vec_tiles(g["L1_ffn_bq"], DQ // 128)
    common["bk_t"] = _vec_tiles(g["L1_ffn_bk"], DQ // 128)
    common["bv_t"] = _vec_tiles(g["L1_ffn_bv"], DT)
    common["lng_t"] = _vec_tiles(g["L1_ffn_ln_g"], DT)
    common["lnb_t"] = _vec_tiles(g["L1_ffn_ln_b"], DT)
    common["bvp_t"] = _vec_tiles(g["L1_ffn_bvp"], DT)
    common["bo1_t"] = _vec_tiles(g["L1_ffn_bo"], DT)

    common["ident"] = np.eye(128, dtype=f32)
    common["onesk"] = np.ones((128, 128), dtype=f32)
    sel8 = np.zeros((NEXP, 128 * NEXP), dtype=f32)
    for e in range(NEXP):
        sel8[e, e * 128:(e + 1) * 128] = 1.0
    common["sel8"] = sel8
    common["ramp0"] = np.broadcast_to(
        np.arange(T, dtype=f32)[None, :], (128, T)).copy()
    common["ramp256"] = np.broadcast_to(
        np.arange(256, dtype=f32)[None, :], (128, 256)).copy()

    x = g["L0_attn_x"].astype(f32)
    in_maps = []
    for cid in range(ncores):
        b = cid // per_batch
        r = cid % per_batch
        m = dict(common)
        xs = x[b, r * T:(r + 1) * T, :]
        m["x0T"] = np.ascontiguousarray(xs.T)
        m8 = np.zeros((128, 8), dtype=f32)
        mb8 = np.zeros((NSLOT, 8), dtype=f32)
        for j in range(8):
            if j // per_batch == b:
                mb8[:, j] = 1.0
                if j < cid:
                    m8[:, j] = 1.0
        m["m8"] = m8
        m["im8"] = 1.0 - m8
        m["mb8c"] = mb8
        in_maps.append(m)

    res = run_bass_kernel_spmd(nc, in_maps, core_ids=list(range(ncores)))
    out = np.empty((B, Tfull, D), dtype=f32)
    for cid in range(ncores):
        b = cid // per_batch
        r = cid % per_batch
        out[b, r * T:(r + 1) * T, :] = res.results[cid]["outT"].T
    return out


# version 17
# speedup vs baseline: 1.0127x; 1.0127x over previous
import contextlib
import numpy as np

import concourse.bass as bass
import concourse.mybir as mybir
from concourse.bass_utils import run_bass_kernel_spmd

F32 = mybir.dt.float32
ALU = mybir.AluOpType
ACTF = mybir.ActivationFunctionType
AX = mybir.AxisListType

D = 1024
NST = 16
NEXP = 8
MOE_H = 2048
NSLOT = 64
DQ = 256
DH = 512
FCH = 4096
GRU = 3072
DT = D // 128

ENGINES = ("tensor", "vector", "scalar", "sync", "gpsimd")


class _Op:
    __slots__ = ("eng", "fn", "reads", "writes", "is_dma", "seq", "waits",
                 "need_drain", "inc", "force_inc", "lane")

    def __init__(self, eng, fn, reads, writes, is_dma, force_inc=False):
        self.eng, self.fn = eng, fn
        self.reads, self.writes = tuple(reads), tuple(writes)
        self.is_dma = is_dma
        self.seq = -1
        self.waits = []
        self.need_drain = False
        self.inc = False
        self.force_inc = force_inc
        self.lane = -1


class Plan:
    def __init__(self):
        self.per_eng = {e: [] for e in ENGINES}
        self.ops = []
        self.last_write = {}
        self.readers = {}

    def op(self, eng, fn, reads=(), writes=(), is_dma=False, force_inc=False):
        o = _Op(eng, fn, reads, writes, is_dma, force_inc)
        o.seq = len(self.per_eng[eng])
        self.per_eng[eng].append(o)
        self.ops.append(o)
        deps = []
        for r in o.reads:
            w = self.last_write.get(r)
            if w is not None:
                deps.append(w)
        for wn in o.writes:
            w = self.last_write.get(wn)
            if w is not None:
                deps.append(w)
            deps.extend(self.readers.get(wn, ()))
        best = {}
        for d in deps:
            if d is o:
                continue
            k = (d.eng, True, d.seq) if d.is_dma else (d.eng, False)
            if k not in best or d.seq > best[k].seq:
                best[k] = d
        o.waits = list(best.values())
        for r in o.reads:
            self.readers.setdefault(r, []).append(o)
        for wn in o.writes:
            self.last_write[wn] = o
            self.readers[wn] = []
        return o

    def dma(self, eng, fn, reads=(), writes=()):
        return self.op(eng, fn, reads, writes, is_dma=True)

    NLANES = 8

    def resolve(self):
        for o in self.ops:
            if o.is_dma or o.force_inc:
                o.inc = True
            real = []
            for d in o.waits:
                if d.eng == o.eng and not d.is_dma and not d.force_inc:
                    if d.eng != "tensor" or o.is_dma:
                        o.need_drain = True
                else:
                    d.inc = True
                    real.append(d)
            o.waits = real
        self.targets = {}
        self.self_wait = {}
        for e in ENGINES:
            cc = 0
            lane_use = [0] * self.NLANES
            dma_idx = 0
            for o in self.per_eng[e]:
                if not o.inc:
                    continue
                if o.is_dma:
                    lane = dma_idx % self.NLANES
                    dma_idx += 1
                    if lane_use[lane] > 0:
                        self.self_wait[(e, o.seq)] = ((e, "dma", lane),
                                                      16 * lane_use[lane])
                    lane_use[lane] += 1
                    self.targets[(e, True, o.seq)] = ((e, "dma", lane),
                                                      16 * lane_use[lane])
                    o.lane = lane
                else:
                    cc += 1
                    self.targets[(e, False, o.seq)] = ((e, "cmp"), cc)

    def make_runner(self, sems):
        waited = {}

        def run_engine(ename, eng):
            for o in self.per_eng[ename]:
                need = {}
                for d in o.waits:
                    sem_key, tgt = self.targets[(d.eng, d.is_dma, d.seq)]
                    if need.get(sem_key, 0) < tgt:
                        need[sem_key] = tgt
                for sem_key, tgt in need.items():
                    wk = (ename, sem_key)
                    if waited.get(wk, 0) < tgt:
                        eng.wait_ge(sems[sem_key], tgt)
                        waited[wk] = tgt
                sw = self.self_wait.get((ename, o.seq))
                if sw is not None and o.is_dma:
                    sem_key, tgt = sw
                    wk = (ename, sem_key)
                    if waited.get(wk, 0) < tgt:
                        eng.wait_ge(sems[sem_key], tgt)
                        waited[wk] = tgt
                if o.need_drain:
                    eng.drain()
                inst = o.fn(eng)
                if o.inc:
                    if o.is_dma:
                        inst.then_inc(sems[(ename, "dma", o.lane)], 16)
                    else:
                        inst.then_inc(sems[(ename, "cmp")], 1)
        return run_engine


def build_program(T, dbg=False):
    assert T % 512 == 0 and T >= 512
    NC = T // 512
    NT = T // 128

    nc = bass.Bass("TRN2", target_bir_lowering=False, debug=False)
    P = Plan()

    def din(name, shape):
        return nc.dram_tensor(name, shape, F32, kind="ExternalInput").ap()

    def dscr(name, shape):
        return nc.dram_tensor(name, shape, F32,
                              kind="ExternalOutput" if dbg else "Internal").ap()

    I = {}
    for name, shape in [
        ("x0T", [D, T]),
        ("WdT", [D, D]), ("WoT0", [D, D]), ("rwT", [D, NEXP]),
        ("moe_w1", [NEXP, D, MOE_H]), ("moe_w2", [NEXP, MOE_H, D]),
        ("WkT", [D, D]), ("WvT", [D, D]), ("WrT", [D, D]), ("WoT1", [D, D]),
        ("WfcT", [D, FCH]), ("WmlpT", [FCH, D]),
        ("slot0", [NSLOT, D]), ("w1a", [D, DH]), ("w1b", [D, DH]),
        ("w2s", [DH, NSLOT]), ("WihT", [D, GRU]), ("WhhT", [D, GRU]),
        ("wq_s", [D, DQ]), ("wk_s", [D, DQ]), ("wv_s", [D, D]),
        ("wvp_s", [D, D]), ("wo_s", [D, D]),
        ("bd_t", [128, DT]), ("bo_t", [128, DT]), ("a_log_t", [128, DT * NST]),
        ("bprm_t", [128, DT * NST]),
        ("td_t", [128, DT]), ("tf_t", [128, DT]), ("ms_t", [128, DT]),
        ("mls_t", [128, DT]),
        ("b1_t", [128, DH // 128]), ("b2_c", [NSLOT, 1]), ("tau_c", [1, 1]),
        ("bih_t", [128, GRU // 128]), ("bhh_t", [128, GRU // 128]),
        ("bq_t", [128, DQ // 128]), ("bk_t", [128, DQ // 128]),
        ("bv_t", [128, DT]), ("lng_t", [128, DT]), ("lnb_t", [128, DT]),
        ("bvp_t", [128, DT]), ("bo1_t", [128, DT]),
        ("ident", [128, 128]), ("onesk", [128, 128]), ("sel8", [NEXP, 128 * NEXP]),
        ("ramp0", [128, T]), ("ramp256", [128, 256]),
        ("m8", [128, 8]), ("im8", [128, 8]),
        ("mb8c", [NSLOT, 8]),
    ]:
        I[name] = din(name, shape)

    outT = nc.dram_tensor("outT", [D, T], F32, kind="ExternalOutput").ap()

    S = {}
    for name in ["xn0", "delta", "Cc", "y0", "s0", "g0", "x1", "xn1", "kv",
                 "rr", "hh", "xr", "s1", "g1", "sv"]:
        S[name] = dscr("d_" + name, [D, T])
    S["xn1N"] = dscr("d_xn1N", [T, D])
    S["xn0N"] = dscr("d_xn0N", [T, D])

    def dscr(name, shape, _d=dscr):
        return nc.dram_tensor(name, shape, F32, kind="Internal").ap()
    CC1 = 128 * DT * NST * 2
    S["cc1i"] = dscr("d_cc1i", [1, CC1])
    S["cc1o"] = dscr("d_cc1o", [8, CC1])
    S["c2ai"] = dscr("d_c2ai", [1, 128 * DT])
    S["c2ao"] = dscr("d_c2ao", [8, 128 * DT])
    CC2 = NSLOT * (D + 1)
    S["c2bi"] = dscr("d_c2bi", [1, CC2])
    S["c2bo"] = dscr("d_c2bo", [8, CC2])

    with contextlib.ExitStack() as stk:
        def sb(name, cols, rows=128, dtype=F32):
            return stk.enter_context(nc.sbuf_tensor(name, [rows, cols], dtype))

        rh = [sb("rh0", DT * 512), sb("rh1", DT * 512)]
        wsl = [sb("wsl0", 4096), sb("wsl1", 4096)]
        ob = sb("ob", DT * 512)
        hb = sb("hb", 4096)
        sc = [sb(f"sc{i}", T) for i in range(4)]
        ya = sb("ya", max(2 * T, DT * 512))
        sm = sb("sm", 256)
        sl = sb("sl", 4096)
        alp = sb("alp", NT * NSLOT)
        lgc = sb("lgc", 512, rows=NSLOT)
        b2sb = sb("b2sb", 1, rows=NSLOT)
        ss64 = sb("ss64", 16, rows=NSLOT)
        vecs = sb("vecs", 160)
        consts = sb("consts", 16)
        apost = sb("apost", DT * NST)
        napost = sb("napost", DT * NST)
        bprmt = sb("bprmt", DT * NST)
        identt = sb("identt", 128)
        oneskt = sb("oneskt", 128)
        sel8t = sb("sel8t", 128 * NEXP, rows=NEXP)
        m8t = sb("m8t", 8)
        im8t = sb("im8t", 8)
        mb8ct = sb("mb8ct", 8, rows=NSLOT)
        cc1s = sb("cc1s", DT * NST * 2)
        carry1 = sb("carry1", DT * NST)
        e1t = sb("e1t", DT)
        ne1t = sb("ne1t", DT)
        eut = sb("eut", DT)
        ewTt = sb("ewTt", DT)
        rwkvS = sb("rwkvS", DT)
        ewc = sb("ewc", DT)
        hprev = sb("hprev", DT)
        carry2 = sb("carry2", DT)
        sistage = sb("sistage", D + 1, rows=NSLOT)
        sigstg = sb("sigstg", D + 1, rows=NSLOT)
        gat8 = sb("gat8", T, rows=NEXP)
        rampt = sb("rampt", T)
        posm = sb("posm", NT * NEXP)
        gtm = sb("gtm", NT * NEXP)
        r256t = sb("r256t", 256)
        tiv = sb("tiv", 4)
        bh1 = sb("bh1", DH // 128)
        mski = sb("mski", NSLOT, dtype=mybir.dt.int32)
        gbt = sb("gbt", 512)

        def psumt(name):
            return stk.enter_context(nc.psum_tensor(name, [128, 512], F32))

        pm = [psumt(f"pm{i}") for i in range(4)]
        pb = [psumt(f"pb{i}") for i in range(2)]
        ptr = [psumt(f"ptr{i}") for i in range(2)]

        sems = {}
        for e in ENGINES:
            sems[(e, "cmp")] = stk.enter_context(nc.semaphore(f"s_{e}"))
        for e in ("sync", "gpsimd"):
            for ln_ in range(Plan.NLANES):
                sems[(e, "dma", ln_)] = stk.enter_context(
                    nc.semaphore(f"m_{e}{ln_}"))

        def LD(dst, src, rd, wr, eng="sync"):
            P.dma(eng, lambda E, dd=dst, ss=src: E.dma_start(out=dd, in_=ss), rd, wr)

        def STR(dst, src, rd, wr, eng="gpsimd"):
            P.dma(eng, lambda E, dd=dst, ss=src: E.dma_start(out=dd, in_=ss), rd, wr)

        def OP(eng, fn, rd=(), wr=()):
            P.op(eng, fn, rd, wr)

        def chunk_of(dram, c, width=512):
            return dram.rearrange("(k p) t -> p k t", p=128)[:, :, c * width:(c + 1) * width]

        def rhv(slot, k, width=512):
            return rh[slot][:, k * width:(k + 1) * width]

        def cname(name, c, width=512):
            c0 = c * width // 512
            c1 = (c * width + width + 511) // 512
            return [f"{name}@{i}" for i in range(c0, c1)]

        def allc(name):
            return [f"{name}@{i}" for i in range(NC)]

        def load_chunk(slot, dram, name, c, width=512):
            LD(rh[slot][:, :DT * width].rearrange("p (k t) -> p k t", k=DT),
               chunk_of(dram, c, width), cname(name, c, width), [f"rh{slot}"])

        def store_chunk(dram, name, c, src_sb, src_name, width=512):
            STR(chunk_of(dram, c, width),
                src_sb[:, :DT * width].rearrange("p (k t) -> p k t", k=DT),
                [src_name], cname(name, c, width))

        def MM(out_ap, bname, lhs_ap, lhs_names, rhs_ap, rhs_names, start, stop):
            OP("tensor",
               lambda E, o=out_ap, l=lhs_ap, r=rhs_ap, s1=start, s2=stop:
                   E.matmul(o, l, r, start=s1, stop=s2),
               list(lhs_names) + list(rhs_names), [bname])

        def mm_accum(bank_ap, bname, lhs_get, lhs_names, rhs_get, rhs_names, nK):
            for k in range(nK):
                MM(bank_ap, bname, lhs_get(k), lhs_names, rhs_get(k), rhs_names,
                   k == 0, k == nK - 1)

        def load_wblk(ws, Wd, Wname, nKt, c0, c1, k0=0):
            mcols = c1 - c0
            LD(wsl[ws][:, :nKt * mcols].rearrange("p (k m) -> p k m", k=nKt),
               Wd.rearrange("(k p) m -> p k m", p=128)[:, k0:k0 + nKt, c0:c1],
               [Wname], [f"wsl{ws}"])

        def big_mm(Wd, Wname, K, M, in_dram, in_name, evac, extra_per_chunk=None):
            nKt = K // 128
            nMb = (M + 511) // 512
            for c in range(NC):
                rslot = c % 2
                load_chunk(rslot, in_dram, in_name, c)
                if extra_per_chunk is not None:
                    extra_per_chunk(c)
                for mb in range(nMb):
                    mcols = min(512, M - mb * 512)
                    ws = mb % 2
                    load_wblk(ws, Wd, Wname, nKt, mb * 512, mb * 512 + mcols)
                    for mt in range(mcols // 128):
                        bank = pm[mt % 4]
                        bname = f"pm{mt % 4}"
                        mm_accum(bank[:, :], bname,
                                 lambda k, ws=ws, mt=mt, mc=mcols:
                                     wsl[ws][:, k * mc + mt * 128:k * mc + mt * 128 + 128],
                                 [f"wsl{ws}"],
                                 lambda k, rs=rslot: rhv(rs, k), [f"rh{rslot}"], nKt)
                        evac(mb * 4 + mt, c, bank, bname)

        LD(identt[:], I["ident"][:, :], ["i_ident"], ["identt"])
        LD(oneskt[:], I["onesk"][:, :], ["i_onesk"], ["oneskt"])
        LD(sel8t[:], I["sel8"][:, :], ["i_sel8"], ["sel8t"])
        LD(m8t[:], I["m8"][:, :], ["i_m8"], ["m8t"])
        LD(im8t[:], I["im8"][:, :], ["i_im8"], ["im8t"])
        LD(mb8ct[:], I["mb8c"][:, :], ["i_mb8c"], ["mb8ct"])
        LD(b2sb[:], I["b2_c"][:, :], ["i_b2"], ["b2sb"])
        LD(rampt[:], I["ramp0"][:, :], ["i_ramp0"], ["rampt"])
        LD(r256t[:], I["ramp256"][:, :], ["i_ramp256"], ["r256t"])
        VC = {}
        col = 0
        for nm_, wd in [("bd", DT), ("bo", DT), ("td", DT), ("tf", DT), ("ms", DT),
                        ("mls", DT), ("b1", DH // 128), ("bih", GRU // 128),
                        ("bhh", GRU // 128), ("bq", DQ // 128), ("bk", DQ // 128),
                        ("bv", DT), ("lng", DT), ("lnb", DT), ("bvp", DT), ("bo1", DT)]:
            VC[nm_] = col
            LD(vecs[:, col:col + wd], I[nm_ + "_t"][:, :], [f"i_{nm_}"], ["vecs"])
            col += wd

        def vc(nm_, idx=0):
            c0 = VC[nm_] + idx
            return vecs[:, c0:c0 + 1]

        CN = {"eps5": 0, "eps6": 1, "invD": 2, "one": 3, "neg30": 4, "scq": 5,
              "invS": 6, "eps8": 7, "mhalf": 8, "mone": 9, "negT": 10, "half": 11,
              "nscq": 12}
        for nm_, v in [("eps5", 1e-5), ("eps6", 1e-6), ("invD", 1.0 / D),
                       ("one", 1.0), ("neg30", -1e30), ("scq", DQ ** -0.5),
                       ("invS", 1.0 / NSLOT), ("eps8", 1e-8), ("mhalf", -0.5),
                       ("mone", -1.0), ("negT", float(-T)), ("half", 0.5),
                       ("nscq", -(DQ ** -0.5))]:
            OP("vector", lambda E, v=v, j=CN[nm_]: E.memset(consts[:, j:j + 1], v),
               (), ["consts"])

        def cn(nm_, rows=128):
            j = CN[nm_]
            return consts[0:rows, j:j + 1]

        nc.const_aps.register(F32, -0.5, consts[:, CN["mhalf"]:CN["mhalf"] + 1])
        nc.const_aps.register(F32, -1.0, consts[:, CN["mone"]:CN["mone"] + 1])
        nc.const_aps.register(F32, float(-T), consts[:, CN["negT"]:CN["negT"] + 1])

        LD(sm[:, 0:DT * NST], I["a_log_t"][:, :], ["i_alog"], ["sm"])
        OP("scalar", lambda E: E.activation(apost[:], sm[:, 0:DT * NST], ACTF.Abs),
           ["sm"], ["apost"])
        OP("scalar", lambda E: E.activation(apost[:], apost[:], ACTF.Exp, scale=-1.0),
           ["apost"], ["apost"])
        OP("scalar", lambda E: E.activation(apost[:], apost[:], ACTF.Ln,
                                            bias=cn("one")), ["apost", "consts"],
           ["apost"])
        OP("vector", lambda E: E.scalar_tensor_tensor(
            apost[:], sm[:, 0:DT * NST], 0.0, apost[:], ALU.max, ALU.add),
            ["sm", "apost"], ["apost"])
        OP("vector", lambda E: E.tensor_scalar(apost[:], apost[:], 1e-4, None, ALU.add),
           ["apost"], ["apost"])
        OP("vector", lambda E: E.tensor_scalar(napost[:], apost[:], -1.0, None, ALU.mult),
           ["apost"], ["napost"])
        LD(bprmt[:], I["bprm_t"][:, :], ["i_bprm"], ["bprmt"])

        LD(tiv[0:1, 0:1], I["tau_c"][:, :], ["i_tau"], ["tiv"])
        OP("scalar", lambda E: E.activation(tiv[0:1, 1:2], tiv[0:1, 0:1], ACTF.Abs),
           ["tiv"], ["tiv"])
        OP("vector", lambda E: E.tensor_scalar(tiv[0:1, 1:2], tiv[0:1, 1:2], 0.1,
                                               None, ALU.add), ["tiv"], ["tiv"])
        OP("vector", lambda E: E.reciprocal(tiv[0:1, 2:3], tiv[0:1, 1:2]),
           ["tiv"], ["tiv"])
        OP("tensor", lambda E: E.matmul(ptr[0][:, 0:1], oneskt[0:1, :], tiv[0:1, 2:3],
                                        start=True, stop=True),
           ["oneskt", "tiv"], ["ptr0"])
        OP("vector", lambda E: E.tensor_copy(tiv[:, 3:4], ptr[0][:, 0:1]),
           ["ptr0"], ["tiv"])

        OP("scalar", lambda E: E.activation(e1t[:, :], vecs[:, VC["td"]:VC["td"] + DT],
                                            ACTF.Exp), ["vecs"], ["e1t"])
        OP("vector", lambda E: E.tensor_scalar(ne1t[:, :], e1t[:, :], -1.0, None,
                                               ALU.mult), ["e1t"], ["ne1t"])
        OP("scalar", lambda E: E.activation(eut[:, :], vecs[:, VC["tf"]:VC["tf"] + DT],
                                            ACTF.Exp), ["vecs"], ["eut"])
        OP("scalar", lambda E: E.activation(ewTt[:, :], e1t[:, :], ACTF.Exp,
                                            scale=float(-T)), ["e1t"], ["ewTt"])

        def stage_norm(src, srcn, dst, dstn, kind):
            for c in range(NC):
                cs = slice(c * 512, (c + 1) * 512)
                slot = c % 2
                load_chunk(slot, src, srcn, c)
                if kind == "ln":
                    mm_accum(pb[0][:, :], "pb0", lambda k: oneskt[:], ["oneskt"],
                             lambda k, s=slot: rhv(s, k), [f"rh{slot}"], DT)
                    OP("vector", lambda E, cs=cs: E.tensor_scalar(
                        sc[1][:, cs], pb[0][:, :], cn("invD"), None, ALU.mult),
                        ["pb0", "consts"], ["sc1"])
                for k in range(DT):
                    half = (k % 2) * 512 if T >= 1024 else 0
                    OP("vector", lambda E, k=k, s=slot, h=half: E.tensor_mul(
                        sc[0][:, h:h + 512], rhv(s, k), rhv(s, k)),
                        [f"rh{slot}"], ["sc0"])
                    OP("tensor", lambda E, k=k, h=half: E.matmul(
                        pb[1][:, :], oneskt[:], sc[0][:, h:h + 512],
                        start=(k == 0), stop=(k == DT - 1)),
                        ["sc0", "oneskt"], ["pb1"])
                OP("vector", lambda E, cs=cs: E.tensor_scalar(
                    sc[2][:, cs], pb[1][:, :], cn("invD"), None, ALU.mult),
                    ["pb1", "consts"], ["sc2"])
                if kind == "ln":
                    OP("vector", lambda E, cs=cs: E.tensor_mul(
                        sc[3][:, cs], sc[1][:, cs], sc[1][:, cs]), ["sc1"], ["sc3"])
                    OP("vector", lambda E, cs=cs: E.tensor_sub(
                        sc[2][:, cs], sc[2][:, cs], sc[3][:, cs]),
                        ["sc2", "sc3"], ["sc2"])
                OP("scalar", lambda E, cs=cs: E.activation(
                    sc[2][:, cs], sc[2][:, cs], ACTF.Ln,
                    bias=cn("eps5") if kind == "ln" else cn("eps6")),
                    ["sc2", "consts"], ["sc2"])
                OP("scalar", lambda E, cs=cs: E.activation(
                    sc[2][:, cs], sc[2][:, cs], ACTF.Exp, scale=-0.5),
                    ["sc2"], ["sc2"])
                for k in range(DT):
                    if kind == "ln":
                        OP("vector", lambda E, k=k, s=slot, cs=cs: E.tensor_sub(
                            ob[:, k * 512:(k + 1) * 512], rhv(s, k), sc[1][:, cs]),
                            [f"rh{slot}", "sc1"], ["ob"])
                        OP("vector", lambda E, k=k, cs=cs: E.tensor_mul(
                            ob[:, k * 512:(k + 1) * 512], ob[:, k * 512:(k + 1) * 512],
                            sc[2][:, cs]), ["ob", "sc2"], ["ob"])
                    else:
                        OP("vector", lambda E, k=k, s=slot, cs=cs: E.tensor_mul(
                            ob[:, k * 512:(k + 1) * 512], rhv(s, k), sc[2][:, cs]),
                            [f"rh{slot}", "sc2"], ["ob"])
                store_chunk(dst, dstn, c, ob, "ob")

        stage_norm(I["x0T"], "i_x0T", S["xn0"], "d_xn0", "ln")

        for tt in range(NT):
            load_chunk(0, S["xn0"], "d_xn0", tt, width=128)
            for k in range(DT):
                OP("tensor", lambda E, k=k: E.transpose(
                    ptr[k % 2][:, 0:128], rh[0][:, k * 128:(k + 1) * 128], identt[:]),
                    ["rh0", "identt"], [f"ptr{k % 2}"])
                OP("vector", lambda E, k=k: E.tensor_copy(
                    ob[:, k * 128:(k + 1) * 128], ptr[k % 2][:, 0:128]),
                    [f"ptr{k % 2}"], ["ob"])
            STR(S["xn0N"][tt * 128:(tt + 1) * 128, :], ob[:, 0:D], ["ob"],
                [f"d_xn0N@{tt}"])

        def evac_delta(mt, c, bank, bname):
            half = (mt % 2) * 512 if T >= 1024 else 0
            o_t = ob[:, mt * 512:(mt + 1) * 512]
            s_t = sc[0][:, half:half + 512]
            OP("scalar", lambda E, bank=bank, o_t=o_t: E.activation(
                o_t, bank[:, :], ACTF.Identity, bias=vc("bd", mt)),
                [bname, "vecs"], ["ob"])
            OP("scalar", lambda E, o_t=o_t, s_t=s_t: E.activation(
                s_t, o_t, ACTF.Abs), ["ob"], ["sc0"])
            OP("scalar", lambda E, s_t=s_t: E.activation(
                s_t, s_t, ACTF.Exp, scale=-1.0), ["sc0"], ["sc0"])
            OP("scalar", lambda E, s_t=s_t: E.activation(
                s_t, s_t, ACTF.Ln, bias=cn("one")), ["sc0", "consts"], ["sc0"])
            OP("vector", lambda E, o_t=o_t, s_t=s_t: E.scalar_tensor_tensor(
                o_t, o_t, 0.0, s_t, ALU.max, ALU.add), ["ob", "sc0"], ["ob"])
            if mt == DT - 1:
                store_chunk(S["delta"], "d_delta", c, ob, "ob")
        big_mm(I["WdT"], "i_WdT", D, D, S["xn0"], "d_xn0", evac_delta)

        side = []

        def SIDE(fn):
            side.append(fn)

        def s3_load(et):
            LD(sc[0][:, :], S["delta"][et * 128:(et + 1) * 128, :], allc("d_delta"),
               ["sc0"], eng="gpsimd")
            LD(sc[1][:, :], S["xn0"][et * 128:(et + 1) * 128, :], allc("d_xn0"),
               ["sc1"], eng="gpsimd")
            OP("gpsimd", lambda E: E.tensor_mul(sc[1][:, :], sc[0][:, :], sc[1][:, :]),
               ["sc0", "sc1"], ["sc1"])
            OP("vector", lambda E: E.tensor_tensor_scan(
                sc[2][:, :], cn("one").broadcast_to([128, T]), sc[0][:, :], 0.0,
                ALU.mult, ALU.add), ["sc0", "consts"], ["sc2"])
            STR(S["Cc"][et * 128:(et + 1) * 128, :], sc[2][:, :], ["sc2"],
                [f"d_Cc@r{et}"])
            for n in range(NST):
                ci = et * NST + n
                OP("scalar", lambda E, ci=ci: E.activation(
                    cc1s[:, ci:ci + 1], sc[2][:, T - 1:T], ACTF.Exp,
                    scale=napost[:, ci:ci + 1]), ["sc2", "napost"], ["cc1s"])

        def s3_n(et, n):
            ci = et * NST + n
            OP("scalar", lambda E, ci=ci: E.activation(
                sc[3][:, :], sc[0][:, :], ACTF.Exp, scale=napost[:, ci:ci + 1]),
                ["sc0", "napost"], ["sc3"])
            OP("gpsimd", lambda E, ci=ci: E.tensor_tensor(
                ya[:, T:2 * T], sc[1][:, :],
                bprmt[:, ci:ci + 1].broadcast_to([128, T]), ALU.mult),
                ["sc1", "bprmt"], ["ya"])
            OP("vector", lambda E: E.tensor_tensor_scan(
                ya[:, T:2 * T], sc[3][:, :], ya[:, T:2 * T], 0.0, ALU.mult,
                ALU.add), ["sc3", "ya"], ["ya"])
            OP("gpsimd", lambda E, ci=ci: E.tensor_copy(
                cc1s[:, DT * NST + ci:DT * NST + ci + 1], ya[:, 2 * T - 1:2 * T]),
                ["ya"], ["cc1s"])
            if n == 0:
                OP("gpsimd", lambda E: E.tensor_copy(ya[:, 0:T], ya[:, T:2 * T]),
                   ["ya"], ["ya"])
            else:
                OP("gpsimd", lambda E: E.tensor_add(
                    ya[:, 0:T], ya[:, 0:T], ya[:, T:2 * T]), ["ya", "ya"], ["ya"])

        def s3_store(et):
            STR(S["y0"][et * 128:(et + 1) * 128, :], ya[:, 0:T], ["ya"],
                [f"d_y0@r{et}"])

        for et in range(DT):
            SIDE(lambda et=et: s3_load(et))
            for n in range(NST):
                SIDE(lambda et=et, n=n: s3_n(et, n))
            SIDE(lambda et=et: s3_store(et))

        def s4_trigger():
            STR(S["cc1i"][0, :].rearrange("(p c) -> p c", p=128), cc1s[:, :],
                ["cc1s"], ["d_cc1i"])
            P.op("gpsimd", lambda E: E.collective_compute(
                "AllGather", ALU.bypass, replica_groups=[list(range(8))],
                ins=[S["cc1i"].opt()], outs=[S["cc1o"].opt()]),
                ["d_cc1i"], ["d_cc1o"], force_inc=True)

        def s4_unit():
            NB = DT * NST
            OP("vector", lambda E: E.memset(carry1[:, :], 0.0), (), ["carry1"])
            for j in range(8):
                LD(cc1s[:, :], S["cc1o"][j, :].rearrange("(p c) -> p c", p=128),
                   ["d_cc1o"], ["cc1s"], eng="gpsimd")
                OP("vector", lambda E, j=j: E.tensor_scalar(
                    sm[:, 0:NB], cc1s[:, 0:NB], m8t[:, j:j + 1],
                    im8t[:, j:j + 1], ALU.mult, ALU.add),
                    ["cc1s", "m8t", "im8t"], ["sm"])
                OP("vector", lambda E: E.tensor_mul(carry1[:, :], carry1[:, :],
                                                    sm[:, 0:NB]), ["carry1", "sm"],
                   ["carry1"])
                OP("vector", lambda E, j=j: E.tensor_scalar(
                    sm[:, 0:NB], cc1s[:, NB:2 * NB],
                    m8t[:, j:j + 1], None, ALU.mult), ["cc1s", "m8t"], ["sm"])
                OP("vector", lambda E: E.tensor_add(carry1[:, :], carry1[:, :],
                                                    sm[:, 0:NB]), ["carry1", "sm"],
                   ["carry1"])

        SIDE(s4_trigger)
        for _ in range(64):
            SIDE(None)
        SIDE(s4_unit)

        def s5_load(et):
            LD(sc[0][:, :], S["Cc"][et * 128:(et + 1) * 128, :], [f"d_Cc@r{et}"],
               ["sc0"], eng="gpsimd")
            LD(ya[:, 0:T], S["y0"][et * 128:(et + 1) * 128, :], [f"d_y0@r{et}"],
               ["ya"], eng="gpsimd")

        def s5_n(et, n):
            ci = et * NST + n
            OP("scalar", lambda E, ci=ci: E.activation(
                sc[3][:, :], sc[0][:, :], ACTF.Exp, scale=napost[:, ci:ci + 1]),
                ["sc0", "napost"], ["sc3"])
            OP("vector", lambda E, ci=ci: E.scalar_tensor_tensor(
                ya[:, 0:T], sc[3][:, :], carry1[:, ci:ci + 1], ya[:, 0:T],
                ALU.mult, ALU.add), ["sc3", "carry1", "ya"], ["ya"])

        def s5_store(et):
            STR(S["y0"][et * 128:(et + 1) * 128, :], ya[:, 0:T], ["ya"],
                [f"d_y0@r{et}"])

        for et in range(DT):
            SIDE(lambda et=et: s5_load(et))
            for n in range(NST):
                SIDE(lambda et=et, n=n: s5_n(et, n))
            SIDE(lambda et=et: s5_store(et))

        LD(wsl[0][:, 0:DT * NEXP].rearrange("p (k m) -> p k m", k=DT),
           I["rwT"].rearrange("(k p) m -> p k m", p=128), ["i_rwT"], ["wsl0"])
        for c in range(NC):
            slot = c % 2
            load_chunk(slot, S["xn0"], "d_xn0", c)
            mm_accum(pb[0][0:NEXP, :], "pb0",
                     lambda k: wsl[0][:, k * NEXP:(k + 1) * NEXP], ["wsl0"],
                     lambda k, s=slot: rhv(s, k), [f"rh{slot}"], DT)
            OP("vector", lambda E: E.tensor_copy(hb[0:NEXP, 0:512], pb[0][0:NEXP, :]),
               ["pb0"], ["hb"])
            for tti in range(4):
                tt = c * 4 + tti
                OP("tensor", lambda E, tti=tti: E.transpose(
                    ptr[0][:, 0:NEXP], hb[0:NEXP, tti * 128:(tti + 1) * 128],
                    identt[0:NEXP, 0:NEXP]), ["hb", "identt"], ["ptr0"])
                OP("vector", lambda E: E.tensor_copy(sm[:, 0:NEXP], ptr[0][:, 0:NEXP]),
                   ["ptr0"], ["sm"])
                OP("vector", lambda E: E.reduce_max(sm[:, 16:17], sm[:, 0:NEXP],
                                                    axis=AX.X), ["sm"], ["sm"])
                OP("vector", lambda E: E.tensor_copy(sm[:, 8:8 + NEXP], sm[:, 0:NEXP]),
                   ["sm"], ["sm"])
                OP("vector", lambda E: E.tensor_scalar(
                    mski[:, 0:NEXP], sm[:, 0:NEXP], sm[:, 16:17], None,
                    ALU.is_ge), ["sm"], ["mski"])
                OP("vector", lambda E: E.copy_predicated(
                    sm[:, 8:8 + NEXP], mski[:, 0:NEXP],
                    cn("neg30").broadcast_to([128, NEXP])),
                    ["sm", "mski", "consts"], ["sm"])
                OP("vector", lambda E: E.reduce_max(sm[:, 17:18], sm[:, 8:8 + NEXP],
                                                    axis=AX.X), ["sm"], ["sm"])
                OP("vector", lambda E: E.tensor_scalar(
                    sm[:, 18:19], sm[:, 16:17], -1.0, None, ALU.mult), ["sm"], ["sm"])
                OP("scalar", lambda E: E.activation(
                    sm[:, 24:24 + NEXP], sm[:, 0:NEXP], ACTF.Exp, bias=sm[:, 18:19]),
                    ["sm"], ["sm"])
                OP("scalar", lambda E: E.activation(
                    sm[:, 19:20], sm[:, 17:18], ACTF.Exp, bias=sm[:, 18:19]),
                    ["sm"], ["sm"])
                OP("vector", lambda E: E.tensor_scalar(
                    sm[:, 19:20], sm[:, 19:20], 1.0, None, ALU.add), ["sm"], ["sm"])
                OP("vector", lambda E: E.reciprocal(sm[:, 20:21], sm[:, 19:20]),
                   ["sm"], ["sm"])
                OP("vector", lambda E: E.tensor_scalar(
                    sm[:, 32:32 + NEXP], sm[:, 0:NEXP], sm[:, 17:18], None,
                    ALU.is_ge), ["sm"], ["sm"])
                OP("vector", lambda E: E.tensor_scalar(
                    sm[:, 24:24 + NEXP], sm[:, 24:24 + NEXP], sm[:, 20:21], None,
                    ALU.mult), ["sm"], ["sm"])
                OP("vector", lambda E: E.tensor_mul(
                    sm[:, 24:24 + NEXP], sm[:, 24:24 + NEXP], sm[:, 32:32 + NEXP]),
                    ["sm"], ["sm"])
                OP("vector", lambda E, tt=tt: E.tensor_copy(
                    gtm[:, tt * NEXP:(tt + 1) * NEXP], sm[:, 24:24 + NEXP]),
                    ["sm"], ["gtm"])
                OP("tensor", lambda E: E.transpose(
                    ptr[1][0:NEXP, 0:128], sm[:, 24:24 + NEXP], identt[:]),
                    ["sm", "identt"], ["ptr1"])
                OP("vector", lambda E, tt=tt: E.tensor_copy(
                    gat8[0:NEXP, tt * 128:(tt + 1) * 128], ptr[1][0:NEXP, 0:128]),
                    ["ptr1"], ["gat8"])
            OP("vector", lambda E, c=c: E.tensor_scalar(
                hb[0:NEXP, 512:1024], gat8[0:NEXP, c * 512:(c + 1) * 512], 0.0,
                None, ALU.is_gt), ["gat8"], ["hb"])
            OP("vector", lambda E: E.tensor_tensor_scan(
                hb[0:NEXP, 1024:1536], cn("one", NEXP).broadcast_to([NEXP, 512]),
                hb[0:NEXP, 512:1024], 0.0, ALU.mult, ALU.add),
                ["hb", "consts"], ["hb"])
            OP("vector", lambda E: E.tensor_mul(
                hb[0:NEXP, 1024:1536], hb[0:NEXP, 1024:1536],
                hb[0:NEXP, 512:1024]), ["hb"], ["hb"])
            OP("vector", lambda E: E.tensor_scalar(
                hb[0:NEXP, 1024:1536], hb[0:NEXP, 1024:1536], -1.0, None,
                ALU.add), ["hb"], ["hb"])
            for tti in range(4):
                tt = c * 4 + tti
                OP("tensor", lambda E, tti=tti: E.transpose(
                    ptr[0][:, 0:NEXP], hb[0:NEXP, 1024 + tti * 128:1024 + (tti + 1) * 128],
                    identt[0:NEXP, 0:NEXP]), ["hb", "identt"], ["ptr0"])
                OP("vector", lambda E, tt=tt: E.tensor_copy(
                    posm[:, tt * NEXP:(tt + 1) * NEXP], ptr[0][:, 0:NEXP]),
                    ["ptr0"], ["posm"])

        n_points = NC * NEXP * 24
        n_side = len(side)
        side_i = [0]
        point_i = [0]

        def pop_side(k):
            point_i[0] += k
            want = min(n_side, (point_i[0] * n_side * 10) // (n_points * 9))
            while side_i[0] < want:
                u = side[side_i[0]]
                if u is not None:
                    r = u()
                side_i[0] += 1

        def drain_side():
            while side_i[0] < len(side):
                u = side[side_i[0]]
                if u is not None:
                    u()
                side_i[0] += 1

        for c in range(NC):
            for tti in range(4):
                tt = c * 4 + tti
                LD(rh[0][:, tti * 1024:(tti + 1) * 1024], S["xn0N"][tt * 128:(tt + 1) * 128, :],
                   [f"d_xn0N@{tt}"], ["rh0"])
            for e in range(NEXP):
                for tti in range(4):
                    tt = c * 4 + tti
                    OP("vector", lambda E, tti=tti, tt=tt, e=e: E.tensor_scalar(
                        hb[:, tti * 256:(tti + 1) * 256], r256t[:, :],
                        posm[:, tt * NEXP + e:tt * NEXP + e + 1], None,
                        ALU.is_equal), ["r256t", "posm"], ["hb"])
                for dtile in range(DT):
                    bank, bname = pm[dtile % 2], f"pm{dtile % 2}"
                    for tti in range(4):
                        MM(bank[:, 0:256], bname,
                           rh[0][:, tti * 1024 + dtile * 128:tti * 1024 + dtile * 128 + 128],
                           ["rh0"], hb[:, tti * 256:(tti + 1) * 256], ["hb"],
                           tti == 0, tti == 3)
                    OP("scalar", lambda E, dtile=dtile, bank=bank, bname=bname:
                       E.copy(rh[1][:, dtile * 256:(dtile + 1) * 256],
                              bank[:, 0:256]), [bname], ["rh1"])
                pop_side(2)
                for half in range(2):
                    for hmb in range(2):
                        ws = hmb
                        h0 = half * 1024 + hmb * 512
                        load_wblk(ws, I["moe_w1"][e], "i_moe_w1", DT, h0, h0 + 512)
                        for mt in range(4):
                            hm = half * 8 + hmb * 4 + mt
                            bank, bname = pm[hm % 4], f"pm{hm % 4}"
                            mm_accum(bank[:, 0:256], bname,
                                     lambda k, ws=ws, mt=mt:
                                         wsl[ws][:, k * 512 + mt * 128:k * 512 + mt * 128 + 128],
                                     [f"wsl{ws}"],
                                     lambda k: rh[1][:, k * 256:(k + 1) * 256],
                                     ["rh1"], DT)
                            OP("scalar", lambda E, hm=hm, bank=bank, bname=bname:
                               E.activation(hb[:, hm * 256:(hm + 1) * 256],
                                            bank[:, 0:256], ACTF.Relu),
                               [bname], ["hb"])
                            OP("scalar", lambda E, hm=hm: E.activation(
                                hb[:, hm * 256:(hm + 1) * 256],
                                hb[:, hm * 256:(hm + 1) * 256], ACTF.Square),
                                ["hb"], ["hb"])
                            pop_side(1)
                for ht in range(16):
                    ws = ht % 2
                    LD(wsl[ws][:, 0:1024], I["moe_w2"][e][ht * 128:(ht + 1) * 128, :],
                       ["i_moe_w2"], [f"wsl{ws}"])
                    for jh in range(2):
                        for dh in range(2):
                            bank, bname = pm[jh * 2 + dh], f"pm{jh * 2 + dh}"
                            MM(bank[:, :], bname,
                               hb[:, ht * 256 + jh * 128:ht * 256 + jh * 128 + 128],
                               ["hb"], wsl[ws][:, dh * 512:(dh + 1) * 512],
                               [f"wsl{ws}"], ht == 0, ht == 15)
                    if ht % 4 == 3:
                        pop_side(1)
                for jh in range(2):
                    for dh in range(2):
                        bank, bname = pm[jh * 2 + dh], f"pm{jh * 2 + dh}"
                        OP("scalar", lambda E, jh=jh, dh=dh, bank=bank, bname=bname:
                           E.copy(sl[:, jh * 1024 + dh * 512:jh * 1024 + dh * 512 + 512],
                                  bank[:, :]), [bname], ["sl"])
                for tti in range(4):
                    tt = c * 4 + tti
                    OP("vector", lambda E, tti=tti, tt=tt, e=e: E.tensor_scalar(
                        hb[:, tti * 256:(tti + 1) * 256], r256t[:, :],
                        posm[:, tt * NEXP + e:tt * NEXP + e + 1], None,
                        ALU.is_equal), ["r256t", "posm"], ["hb"])
                    OP("vector", lambda E, tti=tti, tt=tt, e=e: E.tensor_scalar(
                        hb[:, tti * 256:(tti + 1) * 256],
                        hb[:, tti * 256:(tti + 1) * 256],
                        gtm[:, tt * NEXP + e:tt * NEXP + e + 1], None,
                        ALU.mult), ["hb", "gtm"], ["hb"])
                    for jh in range(2):
                        OP("tensor", lambda E, tti=tti, jh=jh: E.transpose(
                            ptr[jh][:, 0:128],
                            hb[:, tti * 256 + jh * 128:tti * 256 + jh * 128 + 128],
                            identt[:]), ["hb", "identt"], [f"ptr{jh}"])
                        OP("vector", lambda E, tti=tti, jh=jh: E.tensor_copy(
                            hb[:, 2048 + (tti * 2 + jh) * 128:2048 + (tti * 2 + jh) * 128 + 128],
                            ptr[jh][:, 0:128]), [f"ptr{jh}"], ["hb"])
                for tti in range(4):
                    for dh in range(2):
                        bank, bname = pb[dh], f"pb{dh}"
                        for jh in range(2):
                            MM(bank[:, :], bname,
                               hb[:, 2048 + (tti * 2 + jh) * 128:2048 + (tti * 2 + jh) * 128 + 128],
                               ["hb"], sl[:, jh * 1024 + dh * 512:jh * 1024 + dh * 512 + 512],
                               ["sl"], jh == 0, jh == 1)
                        if e == 0:
                            OP("vector", lambda E, tti=tti, dh=dh, bank=bank,
                               bname=bname: E.tensor_copy(
                                   ob[:, tti * 1024 + dh * 512:tti * 1024 + dh * 512 + 512],
                                   bank[:, :]), [bname], ["ob"])
                        else:
                            OP("vector", lambda E, tti=tti, dh=dh, bank=bank,
                               bname=bname: E.tensor_add(
                                   ob[:, tti * 1024 + dh * 512:tti * 1024 + dh * 512 + 512],
                                   ob[:, tti * 1024 + dh * 512:tti * 1024 + dh * 512 + 512],
                                   bank[:, :]), ["ob", bname], ["ob"])
                pop_side(2)
            OP("scalar", lambda E: E.activation(ob[:, 0:2048], ob[:, 0:2048],
                                                ACTF.Silu), ["ob"], ["ob"])
            OP("scalar", lambda E: E.activation(ob[:, 2048:4096], ob[:, 2048:4096],
                                                ACTF.Silu), ["ob"], ["ob"])
            for tti in range(4):
                for dtile in range(DT):
                    pk = (tti * DT + dtile) % 2
                    OP("tensor", lambda E, tti=tti, dtile=dtile, pk=pk: E.transpose(
                        ptr[pk][:, 0:128],
                        ob[:, tti * 1024 + dtile * 128:tti * 1024 + dtile * 128 + 128],
                        identt[:]), ["ob", "identt"], [f"ptr{pk}"])
                    OP("vector", lambda E, tti=tti, dtile=dtile, pk=pk: E.tensor_copy(
                        hb[:, dtile * 512 + tti * 128:dtile * 512 + tti * 128 + 128],
                        ptr[pk][:, 0:128]), [f"ptr{pk}"], ["hb"])
            store_chunk(S["g0"], "d_g0", c, hb, "hb")
        drain_side()

        for c in range(NC):
            rslot = c % 2
            oslot = 1 - rslot
            cs = slice(c * 512, (c + 1) * 512)
            LD(rh[rslot][:, :DT * 512].rearrange("p (k t) -> p k t", k=DT),
               chunk_of(S["y0"], c), [f"d_y0@r{i}" for i in range(DT)],
               [f"rh{rslot}"])
            LD(hb[:, 0:DT * 512].rearrange("p (k t) -> p k t", k=DT),
               chunk_of(S["g0"], c), cname("d_g0", c), ["hb"])
            LD(rh[oslot][:, :DT * 512].rearrange("p (k t) -> p k t", k=DT),
               chunk_of(I["x0T"], c), ["i_x0T"], [f"rh{oslot}"])
            for mb in range(2):
                ws = mb % 2
                load_wblk(ws, I["WoT0"], "i_WoT0", DT, mb * 512, (mb + 1) * 512)
                for mt in range(4):
                    m_g = mb * 4 + mt
                    bank, bname = pm[mt % 2], f"pm{mt % 2}"
                    mm_accum(bank[:, :], bname,
                             lambda k, ws=ws, mt=mt: wsl[ws][:, k * 512 + mt * 128:k * 512 + mt * 128 + 128],
                             [f"wsl{ws}"],
                             lambda k, rs=rslot: rhv(rs, k), [f"rh{rslot}"], DT)
                    OP("scalar", lambda E, m_g=m_g, bank=bank, bname=bname: E.activation(
                        ob[:, m_g * 512:(m_g + 1) * 512], bank[:, :], ACTF.Identity,
                        bias=vc("bo", m_g)), [bname, "vecs"], ["ob"])
            for k in range(DT):
                OP("vector", lambda E, k=k: E.tensor_mul(
                    ob[:, k * 512:(k + 1) * 512], ob[:, k * 512:(k + 1) * 512],
                    hb[:, k * 512:(k + 1) * 512]), ["ob", "hb"], ["ob"])
                OP("vector", lambda E, k=k, s=oslot: E.tensor_add(
                    ob[:, k * 512:(k + 1) * 512], ob[:, k * 512:(k + 1) * 512],
                    rhv(s, k)), ["ob", f"rh{oslot}"], ["ob"])
            store_chunk(S["x1"], "d_x1", c, ob, "ob")
            mm_accum(pb[0][:, :], "pb0", lambda k: oneskt[:], ["oneskt"],
                     lambda k: ob[:, k * 512:(k + 1) * 512], ["ob"], DT)
            OP("vector", lambda E, cs=cs: E.tensor_scalar(
                sc[1][:, cs], pb[0][:, :], cn("invD"), None, ALU.mult),
                ["pb0", "consts"], ["sc1"])
            for k in range(DT):
                half = (k % 2) * 512 if T >= 1024 else 0
                OP("vector", lambda E, k=k, h=half: E.tensor_mul(
                    sc[0][:, h:h + 512], ob[:, k * 512:(k + 1) * 512],
                    ob[:, k * 512:(k + 1) * 512]), ["ob"], ["sc0"])
                OP("tensor", lambda E, k=k, h=half: E.matmul(
                    pb[1][:, :], oneskt[:], sc[0][:, h:h + 512],
                    start=(k == 0), stop=(k == DT - 1)), ["sc0", "oneskt"], ["pb1"])
            OP("vector", lambda E, cs=cs: E.tensor_scalar(
                sc[2][:, cs], pb[1][:, :], cn("invD"), None, ALU.mult),
                ["pb1", "consts"], ["sc2"])
            OP("vector", lambda E, cs=cs: E.tensor_mul(
                sc[3][:, cs], sc[1][:, cs], sc[1][:, cs]), ["sc1"], ["sc3"])
            OP("vector", lambda E, cs=cs: E.tensor_sub(
                sc[2][:, cs], sc[2][:, cs], sc[3][:, cs]), ["sc2", "sc3"], ["sc2"])
            OP("scalar", lambda E, cs=cs: E.activation(
                sc[2][:, cs], sc[2][:, cs], ACTF.Ln, bias=cn("eps5")),
                ["sc2", "consts"], ["sc2"])
            OP("scalar", lambda E, cs=cs: E.activation(
                sc[2][:, cs], sc[2][:, cs], ACTF.Exp, scale=-0.5), ["sc2"], ["sc2"])
            for k in range(DT):
                OP("vector", lambda E, k=k, cs=cs: E.tensor_sub(
                    hb[:, k * 512:(k + 1) * 512], ob[:, k * 512:(k + 1) * 512],
                    sc[1][:, cs]), ["ob", "sc1"], ["hb"])
                OP("vector", lambda E, k=k, cs=cs: E.tensor_mul(
                    hb[:, k * 512:(k + 1) * 512], hb[:, k * 512:(k + 1) * 512],
                    sc[2][:, cs]), ["hb", "sc2"], ["hb"])
            store_chunk(S["xn1"], "d_xn1", c, hb, "hb")
            for tti in range(4):
                tt = c * 4 + tti
                for k in range(DT):
                    OP("tensor", lambda E, k=k, tti=tti: E.transpose(
                        ptr[k % 2][:, 0:128],
                        hb[:, k * 512 + tti * 128:k * 512 + tti * 128 + 128],
                        identt[:]), ["hb", "identt"], [f"ptr{k % 2}"])
                    OP("vector", lambda E, k=k, s=oslot: E.tensor_copy(
                        rh[s][:, k * 128:(k + 1) * 128], ptr[k % 2][:, 0:128]),
                        [f"ptr{k % 2}"], [f"rh{oslot}"])
                STR(S["xn1N"][tt * 128:(tt + 1) * 128, :], rh[oslot][:, 0:D],
                    [f"rh{oslot}"], [f"d_xn1N@{tt}"])

        stage_norm(S["xn1"], "d_xn1", S["xr"], "d_xr", "rms")
        OP("vector", lambda E: E.memset(rwkvS[:, :], 0.0), (), ["rwkvS"])
        OP("scalar", lambda E: E.activation(ewc[:, :], e1t[:, :], ACTF.Exp,
                                            scale=-1.0), ["e1t"], ["ewc"])
        for c in range(NC):
            rslot = c % 2
            load_chunk(rslot, S["xr"], "d_xr", c)
            for mb in range(2):
                load_wblk(0, I["WkT"], "i_WkT", DT, mb * 512, (mb + 1) * 512)
                for mt in range(4):
                    m_g = mb * 4 + mt
                    bank, bname = pm[mt % 2], f"pm{mt % 2}"
                    mm_accum(bank[:, :], bname,
                             lambda k, mt=mt: wsl[0][:, k * 512 + mt * 128:k * 512 + mt * 128 + 128],
                             ["wsl0"],
                             lambda k, rs=rslot: rhv(rs, k), [f"rh{rslot}"], DT)
                    OP("vector", lambda E, m_g=m_g, bank=bank, bname=bname:
                       E.tensor_copy(hb[:, m_g * 512:(m_g + 1) * 512], bank[:, :]),
                       [bname], ["hb"])
                load_wblk(1, I["WvT"], "i_WvT", DT, mb * 512, (mb + 1) * 512)
                for mt in range(4):
                    m_g = mb * 4 + mt
                    bank, bname = pm[2 + mt % 2], f"pm{2 + mt % 2}"
                    mm_accum(bank[:, :], bname,
                             lambda k, mt=mt: wsl[1][:, k * 512 + mt * 128:k * 512 + mt * 128 + 128],
                             ["wsl1"],
                             lambda k, rs=rslot: rhv(rs, k), [f"rh{rslot}"], DT)
                    OP("vector", lambda E, m_g=m_g, bank=bank, bname=bname:
                       E.tensor_mul(hb[:, m_g * 512:(m_g + 1) * 512],
                                    hb[:, m_g * 512:(m_g + 1) * 512], bank[:, :]),
                       ["hb", bname], ["hb"])
            store_chunk(S["kv"], "d_kv", c, hb, "hb")
            for et in range(DT):
                half = (et % 2) * 512 if T >= 1024 else 0
                OP("vector", lambda E, et=et, h=half: E.tensor_tensor_scan(
                    sc[0][:, h:h + 512], ewc[:, et:et + 1].broadcast_to([128, 512]),
                    hb[:, et * 512:(et + 1) * 512], rwkvS[:, et:et + 1],
                    ALU.mult, ALU.add), ["ewc", "hb", "rwkvS"], ["sc0"])
                OP("gpsimd", lambda E, et=et, h=half: E.tensor_copy(
                    rwkvS[:, et:et + 1], sc[0][:, h + 511:h + 512]),
                    ["sc0"], ["rwkvS"])
                STR(S["hh"][et * 128:(et + 1) * 128, c * 512:(c + 1) * 512],
                    sc[0][:, half:half + 512], ["sc0"], [f"d_hh@{c}"])
            for mb in range(2):
                load_wblk(0, I["WrT"], "i_WrT", DT, mb * 512, (mb + 1) * 512)
                for mt in range(4):
                    m_g = mb * 4 + mt
                    bank, bname = pm[mt % 2], f"pm{mt % 2}"
                    mm_accum(bank[:, :], bname,
                             lambda k, mt=mt: wsl[0][:, k * 512 + mt * 128:k * 512 + mt * 128 + 128],
                             ["wsl0"],
                             lambda k, rs=rslot: rhv(rs, k), [f"rh{rslot}"], DT)
                    OP("scalar", lambda E, m_g=m_g, bank=bank, bname=bname:
                       E.activation(ob[:, m_g * 512:(m_g + 1) * 512], bank[:, :],
                                    ACTF.Sigmoid), [bname], ["ob"])
            store_chunk(S["rr"], "d_rr", c, ob, "ob")

        STR(S["c2ai"][0, :].rearrange("(p c) -> p c", p=128), rwkvS[:, :],
            ["rwkvS"], ["d_c2ai"])
        P.op("gpsimd", lambda E: E.collective_compute(
            "AllGather", ALU.bypass, replica_groups=[list(range(8))],
            ins=[S["c2ai"].opt()], outs=[S["c2ao"].opt()]),
            ["d_c2ai"], ["d_c2ao"], force_inc=True)

        LD(sl[0:NSLOT, 0:D], I["slot0"][:, :], ["i_slot0"], ["sl"])
        for k in range(DT):
            OP("tensor", lambda E, k=k: E.transpose(
                ptr[k % 2][:, 0:NSLOT], sl[0:NSLOT, k * 128:(k + 1) * 128],
                identt[0:NSLOT, 0:NSLOT]), ["sl", "identt"], [f"ptr{k % 2}"])
            OP("vector", lambda E, k=k: E.tensor_copy(
                sl[:, D + k * NSLOT:D + (k + 1) * NSLOT], ptr[k % 2][:, 0:NSLOT]),
                [f"ptr{k % 2}"], ["sl"])
        for k in range(DT):
            OP("vector", lambda E, k=k: E.tensor_reduce(
                sm[:, 64 + k:64 + k + 1], sl[:, D + k * NSLOT:D + (k + 1) * NSLOT],
                axis=AX.X, op=ALU.add), ["sl"], ["sm"])
        OP("vector", lambda E: E.tensor_scalar(
            sm[:, 64:64 + DT], sm[:, 64:64 + DT], cn("invS"), None, ALU.mult),
            ["sm", "consts"], ["sm"])
        LD(wsl[0][:, 0:DT * DH].rearrange("p (k m) -> p k m", k=DT),
           I["w1b"].rearrange("(k p) m -> p k m", p=128), ["i_w1b"], ["wsl0"])
        for mt in range(DH // 128):
            mm_accum(pb[0][:, 0:1], "pb0",
                     lambda k, mt=mt: wsl[0][:, k * DH + mt * 128:k * DH + mt * 128 + 128],
                     ["wsl0"],
                     lambda k: sm[:, 64 + k:64 + k + 1], ["sm"], DT)
            OP("vector", lambda E, mt=mt: E.tensor_scalar(
                bh1[:, mt:mt + 1], pb[0][:, 0:1], vc("b1", mt), None, ALU.add),
                ["pb0", "vecs"], ["bh1"])
        LD(wsl[0][:, 0:DT * DH].rearrange("p (k m) -> p k m", k=DT),
           I["w1a"].rearrange("(k p) m -> p k m", p=128), ["i_w1a"], ["wsl0"])
        LD(wsl[1][:, 0:4 * NSLOT].rearrange("p (k m) -> p k m", k=4),
           I["w2s"].rearrange("(k p) m -> p k m", p=128), ["i_w2s"], ["wsl1"])
        for c in range(NC):
            if c == 1:
                OP("vector", lambda E: E.memset(carry2[:, :], 0.0), (), ["carry2"])
                for j in range(8):
                    LD(sm[:, 48:48 + DT], S["c2ao"][j, :].rearrange(
                        "(p c) -> p c", p=128), ["d_c2ao"], ["sm"])
                    OP("vector", lambda E, j=j: E.tensor_scalar(
                        sm[:, 56:56 + DT], ewTt[:, :], m8t[:, j:j + 1], im8t[:, j:j + 1],
                        ALU.mult, ALU.add), ["ewTt", "m8t", "im8t"], ["sm"])
                    OP("vector", lambda E: E.tensor_mul(carry2[:, :], carry2[:, :],
                                                        sm[:, 56:56 + DT]),
                       ["carry2", "sm"], ["carry2"])
                    OP("vector", lambda E, j=j: E.tensor_scalar(
                        sm[:, 48:48 + DT], sm[:, 48:48 + DT], m8t[:, j:j + 1], None,
                        ALU.mult), ["sm", "m8t"], ["sm"])
                    OP("vector", lambda E: E.tensor_add(carry2[:, :], carry2[:, :],
                                                        sm[:, 48:48 + DT]),
                       ["carry2", "sm"], ["carry2"])

            rslot = c % 2
            load_chunk(rslot, S["xn1"], "d_xn1", c)
            for mt in range(4):
                bank, bname = pm[mt % 2], f"pm{mt % 2}"
                mm_accum(bank[:, :], bname,
                         lambda k, mt=mt: wsl[0][:, k * DH + mt * 128:k * DH + mt * 128 + 128],
                         ["wsl0"],
                         lambda k, rs=rslot: rhv(rs, k), [f"rh{rslot}"], DT)
                OP("scalar", lambda E, mt=mt, bank=bank, bname=bname: E.activation(
                    hb[:, mt * 512:(mt + 1) * 512], bank[:, :], ACTF.Gelu,
                    bias=bh1[:, mt:mt + 1]), [bname, "bh1"], ["hb"])
            mm_accum(pm[2][0:NSLOT, :], "pm2",
                     lambda k: wsl[1][:, k * NSLOT:(k + 1) * NSLOT], ["wsl1"],
                     lambda k: hb[:, k * 512:(k + 1) * 512], ["hb"], 4)
            OP("vector", lambda E: E.tensor_scalar(
                lgc[:, :], pm[2][0:NSLOT, :], b2sb[:, 0:1], None, ALU.add),
                ["pm2", "b2sb"], ["lgc"])
            for tti in range(4):
                tt = c * 4 + tti
                OP("tensor", lambda E, tti=tti: E.transpose(
                    ptr[0][:, 0:NSLOT], lgc[:, tti * 128:(tti + 1) * 128],
                    identt[0:NSLOT, 0:NSLOT]), ["lgc", "identt"], ["ptr0"])
                OP("vector", lambda E: E.tensor_copy(sm[:, 128:128 + NSLOT],
                                                     ptr[0][:, 0:NSLOT]),
                   ["ptr0"], ["sm"])
                OP("vector", lambda E: E.tensor_copy(sm[:, 192:192 + NSLOT],
                                                     sm[:, 128:128 + NSLOT]),
                   ["sm"], ["sm"])
                OP("vector", lambda E: E.reduce_max(sm[:, 112:113],
                                                    sm[:, 192:192 + NSLOT],
                                                    axis=AX.X), ["sm"], ["sm"])
                OP("vector", lambda E: E.tensor_copy(sm[:, 113:114], sm[:, 112:113]),
                   ["sm"], ["sm"])
                for it in range(7):
                    OP("vector", lambda E: E.tensor_scalar(
                        mski[:, 0:NSLOT], sm[:, 192:192 + NSLOT], sm[:, 113:114],
                        None, ALU.is_ge), ["sm"], ["mski"])
                    OP("vector", lambda E: E.copy_predicated(
                        sm[:, 192:192 + NSLOT], mski[:, 0:NSLOT],
                        cn("neg30").broadcast_to([128, NSLOT])),
                        ["sm", "mski", "consts"], ["sm"])
                    OP("vector", lambda E: E.reduce_max(
                        sm[:, 113:114], sm[:, 192:192 + NSLOT], axis=AX.X),
                        ["sm"], ["sm"])
                OP("vector", lambda E: E.tensor_scalar(
                    sm[:, 0:NSLOT], sm[:, 128:128 + NSLOT], sm[:, 113:114], None,
                    ALU.is_ge), ["sm"], ["sm"])
                OP("vector", lambda E: E.tensor_mul(sm[:, 114:115], sm[:, 112:113],
                                                    tiv[:, 3:4]), ["sm", "tiv"],
                   ["sm"])
                OP("vector", lambda E: E.tensor_scalar(
                    sm[:, 114:115], sm[:, 114:115], -1.0, None, ALU.mult),
                    ["sm"], ["sm"])
                OP("scalar", lambda E: E.activation(
                    sm[:, 128:128 + NSLOT], sm[:, 128:128 + NSLOT], ACTF.Exp,
                    scale=tiv[:, 3:4], bias=sm[:, 114:115]), ["sm", "tiv"], ["sm"])
                OP("vector", lambda E: E.tensor_mul(
                    sm[:, 128:128 + NSLOT], sm[:, 128:128 + NSLOT], sm[:, 0:NSLOT]),
                    ["sm"], ["sm"])
                OP("vector", lambda E: E.reduce_sum(
                    sm[:, 115:116], sm[:, 128:128 + NSLOT], axis=AX.X), ["sm"], ["sm"])
                OP("vector", lambda E: E.reciprocal(sm[:, 116:117], sm[:, 115:116]),
                   ["sm"], ["sm"])
                OP("vector", lambda E, tt=tt: E.tensor_scalar(
                    alp[:, tt * NSLOT:(tt + 1) * NSLOT], sm[:, 128:128 + NSLOT],
                    sm[:, 116:117], None, ALU.mult), ["sm"], ["alp"])
                LD(ya[:, 0:D], S["xn1N"][tt * 128:(tt + 1) * 128, :],
                   [f"d_xn1N@{tt}"], ["ya"])
                MM(pb[0][0:NSLOT, :], "pb0", alp[:, tt * NSLOT:(tt + 1) * NSLOT],
                   ["alp"], ya[:, 0:512], ["ya"], tt == 0, tt == NT - 1)
                MM(pb[1][0:NSLOT, :], "pb1", alp[:, tt * NSLOT:(tt + 1) * NSLOT],
                   ["alp"], ya[:, 512:1024], ["ya"], tt == 0, tt == NT - 1)
                MM(pm[3][0:NSLOT, 0:1], "pm3", alp[:, tt * NSLOT:(tt + 1) * NSLOT],
                   ["alp"], oneskt[:, 0:1], ["oneskt"], tt == 0, tt == NT - 1)
        OP("vector", lambda E: E.tensor_copy(sistage[:, 0:512], pb[0][0:NSLOT, :]),
           ["pb0"], ["sistage"])
        OP("vector", lambda E: E.tensor_copy(sistage[:, 512:1024], pb[1][0:NSLOT, :]),
           ["pb1"], ["sistage"])
        OP("vector", lambda E: E.tensor_copy(sistage[:, 1024:1025],
                                             pm[3][0:NSLOT, 0:1]), ["pm3"], ["sistage"])

        STR(S["c2bi"][0, :].rearrange("(p c) -> p c", p=NSLOT), sistage[:, :],
            ["sistage"], ["d_c2bi"])
        P.op("gpsimd", lambda E: E.collective_compute(
            "AllGather", ALU.bypass, replica_groups=[list(range(8))],
            ins=[S["c2bi"].opt()], outs=[S["c2bo"].opt()]),
            ["d_c2bi"], ["d_c2bo"], force_inc=True)

        OP("vector", lambda E: E.memset(hprev[:, :], 0.0), (), ["hprev"])

        def wkv_chunk(c):
            cs = slice(c * 512, (c + 1) * 512)
            oslot = c % 2
            rwb = hb if c % 2 == 0 else sl
            rwn = "hb" if c % 2 == 0 else "sl"
            LD(rh[oslot][:, :DT * 512].rearrange("p (k t) -> p k t", k=DT),
               chunk_of(S["hh"], c), cname("d_hh", c), [f"rh{oslot}"])
            LD(rwb[:, 0:DT * 512].rearrange("p (k t) -> p k t", k=DT),
               chunk_of(S["kv"], c), cname("d_kv", c), [rwn])
            LD(ya[:, 0:DT * 512].rearrange("p (k t) -> p k t", k=DT),
               chunk_of(S["rr"], c), cname("d_rr", c), ["ya"])
            units = []
            for et in range(DT):
                def u(et=et, cs=cs, oslot=oslot, rwb=rwb, rwn=rwn):
                    kvt = rwb[:, et * 512:(et + 1) * 512]
                    hht = rh[oslot][:, et * 512:(et + 1) * 512]
                    rrt = ya[:, et * 512:(et + 1) * 512]
                    pslot = (et % 2) * 512 if T >= 1024 else 0
                    pt = sc[0][:, pslot:pslot + 512]
                    et_ = sc[3][:, pslot:pslot + 512]
                    OP("scalar", lambda E, pt=pt: E.activation(
                        pt, rampt[:, cs], ACTF.Exp,
                        scale=ne1t[:, et:et + 1]), ["rampt", "ne1t"], ["sc0"])
                    OP("vector", lambda E, pt=pt: E.tensor_scalar(
                        pt, pt, carry2[:, et:et + 1], None, ALU.mult),
                        ["sc0", "carry2"], ["sc0"])
                    OP("vector", lambda E, pslot=pslot: E.tensor_add(
                        sc[0][:, pslot + 1:pslot + 512],
                        sc[0][:, pslot + 1:pslot + 512],
                        hht[:, 0:511]), ["sc0", f"rh{oslot}"], ["sc0"])
                    OP("vector", lambda E, pslot=pslot: E.tensor_add(
                        sc[0][:, pslot:pslot + 1], sc[0][:, pslot:pslot + 1],
                        hprev[:, et:et + 1]), ["sc0", "hprev"], ["sc0"])
                    OP("gpsimd", lambda E: E.tensor_copy(
                        hprev[:, et:et + 1], hht[:, 511:512]),
                        [f"rh{oslot}"], ["hprev"])
                    OP("gpsimd", lambda E, et_=et_: E.tensor_tensor(
                        et_, kvt,
                        eut[:, et:et + 1].broadcast_to([128, 512]), ALU.mult),
                        [rwn, "eut"], ["sc3"])
                    OP("vector", lambda E, pt=pt, et_=et_: E.tensor_add(
                        pt, pt, et_), ["sc0", "sc3"], ["sc0"])
                    OP("gpsimd", lambda E, pt=pt: E.tensor_mul(
                        kvt, rrt, pt),
                        ["ya", "sc0", rwn], [rwn])
                units.append(u)
            return units

        def mixer_chunk(c):
            cs = slice(c * 512, (c + 1) * 512)
            rslot = c % 2
            rwb = hb if c % 2 == 0 else sl
            rwn = "hb" if c % 2 == 0 else "sl"
            units = []
            def u_load(rslot=rslot, c=c):
                LD(rh[rslot][:, :DT * 512].rearrange("p (k t) -> p k t", k=DT),
                   chunk_of(S["xn1"], c), cname("d_xn1", c), [f"rh{rslot}"])
            units.append(u_load)
            for mb in range(2):
                def u_w(mb=mb):
                    load_wblk(mb % 2, I["WoT1"], "i_WoT1", DT, mb * 512,
                              (mb + 1) * 512)
                units.append(u_w)
                for mt in range(4):
                    def u_mm(mb=mb, mt=mt, rslot=rslot, rwb=rwb, rwn=rwn):
                        m_g = mb * 4 + mt
                        bank, bname = pm[mt % 2], f"pm{mt % 2}"
                        mm_accum(bank[:, :], bname,
                                 lambda k, ws=mb % 2, mt=mt:
                                     wsl[ws][:, k * 512 + mt * 128:k * 512 + mt * 128 + 128],
                                 [f"wsl{mb % 2}"],
                                 lambda k: rwb[:, k * 512:(k + 1) * 512], [rwn], DT)
                        OP("vector", lambda E, m_g=m_g, bank=bank, bname=bname,
                           rs=rslot: E.scalar_tensor_tensor(
                               ob[:, m_g * 512:(m_g + 1) * 512], bank[:, :],
                               vc("ms", m_g), rhv(rs, m_g), ALU.mult, ALU.add),
                           [bname, "vecs", f"rh{rslot}"], ["ob"])
                    units.append(u_mm)
            def u_tail(c=c, cs=cs, rwb=rwb, rwn=rwn):
                store_chunk(S["hh"], "d_hh", c, ob, "ob")
                for k in range(DT):
                    half = (k % 2) * 512 if T >= 1024 else 0
                    OP("vector", lambda E, k=k, h=half: E.tensor_mul(
                        sc[1][:, h:h + 512], ob[:, k * 512:(k + 1) * 512],
                        ob[:, k * 512:(k + 1) * 512]), ["ob"], ["sc1"])
                    OP("tensor", lambda E, k=k, h=half: E.matmul(
                        pb[1][:, :], oneskt[:], sc[1][:, h:h + 512],
                        start=(k == 0), stop=(k == DT - 1)), ["sc1", "oneskt"],
                        ["pb1"])
                OP("vector", lambda E, cs=cs: E.tensor_scalar(
                    sc[2][:, cs], pb[1][:, :], cn("invD"), None, ALU.mult),
                    ["pb1", "consts"], ["sc2"])
                OP("scalar", lambda E, cs=cs: E.activation(
                    sc[2][:, cs], sc[2][:, cs], ACTF.Ln, bias=cn("eps6")),
                    ["sc2", "consts"], ["sc2"])
                OP("scalar", lambda E, cs=cs: E.activation(
                    sc[2][:, cs], sc[2][:, cs], ACTF.Exp, scale=-0.5),
                    ["sc2"], ["sc2"])
                for k in range(DT):
                    OP("vector", lambda E, k=k, cs=cs, rwb=rwb, rwn=rwn:
                       E.tensor_mul(rwb[:, k * 512:(k + 1) * 512],
                                    ob[:, k * 512:(k + 1) * 512], sc[2][:, cs]),
                       ["ob", "sc2"], [rwn])
                store_chunk(S["xr"], "d_xr", c, rwb, rwn)
            units.append(u_tail)
            return units

        prev_mix = None
        for c in range(NC):
            w_units = wkv_chunk(c)
            m_units = list(prev_mix) if prev_mix else []
            n = max(len(w_units), len(m_units))
            for i in range(n):
                if i < len(w_units):
                    w_units[i]()
                if i < len(m_units):
                    m_units[i]()
            prev_mix = mixer_chunk(c)
        for u in prev_mix:
            u()

        for c in range(NC):
            load_chunk(0, S["xr"], "d_xr", c)
            load_chunk(1, S["hh"], "d_hh", c)
            for q in range(4):
                for fb in range(2):
                    ws = fb
                    f0 = q * 1024 + fb * 512
                    load_wblk(ws, I["WfcT"], "i_WfcT", DT, f0, f0 + 512)
                    for mt in range(4):
                        hm = fb * 4 + mt
                        bank, bname = pm[mt % 2], f"pm{mt % 2}"
                        mm_accum(bank[:, :], bname,
                                 lambda k, ws=ws, mt=mt:
                                     wsl[ws][:, k * 512 + mt * 128:k * 512 + mt * 128 + 128],
                                 [f"wsl{ws}"],
                                 lambda k: rhv(0, k), ["rh0"], DT)
                        OP("scalar", lambda E, hm=hm, bank=bank, bname=bname:
                           E.activation(hb[:, hm * 512:(hm + 1) * 512], bank[:, :],
                                        ACTF.Relu), [bname], ["hb"])
                        OP("scalar", lambda E, hm=hm: E.activation(
                            hb[:, hm * 512:(hm + 1) * 512],
                            hb[:, hm * 512:(hm + 1) * 512], ACTF.Square),
                            ["hb"], ["hb"])
                        hs = (hm % 2) * 512 if T >= 1024 else 0
                        OP("scalar", lambda E, hm=hm, bank=bank, bname=bname,
                           hs=hs: E.activation(sc[0][:, hs:hs + 512], bank[:, :],
                                               ACTF.Relu, scale=-1.0),
                           [bname], ["sc0"])
                        OP("scalar", lambda E, hs=hs: E.activation(
                            sc[0][:, hs:hs + 512], sc[0][:, hs:hs + 512],
                            ACTF.Square), ["sc0"], ["sc0"])
                        OP("vector", lambda E, hm=hm, hs=hs: E.scalar_tensor_tensor(
                            hb[:, hm * 512:(hm + 1) * 512], sc[0][:, hs:hs + 512],
                            0.25, hb[:, hm * 512:(hm + 1) * 512], ALU.mult,
                            ALU.add), ["sc0", "hb"], ["hb"])
                for mb in range(2):
                    ws = mb
                    LD(wsl[ws][:, 0:8 * 512].rearrange("p (k m) -> p k m", k=8),
                       I["WmlpT"].rearrange("(k p) m -> p k m", p=128)
                       [:, q * 8:q * 8 + 8, mb * 512:(mb + 1) * 512],
                       ["i_WmlpT"], [f"wsl{ws}"])
                    for mt in range(4):
                        m_g = mb * 4 + mt
                        bank, bname = pm[2 + (m_g % 2)], f"pm{2 + (m_g % 2)}"
                        mm_accum(bank[:, :], bname,
                                 lambda k, ws=ws, mt=mt:
                                     wsl[ws][:, k * 512 + mt * 128:k * 512 + mt * 128 + 128],
                                 [f"wsl{ws}"],
                                 lambda k: hb[:, k * 512:(k + 1) * 512], ["hb"], 8)
                        if q == 0:
                            OP("vector", lambda E, m_g=m_g, bank=bank, bname=bname:
                               E.tensor_copy(ob[:, m_g * 512:(m_g + 1) * 512],
                                             bank[:, :]), [bname], ["ob"])
                        else:
                            OP("vector", lambda E, m_g=m_g, bank=bank, bname=bname:
                               E.tensor_add(ob[:, m_g * 512:(m_g + 1) * 512],
                                            ob[:, m_g * 512:(m_g + 1) * 512],
                                            bank[:, :]), ["ob", bname], ["ob"])
            for mt in range(DT):
                OP("vector", lambda E, mt=mt: E.scalar_tensor_tensor(
                    ob[:, mt * 512:(mt + 1) * 512], ob[:, mt * 512:(mt + 1) * 512],
                    vc("mls", mt), rhv(1, mt), ALU.mult, ALU.add),
                    ["ob", "vecs", "rh1"], ["ob"])
            store_chunk(S["s1"], "d_s1", c, ob, "ob")

        OP("vector", lambda E: E.memset(sistage[:, :], 0.0), ("d_c2bi",), ["sistage"])
        for j in range(8):
            LD(sigstg[:, :], S["c2bo"][j, :]
               .rearrange("(p c) -> p c", p=NSLOT), ["d_c2bo"], ["sigstg"])
            OP("vector", lambda E, j=j: E.scalar_tensor_tensor(
                sistage[:, :], sigstg[:, :], mb8ct[:, j:j + 1],
                sistage[:, :], ALU.mult, ALU.add),
                ["sigstg", "mb8ct", "sistage"], ["sistage"])

        LD(sl[0:NSLOT, 0:D], I["slot0"][:, :], ["i_slot0"], ["sl"])
        for k in range(DT):
            OP("tensor", lambda E, k=k: E.transpose(
                ptr[k % 2][:, 0:NSLOT], sl[0:NSLOT, k * 128:(k + 1) * 128],
                identt[0:NSLOT, 0:NSLOT]), ["sl", "identt"], [f"ptr{k % 2}"])
            OP("vector", lambda E, k=k: E.tensor_copy(
                sl[:, D + k * NSLOT:D + (k + 1) * NSLOT], ptr[k % 2][:, 0:NSLOT]),
                [f"ptr{k % 2}"], ["sl"])
        OP("vector", lambda E: E.tensor_scalar(
            ss64[:, 0:1], sistage[:, 1024:1025], cn("eps8", NSLOT), None, ALU.add),
            ["sistage", "consts"], ["ss64"])
        OP("vector", lambda E: E.reciprocal(ss64[:, 1:2], ss64[:, 0:1]),
           ["ss64"], ["ss64"])
        OP("vector", lambda E: E.tensor_scalar(
            sl[0:NSLOT, 0:D], sistage[:, 0:D], ss64[:, 1:2], None, ALU.mult),
            ["sistage", "ss64"], ["sl"])
        for k in range(DT):
            OP("tensor", lambda E, k=k: E.transpose(
                ptr[k % 2][:, 0:NSLOT], sl[0:NSLOT, k * 128:(k + 1) * 128],
                identt[0:NSLOT, 0:NSLOT]), ["sl", "identt"], [f"ptr{k % 2}"])
            OP("vector", lambda E, k=k: E.tensor_copy(
                sl[:, 1536 + k * NSLOT:1536 + (k + 1) * NSLOT], ptr[k % 2][:, 0:NSLOT]),
                [f"ptr{k % 2}"], ["sl"])
        for which, Wn, rbase, bias_nm, hb0 in [
            (0, "WihT", 1536, "bih", 0), (1, "WhhT", 1024, "bhh", 1536)]:
            for mb in range(6):
                ws = mb % 2
                load_wblk(ws, I[Wn], "i_" + Wn, DT, mb * 512, (mb + 1) * 512)
                for mt in range(4):
                    m_g = mb * 4 + mt
                    bank, bname = pm[mt % 2], f"pm{mt % 2}"
                    mm_accum(bank[:, 0:NSLOT], bname,
                             lambda k, ws=ws, mt=mt:
                                 wsl[ws][:, k * 512 + mt * 128:k * 512 + mt * 128 + 128],
                             [f"wsl{ws}"],
                             lambda k, rb=rbase: sl[:, rb + k * NSLOT:rb + (k + 1) * NSLOT],
                             ["sl"], DT)
                    OP("vector", lambda E, m_g=m_g, bank=bank, bname=bname,
                       bias_nm=bias_nm, hb0=hb0: E.tensor_scalar(
                           hb[:, hb0 + m_g * NSLOT:hb0 + (m_g + 1) * NSLOT],
                           bank[:, 0:NSLOT], vc(bias_nm, m_g), None, ALU.add),
                       [bname, "vecs"], ["hb"])
        OP("vector", lambda E: E.tensor_add(hb[:, 0:512], hb[:, 0:512],
                                            hb[:, 1536:2048]), ["hb"], ["hb"])
        OP("scalar", lambda E: E.activation(hb[:, 0:512], hb[:, 0:512], ACTF.Sigmoid),
           ["hb"], ["hb"])
        OP("vector", lambda E: E.tensor_add(hb[:, 512:1024], hb[:, 512:1024],
                                            hb[:, 2048:2560]), ["hb"], ["hb"])
        OP("scalar", lambda E: E.activation(hb[:, 512:1024], hb[:, 512:1024],
                                            ACTF.Sigmoid), ["hb"], ["hb"])
        OP("vector", lambda E: E.tensor_mul(hb[:, 2560:3072], hb[:, 0:512],
                                            hb[:, 2560:3072]), ["hb"], ["hb"])
        OP("vector", lambda E: E.tensor_add(hb[:, 1024:1536], hb[:, 1024:1536],
                                            hb[:, 2560:3072]), ["hb"], ["hb"])
        OP("scalar", lambda E: E.activation(hb[:, 1024:1536], hb[:, 1024:1536],
                                            ACTF.Tanh), ["hb"], ["hb"])
        OP("vector", lambda E: E.tensor_sub(hb[:, 1536:2048], sl[:, 1024:1536],
                                            hb[:, 1024:1536]), ["sl", "hb"], ["hb"])
        OP("vector", lambda E: E.tensor_mul(hb[:, 1536:2048], hb[:, 512:1024],
                                            hb[:, 1536:2048]), ["hb"], ["hb"])
        OP("vector", lambda E: E.tensor_add(sl[:, 2048:2560], hb[:, 1024:1536],
                                            hb[:, 1536:2048]), ["hb"], ["sl"])
        for Wn, M, bias_nm, dst0 in [("wq_s", DQ, "bq", 2560),
                                     ("wk_s", DQ, "bk", 2688),
                                     ("wv_s", D, "bv", 2816)]:
            for mb in range((M + 511) // 512):
                mcols = min(512, M - mb * 512)
                ws = mb % 2
                load_wblk(ws, I[Wn], "i_" + Wn, DT, mb * 512, mb * 512 + mcols)
                for mt in range(mcols // 128):
                    m_g = mb * 4 + mt
                    bank, bname = pm[mt % 2], f"pm{mt % 2}"
                    mm_accum(bank[:, 0:NSLOT], bname,
                             lambda k, ws=ws, mt=mt, mc=mcols:
                                 wsl[ws][:, k * mc + mt * 128:k * mc + mt * 128 + 128],
                             [f"wsl{ws}"],
                             lambda k: sl[:, 2048 + k * NSLOT:2048 + (k + 1) * NSLOT],
                             ["sl"], DT)
                    OP("vector", lambda E, m_g=m_g, bank=bank, bname=bname,
                       bias_nm=bias_nm, dst0=dst0: E.tensor_scalar(
                           sl[:, dst0 + m_g * NSLOT:dst0 + (m_g + 1) * NSLOT],
                           bank[:, 0:NSLOT], vc(bias_nm, m_g), None, ALU.add),
                       [bname, "vecs"], ["sl"])
        for k in range(2):
            MM(ptr[0][0:NSLOT, 0:NSLOT], "ptr0",
               sl[:, 2560 + k * NSLOT:2560 + (k + 1) * NSLOT], ["sl"],
               sl[:, 2688 + k * NSLOT:2688 + (k + 1) * NSLOT], ["sl"],
               k == 0, k == 1)
        OP("vector", lambda E: E.tensor_copy(sl[0:NSLOT, 3328:3392],
                                             ptr[0][0:NSLOT, 0:NSLOT]),
           ["ptr0"], ["sl"])
        OP("vector", lambda E: E.reduce_max(ss64[:, 2:3], sl[0:NSLOT, 3328:3392],
                                            axis=AX.X), ["sl"], ["ss64"])
        OP("vector", lambda E: E.tensor_mul(ss64[:, 3:4], ss64[:, 2:3],
                                            cn("nscq", NSLOT)),
           ["ss64", "consts"], ["ss64"])
        OP("scalar", lambda E: E.activation(
            sl[0:NSLOT, 3328:3392], sl[0:NSLOT, 3328:3392], ACTF.Exp,
            scale=cn("scq", NSLOT), bias=ss64[:, 3:4]),
            ["sl", "consts", "ss64"], ["sl"])
        OP("vector", lambda E: E.reduce_sum(ss64[:, 4:5], sl[0:NSLOT, 3328:3392],
                                            axis=AX.X), ["sl"], ["ss64"])
        OP("vector", lambda E: E.reciprocal(ss64[:, 5:6], ss64[:, 4:5]),
           ["ss64"], ["ss64"])
        OP("vector", lambda E: E.tensor_scalar(
            sl[0:NSLOT, 3328:3392], sl[0:NSLOT, 3328:3392], ss64[:, 5:6], None,
            ALU.mult), ["sl", "ss64"], ["sl"])
        OP("tensor", lambda E: E.transpose(
            ptr[1][0:NSLOT, 0:NSLOT], sl[0:NSLOT, 3328:3392],
            identt[0:NSLOT, 0:NSLOT]), ["sl", "identt"], ["ptr1"])
        OP("vector", lambda E: E.tensor_copy(sl[0:NSLOT, 3392:3456],
                                             ptr[1][0:NSLOT, 0:NSLOT]),
           ["ptr1"], ["sl"])
        for k in range(DT):
            OP("tensor", lambda E, k=k: E.transpose(
                ptr[k % 2][0:NSLOT, 0:128], sl[:, 2816 + k * NSLOT:2816 + (k + 1) * NSLOT],
                identt[:]), ["sl", "identt"], [f"ptr{k % 2}"])
            OP("vector", lambda E, k=k: E.tensor_copy(
                hb[0:NSLOT, k * 128:(k + 1) * 128], ptr[k % 2][0:NSLOT, 0:128]),
                [f"ptr{k % 2}"], ["hb"])
        for m in range(DT):
            MM(pm[0][:, 0:NSLOT], "pm0", hb[0:NSLOT, m * 128:(m + 1) * 128], ["hb"],
               sl[0:NSLOT, 3392:3456], ["sl"], True, True)
            OP("vector", lambda E, m=m: E.tensor_add(
                hb[:, 1024 + m * NSLOT:1024 + (m + 1) * NSLOT],
                sl[:, 2048 + m * NSLOT:2048 + (m + 1) * NSLOT], pm[0][:, 0:NSLOT]),
                ["sl", "pm0"], ["hb"])
        for k in range(DT):
            OP("tensor", lambda E, k=k: E.matmul(
                pb[0][:, 0:NSLOT], oneskt[:],
                hb[:, 1024 + k * NSLOT:1024 + (k + 1) * NSLOT],
                start=(k == 0), stop=(k == DT - 1)), ["hb", "oneskt"], ["pb0"])
        for k in range(DT):
            OP("vector", lambda E, k=k: E.tensor_mul(
                hb[:, 3584 + k % 2 * NSLOT:3584 + (k % 2 + 1) * NSLOT],
                hb[:, 1024 + k * NSLOT:1024 + (k + 1) * NSLOT],
                hb[:, 1024 + k * NSLOT:1024 + (k + 1) * NSLOT]), ["hb"], ["hb"])
            OP("tensor", lambda E, k=k: E.matmul(
                pb[1][:, 0:NSLOT], oneskt[:],
                hb[:, 3584 + k % 2 * NSLOT:3584 + (k % 2 + 1) * NSLOT],
                start=(k == 0), stop=(k == DT - 1)), ["hb", "oneskt"], ["pb1"])
        OP("vector", lambda E: E.tensor_scalar(
            hb[:, 3712:3776], pb[0][:, 0:NSLOT], cn("invD"), None, ALU.mult),
            ["pb0", "consts"], ["hb"])
        OP("vector", lambda E: E.tensor_scalar(
            hb[:, 3776:3840], pb[1][:, 0:NSLOT], cn("invD"), None, ALU.mult),
            ["pb1", "consts"], ["hb"])
        OP("vector", lambda E: E.tensor_mul(hb[:, 3840:3904], hb[:, 3712:3776],
                                            hb[:, 3712:3776]), ["hb"], ["hb"])
        OP("vector", lambda E: E.tensor_sub(hb[:, 3776:3840], hb[:, 3776:3840],
                                            hb[:, 3840:3904]),
           ["hb", "hb"], ["hb"])
        OP("scalar", lambda E: E.activation(hb[:, 3776:3840], hb[:, 3776:3840],
                                            ACTF.Ln, bias=cn("eps5")),
           ["hb", "consts"], ["hb"])
        OP("scalar", lambda E: E.activation(hb[:, 3776:3840], hb[:, 3776:3840],
                                            ACTF.Exp, scale=-0.5), ["hb"], ["hb"])
        for k in range(DT):
            OP("vector", lambda E, k=k: E.tensor_sub(
                hb[:, 1024 + k * NSLOT:1024 + (k + 1) * NSLOT],
                hb[:, 1024 + k * NSLOT:1024 + (k + 1) * NSLOT], hb[:, 3712:3776]),
                ["hb", "hb"], ["hb"])
            OP("vector", lambda E, k=k: E.tensor_mul(
                hb[:, 1024 + k * NSLOT:1024 + (k + 1) * NSLOT],
                hb[:, 1024 + k * NSLOT:1024 + (k + 1) * NSLOT], hb[:, 3776:3840]),
                ["hb", "hb"], ["hb"])
            OP("vector", lambda E, k=k: E.tensor_scalar(
                hb[:, 1024 + k * NSLOT:1024 + (k + 1) * NSLOT],
                hb[:, 1024 + k * NSLOT:1024 + (k + 1) * NSLOT],
                vc("lng", k), vc("lnb", k), ALU.mult, ALU.add), ["hb", "vecs"], ["hb"])
        for mb in range(2):
            ws = mb
            load_wblk(ws, I["wvp_s"], "i_wvp_s", DT, mb * 512, (mb + 1) * 512)
            for mt in range(4):
                m_g = mb * 4 + mt
                bank, bname = pm[mt % 2], f"pm{mt % 2}"
                mm_accum(bank[:, 0:NSLOT], bname,
                         lambda k, ws=ws, mt=mt:
                             wsl[ws][:, k * 512 + mt * 128:k * 512 + mt * 128 + 128],
                         [f"wsl{ws}"],
                         lambda k: hb[:, 1024 + k * NSLOT:1024 + (k + 1) * NSLOT],
                         ["hb"], DT)
                OP("vector", lambda E, m_g=m_g, bank=bank, bname=bname:
                   E.tensor_scalar(
                       sl[:, 2560 + m_g * NSLOT:2560 + (m_g + 1) * NSLOT],
                       bank[:, 0:NSLOT], vc("bvp", m_g), None, ALU.add),
                   [bname, "vecs"], ["sl"])
        for k in range(DT):
            OP("tensor", lambda E, k=k: E.transpose(
                ptr[k % 2][0:NSLOT, 0:128], sl[:, 2560 + k * NSLOT:2560 + (k + 1) * NSLOT],
                identt[:]), ["sl", "identt"], [f"ptr{k % 2}"])
            OP("vector", lambda E, k=k: E.tensor_copy(
                hb[0:NSLOT, 1536 + k * 128:1536 + (k + 1) * 128],
                ptr[k % 2][0:NSLOT, 0:128]), [f"ptr{k % 2}"], ["hb"])
        for tt in range(NT):
            OP("tensor", lambda E, tt=tt: E.transpose(
                ptr[0][0:NSLOT, 0:128], alp[:, tt * NSLOT:(tt + 1) * NSLOT],
                identt[:]), ["alp", "identt"], ["ptr0"])
            OP("vector", lambda E: E.tensor_copy(sl[0:NSLOT, 3456:3584],
                                                 ptr[0][0:NSLOT, 0:128]),
               ["ptr0"], ["sl"])
            for m in range(DT):
                MM(pm[1][:, 0:128], "pm1", hb[0:NSLOT, 1536 + m * 128:1536 + (m + 1) * 128],
                   ["hb"], sl[0:NSLOT, 3456:3584], ["sl"], True, True)
                OP("vector", lambda E, m=m: E.tensor_copy(
                    ob[:, m * 128:(m + 1) * 128], pm[1][:, 0:128]), ["pm1"], ["ob"])
            store_chunk(S["sv"], "d_sv", tt, ob, "ob", width=128)
        def evac_g1(mt, c, bank, bname):
            OP("scalar", lambda E, mt=mt, bank=bank, bname=bname: E.activation(
                ob[:, mt * 512:(mt + 1) * 512], bank[:, :], ACTF.Silu,
                bias=vc("bo1", mt)), [bname, "vecs"], ["ob"])
            if mt == DT - 1:
                store_chunk(S["g1"], "d_g1", c, ob, "ob")
        big_mm(I["wo_s"], "i_wo_s", D, D, S["sv"], "d_sv", evac_g1)

        for c in range(NC):
            load_chunk(0, S["s1"], "d_s1", c)
            load_chunk(1, S["g1"], "d_g1", c)
            LD(hb[:, 0:DT * 512].rearrange("p (k t) -> p k t", k=DT),
               chunk_of(S["x1"], c), cname("d_x1", c), ["hb"])
            for k in range(DT):
                OP("vector", lambda E, k=k: E.tensor_mul(
                    ob[:, k * 512:(k + 1) * 512], rhv(0, k), rhv(1, k)),
                    ["rh0", "rh1"], ["ob"])
                OP("vector", lambda E, k=k: E.tensor_add(
                    ob[:, k * 512:(k + 1) * 512], ob[:, k * 512:(k + 1) * 512],
                    hb[:, k * 512:(k + 1) * 512]), ["ob", "hb"], ["ob"])
            store_chunk(outT, "o_outT", c, ob, "ob")
        OP("vector", lambda E: E.engine_nop(), ["o_outT"], ())

        P.resolve()
        run = P.make_runner(sems)

        with nc.Block() as block:
            @block.tensor
            def _(eng):
                run("tensor", eng)

            @block.vector
            def _(eng):
                run("vector", eng)

            @block.scalar
            def _(eng):
                run("scalar", eng)

            @block.sync
            def _(eng):
                run("sync", eng)

            @block.gpsimd
            def _(eng):
                run("gpsimd", eng)

    return nc


def _vec_tiles(v, ntiles):
    return np.ascontiguousarray(v.reshape(ntiles, 128).T.astype(np.float32))


def _mat_tiles(v, ntiles, n2):
    return np.ascontiguousarray(
        v.reshape(ntiles, 128, n2).transpose(1, 0, 2).reshape(128, ntiles * n2)
        .astype(np.float32))


_CACHED = {}


def kernel(**inputs):
    B, Tfull, _ = inputs["L0_attn_x"].shape
    ncores = 8
    per_batch = ncores // B
    T = Tfull // per_batch

    key = T
    if key not in _CACHED:
        _CACHED[key] = build_program(T)
    nc = _CACHED[key]

    f32 = np.float32
    g = {k: np.asarray(v) for k, v in inputs.items()}

    common = {}
    common["WdT"] = np.ascontiguousarray(g["L0_attn_Wd"].T.astype(f32))
    common["WoT0"] = np.ascontiguousarray(g["L0_attn_Wo"].T.astype(f32))
    common["rwT"] = np.ascontiguousarray(g["L0_ffn_router_w"].T.astype(f32))
    common["moe_w1"] = np.ascontiguousarray(g["L0_ffn_w1"].astype(f32))
    common["moe_w2"] = np.ascontiguousarray(g["L0_ffn_w2"].astype(f32))
    common["WkT"] = np.ascontiguousarray(g["L1_attn_Wk"].T.astype(f32))
    common["WvT"] = np.ascontiguousarray(g["L1_attn_Wv"].T.astype(f32))
    common["WrT"] = np.ascontiguousarray(g["L1_attn_Wr"].T.astype(f32))
    common["WoT1"] = np.ascontiguousarray(g["L1_attn_Wo"].T.astype(f32))
    common["WfcT"] = np.ascontiguousarray(g["L1_attn_Wfc"].T.astype(f32))
    common["WmlpT"] = np.ascontiguousarray(g["L1_attn_Wmlp"].T.astype(f32))
    common["slot0"] = np.ascontiguousarray(g["L1_ffn_slot_init"].astype(f32))
    common["w1a"] = np.ascontiguousarray(g["L1_ffn_w1"][:D].astype(f32))
    common["w1b"] = np.ascontiguousarray(g["L1_ffn_w1"][D:].astype(f32))
    common["w2s"] = np.ascontiguousarray(g["L1_ffn_w2"].astype(f32))
    common["WihT"] = np.ascontiguousarray(g["L1_ffn_Wih"].T.astype(f32))
    common["WhhT"] = np.ascontiguousarray(g["L1_ffn_Whh"].T.astype(f32))
    common["wq_s"] = np.ascontiguousarray(g["L1_ffn_wq"].astype(f32))
    common["wk_s"] = np.ascontiguousarray(g["L1_ffn_wk"].astype(f32))
    common["wv_s"] = np.ascontiguousarray(g["L1_ffn_wv"].astype(f32))
    common["wvp_s"] = np.ascontiguousarray(g["L1_ffn_wvp"].astype(f32))
    common["wo_s"] = np.ascontiguousarray(g["L1_ffn_wo"].astype(f32))

    common["bd_t"] = _vec_tiles(g["L0_attn_bd"], DT)
    common["bo_t"] = _vec_tiles(g["L0_attn_bo"], DT)
    common["a_log_t"] = _mat_tiles(g["L0_attn_a_log"], DT, NST)
    common["bprm_t"] = _mat_tiles(g["L0_attn_b_param"], DT, NST)
    common["td_t"] = _vec_tiles(g["L1_attn_time_decay"], DT)
    common["tf_t"] = _vec_tiles(g["L1_attn_time_first"], DT)
    common["ms_t"] = _vec_tiles(g["L1_attn_mixer_scale"], DT)
    common["mls_t"] = _vec_tiles(g["L1_attn_mlp_scale"], DT)
    common["b1_t"] = _vec_tiles(g["L1_ffn_b1"], DH // 128)
    common["b2_c"] = np.ascontiguousarray(
        g["L1_ffn_b2"].reshape(NSLOT, 1).astype(f32))
    common["tau_c"] = np.ascontiguousarray(
        g["L1_ffn_tau"].reshape(1, 1).astype(f32))
    common["bih_t"] = _vec_tiles(g["L1_ffn_bih"], GRU // 128)
    common["bhh_t"] = _vec_tiles(g["L1_ffn_bhh"], GRU // 128)
    common["bq_t"] = _vec_tiles(g["L1_ffn_bq"], DQ // 128)
    common["bk_t"] = _vec_tiles(g["L1_ffn_bk"], DQ // 128)
    common["bv_t"] = _vec_tiles(g["L1_ffn_bv"], DT)
    common["lng_t"] = _vec_tiles(g["L1_ffn_ln_g"], DT)
    common["lnb_t"] = _vec_tiles(g["L1_ffn_ln_b"], DT)
    common["bvp_t"] = _vec_tiles(g["L1_ffn_bvp"], DT)
    common["bo1_t"] = _vec_tiles(g["L1_ffn_bo"], DT)

    common["ident"] = np.eye(128, dtype=f32)
    common["onesk"] = np.ones((128, 128), dtype=f32)
    sel8 = np.zeros((NEXP, 128 * NEXP), dtype=f32)
    for e in range(NEXP):
        sel8[e, e * 128:(e + 1) * 128] = 1.0
    common["sel8"] = sel8
    common["ramp0"] = np.broadcast_to(
        np.arange(T, dtype=f32)[None, :], (128, T)).copy()
    common["ramp256"] = np.broadcast_to(
        np.arange(256, dtype=f32)[None, :], (128, 256)).copy()

    x = g["L0_attn_x"].astype(f32)
    in_maps = []
    for cid in range(ncores):
        b = cid // per_batch
        r = cid % per_batch
        m = dict(common)
        xs = x[b, r * T:(r + 1) * T, :]
        m["x0T"] = np.ascontiguousarray(xs.T)
        m8 = np.zeros((128, 8), dtype=f32)
        mb8 = np.zeros((NSLOT, 8), dtype=f32)
        for j in range(8):
            if j // per_batch == b:
                mb8[:, j] = 1.0
                if j < cid:
                    m8[:, j] = 1.0
        m["m8"] = m8
        m["im8"] = 1.0 - m8
        m["mb8c"] = mb8
        in_maps.append(m)

    res = run_bass_kernel_spmd(nc, in_maps, core_ids=list(range(ncores)))
    out = np.empty((B, Tfull, D), dtype=f32)
    for cid in range(ncores):
        b = cid // per_batch
        r = cid % per_batch
        out[b, r * T:(r + 1) * T, :] = res.results[cid]["outT"].T
    return out


# version 18
# speedup vs baseline: 1.0236x; 1.0108x over previous
import contextlib
import numpy as np

import concourse.bass as bass
import concourse.mybir as mybir
from concourse.bass_utils import run_bass_kernel_spmd

F32 = mybir.dt.float32
ALU = mybir.AluOpType
ACTF = mybir.ActivationFunctionType
AX = mybir.AxisListType

D = 1024
NST = 16
NEXP = 8
MOE_H = 2048
NSLOT = 64
DQ = 256
DH = 512
FCH = 4096
GRU = 3072
DT = D // 128

ENGINES = ("tensor", "vector", "scalar", "sync", "gpsimd")


class _Op:
    __slots__ = ("eng", "fn", "reads", "writes", "is_dma", "seq", "waits",
                 "need_drain", "inc", "force_inc", "lane")

    def __init__(self, eng, fn, reads, writes, is_dma, force_inc=False):
        self.eng, self.fn = eng, fn
        self.reads, self.writes = tuple(reads), tuple(writes)
        self.is_dma = is_dma
        self.seq = -1
        self.waits = []
        self.need_drain = False
        self.inc = False
        self.force_inc = force_inc
        self.lane = -1


class Plan:
    def __init__(self):
        self.per_eng = {e: [] for e in ENGINES}
        self.ops = []
        self.last_write = {}
        self.readers = {}

    def op(self, eng, fn, reads=(), writes=(), is_dma=False, force_inc=False):
        o = _Op(eng, fn, reads, writes, is_dma, force_inc)
        o.seq = len(self.per_eng[eng])
        self.per_eng[eng].append(o)
        self.ops.append(o)
        deps = []
        for r in o.reads:
            w = self.last_write.get(r)
            if w is not None:
                deps.append(w)
        for wn in o.writes:
            w = self.last_write.get(wn)
            if w is not None:
                deps.append(w)
            deps.extend(self.readers.get(wn, ()))
        best = {}
        for d in deps:
            if d is o:
                continue
            k = (d.eng, True, d.seq) if d.is_dma else (d.eng, False)
            if k not in best or d.seq > best[k].seq:
                best[k] = d
        o.waits = list(best.values())
        for r in o.reads:
            self.readers.setdefault(r, []).append(o)
        for wn in o.writes:
            self.last_write[wn] = o
            self.readers[wn] = []
        return o

    def dma(self, eng, fn, reads=(), writes=()):
        return self.op(eng, fn, reads, writes, is_dma=True)

    NLANES = 8

    def resolve(self):
        for o in self.ops:
            if o.is_dma or o.force_inc:
                o.inc = True
            real = []
            for d in o.waits:
                if d.eng == o.eng and not d.is_dma and not d.force_inc:
                    if d.eng != "tensor" or o.is_dma:
                        o.need_drain = True
                else:
                    d.inc = True
                    real.append(d)
            o.waits = real
        self.targets = {}
        self.self_wait = {}
        for e in ENGINES:
            cc = 0
            lane_use = [0] * self.NLANES
            dma_idx = 0
            for o in self.per_eng[e]:
                if not o.inc:
                    continue
                if o.is_dma:
                    lane = dma_idx % self.NLANES
                    dma_idx += 1
                    if lane_use[lane] > 0:
                        self.self_wait[(e, o.seq)] = ((e, "dma", lane),
                                                      16 * lane_use[lane])
                    lane_use[lane] += 1
                    self.targets[(e, True, o.seq)] = ((e, "dma", lane),
                                                      16 * lane_use[lane])
                    o.lane = lane
                else:
                    cc += 1
                    self.targets[(e, False, o.seq)] = ((e, "cmp"), cc)

    def make_runner(self, sems):
        waited = {}

        def run_engine(ename, eng):
            for o in self.per_eng[ename]:
                need = {}
                for d in o.waits:
                    sem_key, tgt = self.targets[(d.eng, d.is_dma, d.seq)]
                    if need.get(sem_key, 0) < tgt:
                        need[sem_key] = tgt
                for sem_key, tgt in need.items():
                    wk = (ename, sem_key)
                    if waited.get(wk, 0) < tgt:
                        eng.wait_ge(sems[sem_key], tgt)
                        waited[wk] = tgt
                sw = self.self_wait.get((ename, o.seq))
                if sw is not None and o.is_dma:
                    sem_key, tgt = sw
                    wk = (ename, sem_key)
                    if waited.get(wk, 0) < tgt:
                        eng.wait_ge(sems[sem_key], tgt)
                        waited[wk] = tgt
                if o.need_drain:
                    eng.drain()
                inst = o.fn(eng)
                if o.inc:
                    if o.is_dma:
                        inst.then_inc(sems[(ename, "dma", o.lane)], 16)
                    else:
                        inst.then_inc(sems[(ename, "cmp")], 1)
        return run_engine


def build_program(T, dbg=False):
    assert T % 512 == 0 and T >= 512
    NC = T // 512
    NT = T // 128

    nc = bass.Bass("TRN2", target_bir_lowering=False, debug=False)
    P = Plan()

    def din(name, shape):
        return nc.dram_tensor(name, shape, F32, kind="ExternalInput").ap()

    def dscr(name, shape):
        return nc.dram_tensor(name, shape, F32,
                              kind="ExternalOutput" if dbg else "Internal").ap()

    I = {}
    for name, shape in [
        ("x0T", [D, T]),
        ("WdT", [D, D]), ("WoT0", [D, D]), ("rwT", [D, NEXP]),
        ("moe_w1", [NEXP, D, MOE_H]), ("moe_w2", [NEXP, MOE_H, D]),
        ("WkT", [D, D]), ("WvT", [D, D]), ("WrT", [D, D]), ("WoT1", [D, D]),
        ("WfcT", [D, FCH]), ("WmlpT", [FCH, D]),
        ("slot0", [NSLOT, D]), ("w1a", [D, DH]), ("w1b", [D, DH]),
        ("w2s", [DH, NSLOT]), ("WihT", [D, GRU]), ("WhhT", [D, GRU]),
        ("wq_s", [D, DQ]), ("wk_s", [D, DQ]), ("wv_s", [D, D]),
        ("wvp_s", [D, D]), ("wo_s", [D, D]),
        ("bd_t", [128, DT]), ("bo_t", [128, DT]), ("a_log_t", [128, DT * NST]),
        ("bprm_t", [128, DT * NST]),
        ("td_t", [128, DT]), ("tf_t", [128, DT]), ("ms_t", [128, DT]),
        ("mls_t", [128, DT]),
        ("b1_t", [128, DH // 128]), ("b2_c", [NSLOT, 1]), ("tau_c", [1, 1]),
        ("bih_t", [128, GRU // 128]), ("bhh_t", [128, GRU // 128]),
        ("bq_t", [128, DQ // 128]), ("bk_t", [128, DQ // 128]),
        ("bv_t", [128, DT]), ("lng_t", [128, DT]), ("lnb_t", [128, DT]),
        ("bvp_t", [128, DT]), ("bo1_t", [128, DT]),
        ("ident", [128, 128]), ("onesk", [128, 128]), ("sel8", [NEXP, 128 * NEXP]),
        ("ramp0", [128, T]), ("ramp256", [128, 256]),
        ("m8", [128, 8]), ("im8", [128, 8]),
        ("mb8c", [NSLOT, 8]),
    ]:
        I[name] = din(name, shape)

    outT = nc.dram_tensor("outT", [D, T], F32, kind="ExternalOutput").ap()

    S = {}
    for name in ["xn0", "delta", "Cc", "y0", "s0", "g0", "x1", "xn1", "kv",
                 "rr", "hh", "xr", "s1", "g1", "sv"]:
        S[name] = dscr("d_" + name, [D, T])
    S["xn1N"] = dscr("d_xn1N", [T, D])
    S["xn0N"] = dscr("d_xn0N", [T, D])

    def dscr(name, shape, _d=dscr):
        return nc.dram_tensor(name, shape, F32, kind="Internal").ap()
    CC1 = 128 * DT * NST * 2
    S["cc1i"] = dscr("d_cc1i", [1, CC1])
    S["cc1o"] = dscr("d_cc1o", [8, CC1])
    S["c2ai"] = dscr("d_c2ai", [1, 128 * DT])
    S["c2ao"] = dscr("d_c2ao", [8, 128 * DT])
    CC2 = NSLOT * (D + 1)
    S["c2bi"] = dscr("d_c2bi", [1, CC2])
    S["c2bo"] = dscr("d_c2bo", [8, CC2])
    S["gh"] = dscr("d_gh", [128, (GRU // 128) * NSLOT])

    with contextlib.ExitStack() as stk:
        def sb(name, cols, rows=128, dtype=F32):
            return stk.enter_context(nc.sbuf_tensor(name, [rows, cols], dtype))

        rh = [sb("rh0", DT * 512), sb("rh1", DT * 512)]
        wsl = [sb("wsl0", 4096), sb("wsl1", 4096)]
        ob = sb("ob", DT * 512)
        hb = sb("hb", 4096)
        sc = [sb(f"sc{i}", T) for i in range(4)]
        ya = sb("ya", max(2 * T, DT * 512))
        sm = sb("sm", 256)
        sl = sb("sl", 4096)
        alp = sb("alp", NT * NSLOT)
        lgc = sb("lgc", 512, rows=NSLOT)
        b2sb = sb("b2sb", 1, rows=NSLOT)
        ss64 = sb("ss64", 16, rows=NSLOT)
        vecs = sb("vecs", 160)
        consts = sb("consts", 16)
        apost = sb("apost", DT * NST)
        napost = sb("napost", DT * NST)
        bprmt = sb("bprmt", DT * NST)
        identt = sb("identt", 128)
        oneskt = sb("oneskt", 128)
        sel8t = sb("sel8t", 128 * NEXP, rows=NEXP)
        m8t = sb("m8t", 8)
        im8t = sb("im8t", 8)
        mb8ct = sb("mb8ct", 8, rows=NSLOT)
        cc1s = sb("cc1s", DT * NST * 2)
        carry1 = sb("carry1", DT * NST)
        e1t = sb("e1t", DT)
        ne1t = sb("ne1t", DT)
        eut = sb("eut", DT)
        ewTt = sb("ewTt", DT)
        rwkvS = sb("rwkvS", DT)
        ewc = sb("ewc", DT)
        hprev = sb("hprev", DT)
        carry2 = sb("carry2", DT)
        sistage = sb("sistage", D + 1, rows=NSLOT)
        sigstg = sb("sigstg", D + 1, rows=NSLOT)
        gat8 = sb("gat8", T, rows=NEXP)
        rampt = sb("rampt", T)
        posm = sb("posm", NT * NEXP)
        gtm = sb("gtm", NT * NEXP)
        r256t = sb("r256t", 256)
        tiv = sb("tiv", 4)
        bh1 = sb("bh1", DH // 128)
        mski = sb("mski", NSLOT, dtype=mybir.dt.int32)
        gbt = sb("gbt", 512)

        def psumt(name):
            return stk.enter_context(nc.psum_tensor(name, [128, 512], F32))

        pm = [psumt(f"pm{i}") for i in range(4)]
        pb = [psumt(f"pb{i}") for i in range(2)]
        ptr = [psumt(f"ptr{i}") for i in range(2)]

        sems = {}
        for e in ENGINES:
            sems[(e, "cmp")] = stk.enter_context(nc.semaphore(f"s_{e}"))
        for e in ("sync", "gpsimd"):
            for ln_ in range(Plan.NLANES):
                sems[(e, "dma", ln_)] = stk.enter_context(
                    nc.semaphore(f"m_{e}{ln_}"))

        def LD(dst, src, rd, wr, eng="sync"):
            P.dma(eng, lambda E, dd=dst, ss=src: E.dma_start(out=dd, in_=ss), rd, wr)

        def STR(dst, src, rd, wr, eng="gpsimd"):
            P.dma(eng, lambda E, dd=dst, ss=src: E.dma_start(out=dd, in_=ss), rd, wr)

        def OP(eng, fn, rd=(), wr=()):
            P.op(eng, fn, rd, wr)

        def chunk_of(dram, c, width=512):
            return dram.rearrange("(k p) t -> p k t", p=128)[:, :, c * width:(c + 1) * width]

        def rhv(slot, k, width=512):
            return rh[slot][:, k * width:(k + 1) * width]

        def cname(name, c, width=512):
            c0 = c * width // 512
            c1 = (c * width + width + 511) // 512
            return [f"{name}@{i}" for i in range(c0, c1)]

        def allc(name):
            return [f"{name}@{i}" for i in range(NC)]

        def load_chunk(slot, dram, name, c, width=512):
            LD(rh[slot][:, :DT * width].rearrange("p (k t) -> p k t", k=DT),
               chunk_of(dram, c, width), cname(name, c, width), [f"rh{slot}"])

        def store_chunk(dram, name, c, src_sb, src_name, width=512):
            STR(chunk_of(dram, c, width),
                src_sb[:, :DT * width].rearrange("p (k t) -> p k t", k=DT),
                [src_name], cname(name, c, width))

        def MM(out_ap, bname, lhs_ap, lhs_names, rhs_ap, rhs_names, start, stop):
            OP("tensor",
               lambda E, o=out_ap, l=lhs_ap, r=rhs_ap, s1=start, s2=stop:
                   E.matmul(o, l, r, start=s1, stop=s2),
               list(lhs_names) + list(rhs_names), [bname])

        def mm_accum(bank_ap, bname, lhs_get, lhs_names, rhs_get, rhs_names, nK):
            for k in range(nK):
                MM(bank_ap, bname, lhs_get(k), lhs_names, rhs_get(k), rhs_names,
                   k == 0, k == nK - 1)

        def load_wblk(ws, Wd, Wname, nKt, c0, c1, k0=0):
            mcols = c1 - c0
            LD(wsl[ws][:, :nKt * mcols].rearrange("p (k m) -> p k m", k=nKt),
               Wd.rearrange("(k p) m -> p k m", p=128)[:, k0:k0 + nKt, c0:c1],
               [Wname], [f"wsl{ws}"])

        def big_mm(Wd, Wname, K, M, in_dram, in_name, evac, extra_per_chunk=None):
            nKt = K // 128
            nMb = (M + 511) // 512
            for c in range(NC):
                rslot = c % 2
                load_chunk(rslot, in_dram, in_name, c)
                if extra_per_chunk is not None:
                    extra_per_chunk(c)
                for mb in range(nMb):
                    mcols = min(512, M - mb * 512)
                    ws = mb % 2
                    load_wblk(ws, Wd, Wname, nKt, mb * 512, mb * 512 + mcols)
                    for mt in range(mcols // 128):
                        bank = pm[mt % 4]
                        bname = f"pm{mt % 4}"
                        mm_accum(bank[:, :], bname,
                                 lambda k, ws=ws, mt=mt, mc=mcols:
                                     wsl[ws][:, k * mc + mt * 128:k * mc + mt * 128 + 128],
                                 [f"wsl{ws}"],
                                 lambda k, rs=rslot: rhv(rs, k), [f"rh{rslot}"], nKt)
                        evac(mb * 4 + mt, c, bank, bname)

        LD(identt[:], I["ident"][:, :], ["i_ident"], ["identt"])
        LD(oneskt[:], I["onesk"][:, :], ["i_onesk"], ["oneskt"])
        LD(sel8t[:], I["sel8"][:, :], ["i_sel8"], ["sel8t"])
        LD(m8t[:], I["m8"][:, :], ["i_m8"], ["m8t"])
        LD(im8t[:], I["im8"][:, :], ["i_im8"], ["im8t"])
        LD(mb8ct[:], I["mb8c"][:, :], ["i_mb8c"], ["mb8ct"])
        LD(b2sb[:], I["b2_c"][:, :], ["i_b2"], ["b2sb"])
        LD(rampt[:], I["ramp0"][:, :], ["i_ramp0"], ["rampt"])
        LD(r256t[:], I["ramp256"][:, :], ["i_ramp256"], ["r256t"])
        VC = {}
        col = 0
        for nm_, wd in [("bd", DT), ("bo", DT), ("td", DT), ("tf", DT), ("ms", DT),
                        ("mls", DT), ("b1", DH // 128), ("bih", GRU // 128),
                        ("bhh", GRU // 128), ("bq", DQ // 128), ("bk", DQ // 128),
                        ("bv", DT), ("lng", DT), ("lnb", DT), ("bvp", DT), ("bo1", DT)]:
            VC[nm_] = col
            LD(vecs[:, col:col + wd], I[nm_ + "_t"][:, :], [f"i_{nm_}"], ["vecs"])
            col += wd

        def vc(nm_, idx=0):
            c0 = VC[nm_] + idx
            return vecs[:, c0:c0 + 1]

        CN = {"eps5": 0, "eps6": 1, "invD": 2, "one": 3, "neg30": 4, "scq": 5,
              "invS": 6, "eps8": 7, "mhalf": 8, "mone": 9, "negT": 10, "half": 11,
              "nscq": 12}
        for nm_, v in [("eps5", 1e-5), ("eps6", 1e-6), ("invD", 1.0 / D),
                       ("one", 1.0), ("neg30", -1e30), ("scq", DQ ** -0.5),
                       ("invS", 1.0 / NSLOT), ("eps8", 1e-8), ("mhalf", -0.5),
                       ("mone", -1.0), ("negT", float(-T)), ("half", 0.5),
                       ("nscq", -(DQ ** -0.5))]:
            OP("vector", lambda E, v=v, j=CN[nm_]: E.memset(consts[:, j:j + 1], v),
               (), ["consts"])

        def cn(nm_, rows=128):
            j = CN[nm_]
            return consts[0:rows, j:j + 1]

        nc.const_aps.register(F32, -0.5, consts[:, CN["mhalf"]:CN["mhalf"] + 1])
        nc.const_aps.register(F32, -1.0, consts[:, CN["mone"]:CN["mone"] + 1])
        nc.const_aps.register(F32, float(-T), consts[:, CN["negT"]:CN["negT"] + 1])

        LD(sm[:, 0:DT * NST], I["a_log_t"][:, :], ["i_alog"], ["sm"])
        OP("scalar", lambda E: E.activation(apost[:], sm[:, 0:DT * NST], ACTF.Abs),
           ["sm"], ["apost"])
        OP("scalar", lambda E: E.activation(apost[:], apost[:], ACTF.Exp, scale=-1.0),
           ["apost"], ["apost"])
        OP("scalar", lambda E: E.activation(apost[:], apost[:], ACTF.Ln,
                                            bias=cn("one")), ["apost", "consts"],
           ["apost"])
        OP("vector", lambda E: E.scalar_tensor_tensor(
            apost[:], sm[:, 0:DT * NST], 0.0, apost[:], ALU.max, ALU.add),
            ["sm", "apost"], ["apost"])
        OP("vector", lambda E: E.tensor_scalar(apost[:], apost[:], 1e-4, None, ALU.add),
           ["apost"], ["apost"])
        OP("vector", lambda E: E.tensor_scalar(napost[:], apost[:], -1.0, None, ALU.mult),
           ["apost"], ["napost"])
        LD(bprmt[:], I["bprm_t"][:, :], ["i_bprm"], ["bprmt"])

        LD(tiv[0:1, 0:1], I["tau_c"][:, :], ["i_tau"], ["tiv"])
        OP("scalar", lambda E: E.activation(tiv[0:1, 1:2], tiv[0:1, 0:1], ACTF.Abs),
           ["tiv"], ["tiv"])
        OP("vector", lambda E: E.tensor_scalar(tiv[0:1, 1:2], tiv[0:1, 1:2], 0.1,
                                               None, ALU.add), ["tiv"], ["tiv"])
        OP("vector", lambda E: E.reciprocal(tiv[0:1, 2:3], tiv[0:1, 1:2]),
           ["tiv"], ["tiv"])
        OP("tensor", lambda E: E.matmul(ptr[0][:, 0:1], oneskt[0:1, :], tiv[0:1, 2:3],
                                        start=True, stop=True),
           ["oneskt", "tiv"], ["ptr0"])
        OP("vector", lambda E: E.tensor_copy(tiv[:, 3:4], ptr[0][:, 0:1]),
           ["ptr0"], ["tiv"])

        OP("scalar", lambda E: E.activation(e1t[:, :], vecs[:, VC["td"]:VC["td"] + DT],
                                            ACTF.Exp), ["vecs"], ["e1t"])
        OP("vector", lambda E: E.tensor_scalar(ne1t[:, :], e1t[:, :], -1.0, None,
                                               ALU.mult), ["e1t"], ["ne1t"])
        OP("scalar", lambda E: E.activation(eut[:, :], vecs[:, VC["tf"]:VC["tf"] + DT],
                                            ACTF.Exp), ["vecs"], ["eut"])
        OP("scalar", lambda E: E.activation(ewTt[:, :], e1t[:, :], ACTF.Exp,
                                            scale=float(-T)), ["e1t"], ["ewTt"])

        def stage_norm(src, srcn, dst, dstn, kind):
            for c in range(NC):
                cs = slice(c * 512, (c + 1) * 512)
                slot = c % 2
                load_chunk(slot, src, srcn, c)
                if kind == "ln":
                    mm_accum(pb[0][:, :], "pb0", lambda k: oneskt[:], ["oneskt"],
                             lambda k, s=slot: rhv(s, k), [f"rh{slot}"], DT)
                    OP("vector", lambda E, cs=cs: E.tensor_scalar(
                        sc[1][:, cs], pb[0][:, :], cn("invD"), None, ALU.mult),
                        ["pb0", "consts"], ["sc1"])
                for k in range(DT):
                    half = (k % 2) * 512 if T >= 1024 else 0
                    OP("vector", lambda E, k=k, s=slot, h=half: E.tensor_mul(
                        sc[0][:, h:h + 512], rhv(s, k), rhv(s, k)),
                        [f"rh{slot}"], ["sc0"])
                    OP("tensor", lambda E, k=k, h=half: E.matmul(
                        pb[1][:, :], oneskt[:], sc[0][:, h:h + 512],
                        start=(k == 0), stop=(k == DT - 1)),
                        ["sc0", "oneskt"], ["pb1"])
                OP("vector", lambda E, cs=cs: E.tensor_scalar(
                    sc[2][:, cs], pb[1][:, :], cn("invD"), None, ALU.mult),
                    ["pb1", "consts"], ["sc2"])
                if kind == "ln":
                    OP("vector", lambda E, cs=cs: E.tensor_mul(
                        sc[3][:, cs], sc[1][:, cs], sc[1][:, cs]), ["sc1"], ["sc3"])
                    OP("vector", lambda E, cs=cs: E.tensor_sub(
                        sc[2][:, cs], sc[2][:, cs], sc[3][:, cs]),
                        ["sc2", "sc3"], ["sc2"])
                OP("scalar", lambda E, cs=cs: E.activation(
                    sc[2][:, cs], sc[2][:, cs], ACTF.Ln,
                    bias=cn("eps5") if kind == "ln" else cn("eps6")),
                    ["sc2", "consts"], ["sc2"])
                OP("scalar", lambda E, cs=cs: E.activation(
                    sc[2][:, cs], sc[2][:, cs], ACTF.Exp, scale=-0.5),
                    ["sc2"], ["sc2"])
                for k in range(DT):
                    if kind == "ln":
                        OP("vector", lambda E, k=k, s=slot, cs=cs: E.tensor_sub(
                            ob[:, k * 512:(k + 1) * 512], rhv(s, k), sc[1][:, cs]),
                            [f"rh{slot}", "sc1"], ["ob"])
                        OP("vector", lambda E, k=k, cs=cs: E.tensor_mul(
                            ob[:, k * 512:(k + 1) * 512], ob[:, k * 512:(k + 1) * 512],
                            sc[2][:, cs]), ["ob", "sc2"], ["ob"])
                    else:
                        OP("vector", lambda E, k=k, s=slot, cs=cs: E.tensor_mul(
                            ob[:, k * 512:(k + 1) * 512], rhv(s, k), sc[2][:, cs]),
                            [f"rh{slot}", "sc2"], ["ob"])
                store_chunk(dst, dstn, c, ob, "ob")

        stage_norm(I["x0T"], "i_x0T", S["xn0"], "d_xn0", "ln")

        for tt in range(NT):
            load_chunk(0, S["xn0"], "d_xn0", tt, width=128)
            for k in range(DT):
                OP("tensor", lambda E, k=k: E.transpose(
                    ptr[k % 2][:, 0:128], rh[0][:, k * 128:(k + 1) * 128], identt[:]),
                    ["rh0", "identt"], [f"ptr{k % 2}"])
                OP("vector", lambda E, k=k: E.tensor_copy(
                    ob[:, k * 128:(k + 1) * 128], ptr[k % 2][:, 0:128]),
                    [f"ptr{k % 2}"], ["ob"])
            STR(S["xn0N"][tt * 128:(tt + 1) * 128, :], ob[:, 0:D], ["ob"],
                [f"d_xn0N@{tt}"])

        def evac_delta(mt, c, bank, bname):
            half = (mt % 2) * 512 if T >= 1024 else 0
            o_t = ob[:, mt * 512:(mt + 1) * 512]
            s_t = sc[0][:, half:half + 512]
            OP("scalar", lambda E, bank=bank, o_t=o_t: E.activation(
                o_t, bank[:, :], ACTF.Identity, bias=vc("bd", mt)),
                [bname, "vecs"], ["ob"])
            OP("scalar", lambda E, o_t=o_t, s_t=s_t: E.activation(
                s_t, o_t, ACTF.Abs), ["ob"], ["sc0"])
            OP("scalar", lambda E, s_t=s_t: E.activation(
                s_t, s_t, ACTF.Exp, scale=-1.0), ["sc0"], ["sc0"])
            OP("scalar", lambda E, s_t=s_t: E.activation(
                s_t, s_t, ACTF.Ln, bias=cn("one")), ["sc0", "consts"], ["sc0"])
            OP("vector", lambda E, o_t=o_t, s_t=s_t: E.scalar_tensor_tensor(
                o_t, o_t, 0.0, s_t, ALU.max, ALU.add), ["ob", "sc0"], ["ob"])
            if mt == DT - 1:
                store_chunk(S["delta"], "d_delta", c, ob, "ob")
        big_mm(I["WdT"], "i_WdT", D, D, S["xn0"], "d_xn0", evac_delta)

        side = []

        def SIDE(fn):
            side.append(fn)

        def s3_load(et):
            LD(sc[0][:, :], S["delta"][et * 128:(et + 1) * 128, :], allc("d_delta"),
               ["sc0"], eng="gpsimd")
            LD(sc[1][:, :], S["xn0"][et * 128:(et + 1) * 128, :], allc("d_xn0"),
               ["sc1"], eng="gpsimd")
            OP("gpsimd", lambda E: E.tensor_mul(sc[1][:, :], sc[0][:, :], sc[1][:, :]),
               ["sc0", "sc1"], ["sc1"])
            OP("vector", lambda E: E.tensor_tensor_scan(
                sc[2][:, :], cn("one").broadcast_to([128, T]), sc[0][:, :], 0.0,
                ALU.mult, ALU.add), ["sc0", "consts"], ["sc2"])
            STR(S["Cc"][et * 128:(et + 1) * 128, :], sc[2][:, :], ["sc2"],
                [f"d_Cc@r{et}"])
            for n in range(NST):
                ci = et * NST + n
                OP("scalar", lambda E, ci=ci: E.activation(
                    cc1s[:, ci:ci + 1], sc[2][:, T - 1:T], ACTF.Exp,
                    scale=napost[:, ci:ci + 1]), ["sc2", "napost"], ["cc1s"])

        def s3_n(et, n):
            ci = et * NST + n
            OP("scalar", lambda E, ci=ci: E.activation(
                sc[3][:, :], sc[0][:, :], ACTF.Exp, scale=napost[:, ci:ci + 1]),
                ["sc0", "napost"], ["sc3"])
            OP("gpsimd", lambda E, ci=ci: E.tensor_tensor(
                ya[:, T:2 * T], sc[1][:, :],
                bprmt[:, ci:ci + 1].broadcast_to([128, T]), ALU.mult),
                ["sc1", "bprmt"], ["ya"])
            OP("vector", lambda E: E.tensor_tensor_scan(
                ya[:, T:2 * T], sc[3][:, :], ya[:, T:2 * T], 0.0, ALU.mult,
                ALU.add), ["sc3", "ya"], ["ya"])
            OP("gpsimd", lambda E, ci=ci: E.tensor_copy(
                cc1s[:, DT * NST + ci:DT * NST + ci + 1], ya[:, 2 * T - 1:2 * T]),
                ["ya"], ["cc1s"])
            if n == 0:
                OP("gpsimd", lambda E: E.tensor_copy(ya[:, 0:T], ya[:, T:2 * T]),
                   ["ya"], ["ya"])
            else:
                OP("gpsimd", lambda E: E.tensor_add(
                    ya[:, 0:T], ya[:, 0:T], ya[:, T:2 * T]), ["ya", "ya"], ["ya"])

        def s3_store(et):
            STR(S["y0"][et * 128:(et + 1) * 128, :], ya[:, 0:T], ["ya"],
                [f"d_y0@r{et}"])

        for et in range(DT):
            SIDE(lambda et=et: s3_load(et))
            for n in range(NST):
                SIDE(lambda et=et, n=n: s3_n(et, n))
            SIDE(lambda et=et: s3_store(et))

        def s4_unit():
            STR(S["cc1i"][0, :].rearrange("(p c) -> p c", p=128), cc1s[:, :],
                ["cc1s"], ["d_cc1i"])
            P.op("gpsimd", lambda E: E.collective_compute(
                "AllGather", ALU.bypass, replica_groups=[list(range(8))],
                ins=[S["cc1i"].opt()], outs=[S["cc1o"].opt()]),
                ["d_cc1i"], ["d_cc1o"], force_inc=True)
            NB = DT * NST
            OP("vector", lambda E: E.memset(carry1[:, :], 0.0), (), ["carry1"])
            for j in range(8):
                LD(cc1s[:, :], S["cc1o"][j, :].rearrange("(p c) -> p c", p=128),
                   ["d_cc1o"], ["cc1s"], eng="gpsimd")
                OP("vector", lambda E, j=j: E.tensor_scalar(
                    sm[:, 0:NB], cc1s[:, 0:NB], m8t[:, j:j + 1],
                    im8t[:, j:j + 1], ALU.mult, ALU.add),
                    ["cc1s", "m8t", "im8t"], ["sm"])
                OP("vector", lambda E: E.tensor_mul(carry1[:, :], carry1[:, :],
                                                    sm[:, 0:NB]), ["carry1", "sm"],
                   ["carry1"])
                OP("vector", lambda E, j=j: E.tensor_scalar(
                    sm[:, 0:NB], cc1s[:, NB:2 * NB],
                    m8t[:, j:j + 1], None, ALU.mult), ["cc1s", "m8t"], ["sm"])
                OP("vector", lambda E: E.tensor_add(carry1[:, :], carry1[:, :],
                                                    sm[:, 0:NB]), ["carry1", "sm"],
                   ["carry1"])

        SIDE(s4_unit)

        def s5_load(et):
            LD(sc[0][:, :], S["Cc"][et * 128:(et + 1) * 128, :], [f"d_Cc@r{et}"],
               ["sc0"], eng="gpsimd")
            LD(ya[:, 0:T], S["y0"][et * 128:(et + 1) * 128, :], [f"d_y0@r{et}"],
               ["ya"], eng="gpsimd")

        def s5_n(et, n):
            ci = et * NST + n
            OP("scalar", lambda E, ci=ci: E.activation(
                sc[3][:, :], sc[0][:, :], ACTF.Exp, scale=napost[:, ci:ci + 1]),
                ["sc0", "napost"], ["sc3"])
            OP("gpsimd", lambda E, ci=ci: E.tensor_tensor(
                sc[2][:, :], sc[3][:, :],
                carry1[:, ci:ci + 1].broadcast_to([128, T]), ALU.mult),
                ["sc3", "carry1"], ["sc2"])
            OP("gpsimd", lambda E: E.tensor_add(
                ya[:, 0:T], ya[:, 0:T], sc[2][:, :]), ["ya", "sc2"], ["ya"])

        def s5_store(et):
            STR(S["y0"][et * 128:(et + 1) * 128, :], ya[:, 0:T], ["ya"],
                [f"d_y0@r{et}"])

        for et in range(DT):
            SIDE(lambda et=et: s5_load(et))
            for n in range(NST):
                SIDE(lambda et=et, n=n: s5_n(et, n))
            SIDE(lambda et=et: s5_store(et))

        LD(wsl[0][:, 0:DT * NEXP].rearrange("p (k m) -> p k m", k=DT),
           I["rwT"].rearrange("(k p) m -> p k m", p=128), ["i_rwT"], ["wsl0"])
        for c in range(NC):
            slot = c % 2
            load_chunk(slot, S["xn0"], "d_xn0", c)
            mm_accum(pb[0][0:NEXP, :], "pb0",
                     lambda k: wsl[0][:, k * NEXP:(k + 1) * NEXP], ["wsl0"],
                     lambda k, s=slot: rhv(s, k), [f"rh{slot}"], DT)
            OP("vector", lambda E: E.tensor_copy(hb[0:NEXP, 0:512], pb[0][0:NEXP, :]),
               ["pb0"], ["hb"])
            for tti in range(4):
                tt = c * 4 + tti
                OP("tensor", lambda E, tti=tti: E.transpose(
                    ptr[0][:, 0:NEXP], hb[0:NEXP, tti * 128:(tti + 1) * 128],
                    identt[0:NEXP, 0:NEXP]), ["hb", "identt"], ["ptr0"])
                OP("vector", lambda E: E.tensor_copy(sm[:, 0:NEXP], ptr[0][:, 0:NEXP]),
                   ["ptr0"], ["sm"])
                OP("vector", lambda E: E.reduce_max(sm[:, 16:17], sm[:, 0:NEXP],
                                                    axis=AX.X), ["sm"], ["sm"])
                OP("vector", lambda E: E.tensor_copy(sm[:, 8:8 + NEXP], sm[:, 0:NEXP]),
                   ["sm"], ["sm"])
                OP("vector", lambda E: E.tensor_scalar(
                    mski[:, 0:NEXP], sm[:, 0:NEXP], sm[:, 16:17], None,
                    ALU.is_ge), ["sm"], ["mski"])
                OP("vector", lambda E: E.copy_predicated(
                    sm[:, 8:8 + NEXP], mski[:, 0:NEXP],
                    cn("neg30").broadcast_to([128, NEXP])),
                    ["sm", "mski", "consts"], ["sm"])
                OP("vector", lambda E: E.reduce_max(sm[:, 17:18], sm[:, 8:8 + NEXP],
                                                    axis=AX.X), ["sm"], ["sm"])
                OP("vector", lambda E: E.tensor_scalar(
                    sm[:, 18:19], sm[:, 16:17], -1.0, None, ALU.mult), ["sm"], ["sm"])
                OP("scalar", lambda E: E.activation(
                    sm[:, 24:24 + NEXP], sm[:, 0:NEXP], ACTF.Exp, bias=sm[:, 18:19]),
                    ["sm"], ["sm"])
                OP("scalar", lambda E: E.activation(
                    sm[:, 19:20], sm[:, 17:18], ACTF.Exp, bias=sm[:, 18:19]),
                    ["sm"], ["sm"])
                OP("vector", lambda E: E.tensor_scalar(
                    sm[:, 19:20], sm[:, 19:20], 1.0, None, ALU.add), ["sm"], ["sm"])
                OP("vector", lambda E: E.reciprocal(sm[:, 20:21], sm[:, 19:20]),
                   ["sm"], ["sm"])
                OP("vector", lambda E: E.tensor_scalar(
                    sm[:, 32:32 + NEXP], sm[:, 0:NEXP], sm[:, 17:18], None,
                    ALU.is_ge), ["sm"], ["sm"])
                OP("vector", lambda E: E.tensor_scalar(
                    sm[:, 24:24 + NEXP], sm[:, 24:24 + NEXP], sm[:, 20:21], None,
                    ALU.mult), ["sm"], ["sm"])
                OP("vector", lambda E: E.tensor_mul(
                    sm[:, 24:24 + NEXP], sm[:, 24:24 + NEXP], sm[:, 32:32 + NEXP]),
                    ["sm"], ["sm"])
                OP("vector", lambda E, tt=tt: E.tensor_copy(
                    gtm[:, tt * NEXP:(tt + 1) * NEXP], sm[:, 24:24 + NEXP]),
                    ["sm"], ["gtm"])
                OP("tensor", lambda E: E.transpose(
                    ptr[1][0:NEXP, 0:128], sm[:, 24:24 + NEXP], identt[:]),
                    ["sm", "identt"], ["ptr1"])
                OP("vector", lambda E, tt=tt: E.tensor_copy(
                    gat8[0:NEXP, tt * 128:(tt + 1) * 128], ptr[1][0:NEXP, 0:128]),
                    ["ptr1"], ["gat8"])
            OP("vector", lambda E, c=c: E.tensor_scalar(
                hb[0:NEXP, 512:1024], gat8[0:NEXP, c * 512:(c + 1) * 512], 0.0,
                None, ALU.is_gt), ["gat8"], ["hb"])
            OP("vector", lambda E: E.tensor_tensor_scan(
                hb[0:NEXP, 1024:1536], cn("one", NEXP).broadcast_to([NEXP, 512]),
                hb[0:NEXP, 512:1024], 0.0, ALU.mult, ALU.add),
                ["hb", "consts"], ["hb"])
            OP("vector", lambda E: E.tensor_mul(
                hb[0:NEXP, 1024:1536], hb[0:NEXP, 1024:1536],
                hb[0:NEXP, 512:1024]), ["hb"], ["hb"])
            OP("vector", lambda E: E.tensor_scalar(
                hb[0:NEXP, 1024:1536], hb[0:NEXP, 1024:1536], -1.0, None,
                ALU.add), ["hb"], ["hb"])
            for tti in range(4):
                tt = c * 4 + tti
                OP("tensor", lambda E, tti=tti: E.transpose(
                    ptr[0][:, 0:NEXP], hb[0:NEXP, 1024 + tti * 128:1024 + (tti + 1) * 128],
                    identt[0:NEXP, 0:NEXP]), ["hb", "identt"], ["ptr0"])
                OP("vector", lambda E, tt=tt: E.tensor_copy(
                    posm[:, tt * NEXP:(tt + 1) * NEXP], ptr[0][:, 0:NEXP]),
                    ["ptr0"], ["posm"])

        n_points = NC * NEXP * 24
        n_side = len(side)
        side_i = [0]
        point_i = [0]

        def pop_side(k):
            point_i[0] += k
            want = min(n_side, (point_i[0] * n_side * 10) // (n_points * 9))
            while side_i[0] < want:
                side[side_i[0]]()
                side_i[0] += 1

        def drain_side():
            while side_i[0] < len(side):
                side[side_i[0]]()
                side_i[0] += 1

        for c in range(NC):
            for tti in range(4):
                tt = c * 4 + tti
                LD(rh[0][:, tti * 1024:(tti + 1) * 1024], S["xn0N"][tt * 128:(tt + 1) * 128, :],
                   [f"d_xn0N@{tt}"], ["rh0"])
            for e in range(NEXP):
                for tti in range(4):
                    tt = c * 4 + tti
                    OP("vector", lambda E, tti=tti, tt=tt, e=e: E.tensor_scalar(
                        hb[:, tti * 256:(tti + 1) * 256], r256t[:, :],
                        posm[:, tt * NEXP + e:tt * NEXP + e + 1], None,
                        ALU.is_equal), ["r256t", "posm"], ["hb"])
                for dtile in range(DT):
                    bank, bname = pm[dtile % 2], f"pm{dtile % 2}"
                    for tti in range(4):
                        MM(bank[:, 0:256], bname,
                           rh[0][:, tti * 1024 + dtile * 128:tti * 1024 + dtile * 128 + 128],
                           ["rh0"], hb[:, tti * 256:(tti + 1) * 256], ["hb"],
                           tti == 0, tti == 3)
                    OP("scalar", lambda E, dtile=dtile, bank=bank, bname=bname:
                       E.copy(rh[1][:, dtile * 256:(dtile + 1) * 256],
                              bank[:, 0:256]), [bname], ["rh1"])
                pop_side(2)
                for half in range(2):
                    for hmb in range(2):
                        ws = hmb
                        h0 = half * 1024 + hmb * 512
                        load_wblk(ws, I["moe_w1"][e], "i_moe_w1", DT, h0, h0 + 512)
                        for mt in range(4):
                            hm = half * 8 + hmb * 4 + mt
                            bank, bname = pm[hm % 4], f"pm{hm % 4}"
                            mm_accum(bank[:, 0:256], bname,
                                     lambda k, ws=ws, mt=mt:
                                         wsl[ws][:, k * 512 + mt * 128:k * 512 + mt * 128 + 128],
                                     [f"wsl{ws}"],
                                     lambda k: rh[1][:, k * 256:(k + 1) * 256],
                                     ["rh1"], DT)
                            OP("scalar", lambda E, hm=hm, bank=bank, bname=bname:
                               E.activation(hb[:, hm * 256:(hm + 1) * 256],
                                            bank[:, 0:256], ACTF.Relu),
                               [bname], ["hb"])
                            OP("scalar", lambda E, hm=hm: E.activation(
                                hb[:, hm * 256:(hm + 1) * 256],
                                hb[:, hm * 256:(hm + 1) * 256], ACTF.Square),
                                ["hb"], ["hb"])
                            pop_side(1)
                for ht in range(16):
                    ws = ht % 2
                    LD(wsl[ws][:, 0:1024], I["moe_w2"][e][ht * 128:(ht + 1) * 128, :],
                       ["i_moe_w2"], [f"wsl{ws}"])
                    for jh in range(2):
                        for dh in range(2):
                            bank, bname = pm[jh * 2 + dh], f"pm{jh * 2 + dh}"
                            MM(bank[:, :], bname,
                               hb[:, ht * 256 + jh * 128:ht * 256 + jh * 128 + 128],
                               ["hb"], wsl[ws][:, dh * 512:(dh + 1) * 512],
                               [f"wsl{ws}"], ht == 0, ht == 15)
                    if ht % 4 == 3:
                        pop_side(1)
                for jh in range(2):
                    for dh in range(2):
                        bank, bname = pm[jh * 2 + dh], f"pm{jh * 2 + dh}"
                        OP("scalar", lambda E, jh=jh, dh=dh, bank=bank, bname=bname:
                           E.copy(sl[:, jh * 1024 + dh * 512:jh * 1024 + dh * 512 + 512],
                                  bank[:, :]), [bname], ["sl"])
                for tti in range(4):
                    tt = c * 4 + tti
                    OP("vector", lambda E, tti=tti, tt=tt, e=e: E.tensor_scalar(
                        hb[:, tti * 256:(tti + 1) * 256], r256t[:, :],
                        posm[:, tt * NEXP + e:tt * NEXP + e + 1], None,
                        ALU.is_equal), ["r256t", "posm"], ["hb"])
                    OP("vector", lambda E, tti=tti, tt=tt, e=e: E.tensor_scalar(
                        hb[:, tti * 256:(tti + 1) * 256],
                        hb[:, tti * 256:(tti + 1) * 256],
                        gtm[:, tt * NEXP + e:tt * NEXP + e + 1], None,
                        ALU.mult), ["hb", "gtm"], ["hb"])
                    for jh in range(2):
                        OP("tensor", lambda E, tti=tti, jh=jh: E.transpose(
                            ptr[jh][:, 0:128],
                            hb[:, tti * 256 + jh * 128:tti * 256 + jh * 128 + 128],
                            identt[:]), ["hb", "identt"], [f"ptr{jh}"])
                        OP("vector", lambda E, tti=tti, jh=jh: E.tensor_copy(
                            hb[:, 2048 + (tti * 2 + jh) * 128:2048 + (tti * 2 + jh) * 128 + 128],
                            ptr[jh][:, 0:128]), [f"ptr{jh}"], ["hb"])
                for tti in range(4):
                    for dh in range(2):
                        bank, bname = pb[dh], f"pb{dh}"
                        for jh in range(2):
                            MM(bank[:, :], bname,
                               hb[:, 2048 + (tti * 2 + jh) * 128:2048 + (tti * 2 + jh) * 128 + 128],
                               ["hb"], sl[:, jh * 1024 + dh * 512:jh * 1024 + dh * 512 + 512],
                               ["sl"], jh == 0, jh == 1)
                        if e == 0:
                            OP("vector", lambda E, tti=tti, dh=dh, bank=bank,
                               bname=bname: E.tensor_copy(
                                   ob[:, tti * 1024 + dh * 512:tti * 1024 + dh * 512 + 512],
                                   bank[:, :]), [bname], ["ob"])
                        else:
                            OP("vector", lambda E, tti=tti, dh=dh, bank=bank,
                               bname=bname: E.tensor_add(
                                   ob[:, tti * 1024 + dh * 512:tti * 1024 + dh * 512 + 512],
                                   ob[:, tti * 1024 + dh * 512:tti * 1024 + dh * 512 + 512],
                                   bank[:, :]), ["ob", bname], ["ob"])
                pop_side(2)
            OP("scalar", lambda E: E.activation(ob[:, 0:2048], ob[:, 0:2048],
                                                ACTF.Silu), ["ob"], ["ob"])
            OP("scalar", lambda E: E.activation(ob[:, 2048:4096], ob[:, 2048:4096],
                                                ACTF.Silu), ["ob"], ["ob"])
            for tti in range(4):
                for dtile in range(DT):
                    pk = (tti * DT + dtile) % 2
                    OP("tensor", lambda E, tti=tti, dtile=dtile, pk=pk: E.transpose(
                        ptr[pk][:, 0:128],
                        ob[:, tti * 1024 + dtile * 128:tti * 1024 + dtile * 128 + 128],
                        identt[:]), ["ob", "identt"], [f"ptr{pk}"])
                    OP("vector", lambda E, tti=tti, dtile=dtile, pk=pk: E.tensor_copy(
                        hb[:, dtile * 512 + tti * 128:dtile * 512 + tti * 128 + 128],
                        ptr[pk][:, 0:128]), [f"ptr{pk}"], ["hb"])
            store_chunk(S["g0"], "d_g0", c, hb, "hb")
        drain_side()

        for c in range(NC):
            rslot = c % 2
            oslot = 1 - rslot
            cs = slice(c * 512, (c + 1) * 512)
            LD(rh[rslot][:, :DT * 512].rearrange("p (k t) -> p k t", k=DT),
               chunk_of(S["y0"], c), [f"d_y0@r{i}" for i in range(DT)],
               [f"rh{rslot}"])
            LD(hb[:, 0:DT * 512].rearrange("p (k t) -> p k t", k=DT),
               chunk_of(S["g0"], c), cname("d_g0", c), ["hb"])
            LD(rh[oslot][:, :DT * 512].rearrange("p (k t) -> p k t", k=DT),
               chunk_of(I["x0T"], c), ["i_x0T"], [f"rh{oslot}"])
            for mb in range(2):
                ws = mb % 2
                load_wblk(ws, I["WoT0"], "i_WoT0", DT, mb * 512, (mb + 1) * 512)
                for mt in range(4):
                    m_g = mb * 4 + mt
                    bank, bname = pm[mt % 2], f"pm{mt % 2}"
                    mm_accum(bank[:, :], bname,
                             lambda k, ws=ws, mt=mt: wsl[ws][:, k * 512 + mt * 128:k * 512 + mt * 128 + 128],
                             [f"wsl{ws}"],
                             lambda k, rs=rslot: rhv(rs, k), [f"rh{rslot}"], DT)
                    OP("scalar", lambda E, m_g=m_g, bank=bank, bname=bname: E.activation(
                        ob[:, m_g * 512:(m_g + 1) * 512], bank[:, :], ACTF.Identity,
                        bias=vc("bo", m_g)), [bname, "vecs"], ["ob"])
            for k in range(DT):
                OP("vector", lambda E, k=k: E.tensor_mul(
                    ob[:, k * 512:(k + 1) * 512], ob[:, k * 512:(k + 1) * 512],
                    hb[:, k * 512:(k + 1) * 512]), ["ob", "hb"], ["ob"])
                OP("vector", lambda E, k=k, s=oslot: E.tensor_add(
                    ob[:, k * 512:(k + 1) * 512], ob[:, k * 512:(k + 1) * 512],
                    rhv(s, k)), ["ob", f"rh{oslot}"], ["ob"])
            store_chunk(S["x1"], "d_x1", c, ob, "ob")
            mm_accum(pb[0][:, :], "pb0", lambda k: oneskt[:], ["oneskt"],
                     lambda k: ob[:, k * 512:(k + 1) * 512], ["ob"], DT)
            OP("vector", lambda E, cs=cs: E.tensor_scalar(
                sc[1][:, cs], pb[0][:, :], cn("invD"), None, ALU.mult),
                ["pb0", "consts"], ["sc1"])
            for k in range(DT):
                half = (k % 2) * 512 if T >= 1024 else 0
                OP("vector", lambda E, k=k, h=half: E.tensor_mul(
                    sc[0][:, h:h + 512], ob[:, k * 512:(k + 1) * 512],
                    ob[:, k * 512:(k + 1) * 512]), ["ob"], ["sc0"])
                OP("tensor", lambda E, k=k, h=half: E.matmul(
                    pb[1][:, :], oneskt[:], sc[0][:, h:h + 512],
                    start=(k == 0), stop=(k == DT - 1)), ["sc0", "oneskt"], ["pb1"])
            OP("vector", lambda E, cs=cs: E.tensor_scalar(
                sc[2][:, cs], pb[1][:, :], cn("invD"), None, ALU.mult),
                ["pb1", "consts"], ["sc2"])
            OP("vector", lambda E, cs=cs: E.tensor_mul(
                sc[3][:, cs], sc[1][:, cs], sc[1][:, cs]), ["sc1"], ["sc3"])
            OP("vector", lambda E, cs=cs: E.tensor_sub(
                sc[2][:, cs], sc[2][:, cs], sc[3][:, cs]), ["sc2", "sc3"], ["sc2"])
            OP("scalar", lambda E, cs=cs: E.activation(
                sc[2][:, cs], sc[2][:, cs], ACTF.Ln, bias=cn("eps5")),
                ["sc2", "consts"], ["sc2"])
            OP("scalar", lambda E, cs=cs: E.activation(
                sc[2][:, cs], sc[2][:, cs], ACTF.Exp, scale=-0.5), ["sc2"], ["sc2"])
            for k in range(DT):
                OP("vector", lambda E, k=k, cs=cs: E.tensor_sub(
                    hb[:, k * 512:(k + 1) * 512], ob[:, k * 512:(k + 1) * 512],
                    sc[1][:, cs]), ["ob", "sc1"], ["hb"])
                OP("vector", lambda E, k=k, cs=cs: E.tensor_mul(
                    hb[:, k * 512:(k + 1) * 512], hb[:, k * 512:(k + 1) * 512],
                    sc[2][:, cs]), ["hb", "sc2"], ["hb"])
            store_chunk(S["xn1"], "d_xn1", c, hb, "hb")
            for tti in range(4):
                tt = c * 4 + tti
                for k in range(DT):
                    OP("tensor", lambda E, k=k, tti=tti: E.transpose(
                        ptr[k % 2][:, 0:128],
                        hb[:, k * 512 + tti * 128:k * 512 + tti * 128 + 128],
                        identt[:]), ["hb", "identt"], [f"ptr{k % 2}"])
                    OP("vector", lambda E, k=k, s=oslot: E.tensor_copy(
                        rh[s][:, k * 128:(k + 1) * 128], ptr[k % 2][:, 0:128]),
                        [f"ptr{k % 2}"], [f"rh{oslot}"])
                STR(S["xn1N"][tt * 128:(tt + 1) * 128, :], rh[oslot][:, 0:D],
                    [f"rh{oslot}"], [f"d_xn1N@{tt}"])

        stage_norm(S["xn1"], "d_xn1", S["xr"], "d_xr", "rms")
        OP("vector", lambda E: E.memset(rwkvS[:, :], 0.0), (), ["rwkvS"])
        OP("scalar", lambda E: E.activation(ewc[:, :], e1t[:, :], ACTF.Exp,
                                            scale=-1.0), ["e1t"], ["ewc"])
        for c in range(NC):
            rslot = c % 2
            load_chunk(rslot, S["xr"], "d_xr", c)
            for mb in range(2):
                load_wblk(0, I["WkT"], "i_WkT", DT, mb * 512, (mb + 1) * 512)
                for mt in range(4):
                    m_g = mb * 4 + mt
                    bank, bname = pm[mt % 2], f"pm{mt % 2}"
                    mm_accum(bank[:, :], bname,
                             lambda k, mt=mt: wsl[0][:, k * 512 + mt * 128:k * 512 + mt * 128 + 128],
                             ["wsl0"],
                             lambda k, rs=rslot: rhv(rs, k), [f"rh{rslot}"], DT)
                    OP("vector", lambda E, m_g=m_g, bank=bank, bname=bname:
                       E.tensor_copy(hb[:, m_g * 512:(m_g + 1) * 512], bank[:, :]),
                       [bname], ["hb"])
                load_wblk(1, I["WvT"], "i_WvT", DT, mb * 512, (mb + 1) * 512)
                for mt in range(4):
                    m_g = mb * 4 + mt
                    bank, bname = pm[2 + mt % 2], f"pm{2 + mt % 2}"
                    mm_accum(bank[:, :], bname,
                             lambda k, mt=mt: wsl[1][:, k * 512 + mt * 128:k * 512 + mt * 128 + 128],
                             ["wsl1"],
                             lambda k, rs=rslot: rhv(rs, k), [f"rh{rslot}"], DT)
                    OP("vector", lambda E, m_g=m_g, bank=bank, bname=bname:
                       E.tensor_mul(hb[:, m_g * 512:(m_g + 1) * 512],
                                    hb[:, m_g * 512:(m_g + 1) * 512], bank[:, :]),
                       ["hb", bname], ["hb"])
            store_chunk(S["kv"], "d_kv", c, hb, "hb")
            for et in range(DT):
                half = (et % 2) * 512 if T >= 1024 else 0
                OP("vector", lambda E, et=et, h=half: E.tensor_tensor_scan(
                    sc[0][:, h:h + 512], ewc[:, et:et + 1].broadcast_to([128, 512]),
                    hb[:, et * 512:(et + 1) * 512], rwkvS[:, et:et + 1],
                    ALU.mult, ALU.add), ["ewc", "hb", "rwkvS"], ["sc0"])
                OP("gpsimd", lambda E, et=et, h=half: E.tensor_copy(
                    rwkvS[:, et:et + 1], sc[0][:, h + 511:h + 512]),
                    ["sc0"], ["rwkvS"])
                STR(S["hh"][et * 128:(et + 1) * 128, c * 512:(c + 1) * 512],
                    sc[0][:, half:half + 512], ["sc0"], [f"d_hh@{c}"])
            for mb in range(2):
                load_wblk(0, I["WrT"], "i_WrT", DT, mb * 512, (mb + 1) * 512)
                for mt in range(4):
                    m_g = mb * 4 + mt
                    bank, bname = pm[mt % 2], f"pm{mt % 2}"
                    mm_accum(bank[:, :], bname,
                             lambda k, mt=mt: wsl[0][:, k * 512 + mt * 128:k * 512 + mt * 128 + 128],
                             ["wsl0"],
                             lambda k, rs=rslot: rhv(rs, k), [f"rh{rslot}"], DT)
                    OP("scalar", lambda E, m_g=m_g, bank=bank, bname=bname:
                       E.activation(ob[:, m_g * 512:(m_g + 1) * 512], bank[:, :],
                                    ACTF.Sigmoid), [bname], ["ob"])
            store_chunk(S["rr"], "d_rr", c, ob, "ob")

        STR(S["c2ai"][0, :].rearrange("(p c) -> p c", p=128), rwkvS[:, :],
            ["rwkvS"], ["d_c2ai"])
        P.op("gpsimd", lambda E: E.collective_compute(
            "AllGather", ALU.bypass, replica_groups=[list(range(8))],
            ins=[S["c2ai"].opt()], outs=[S["c2ao"].opt()]),
            ["d_c2ai"], ["d_c2ao"], force_inc=True)

        LD(sl[0:NSLOT, 0:D], I["slot0"][:, :], ["i_slot0"], ["sl"])
        for k in range(DT):
            OP("tensor", lambda E, k=k: E.transpose(
                ptr[k % 2][:, 0:NSLOT], sl[0:NSLOT, k * 128:(k + 1) * 128],
                identt[0:NSLOT, 0:NSLOT]), ["sl", "identt"], [f"ptr{k % 2}"])
            OP("vector", lambda E, k=k: E.tensor_copy(
                sl[:, D + k * NSLOT:D + (k + 1) * NSLOT], ptr[k % 2][:, 0:NSLOT]),
                [f"ptr{k % 2}"], ["sl"])
        for k in range(DT):
            OP("vector", lambda E, k=k: E.tensor_reduce(
                sm[:, 64 + k:64 + k + 1], sl[:, D + k * NSLOT:D + (k + 1) * NSLOT],
                axis=AX.X, op=ALU.add), ["sl"], ["sm"])
        OP("vector", lambda E: E.tensor_scalar(
            sm[:, 64:64 + DT], sm[:, 64:64 + DT], cn("invS"), None, ALU.mult),
            ["sm", "consts"], ["sm"])
        LD(wsl[0][:, 0:DT * DH].rearrange("p (k m) -> p k m", k=DT),
           I["w1b"].rearrange("(k p) m -> p k m", p=128), ["i_w1b"], ["wsl0"])
        for mt in range(DH // 128):
            mm_accum(pb[0][:, 0:1], "pb0",
                     lambda k, mt=mt: wsl[0][:, k * DH + mt * 128:k * DH + mt * 128 + 128],
                     ["wsl0"],
                     lambda k: sm[:, 64 + k:64 + k + 1], ["sm"], DT)
            OP("vector", lambda E, mt=mt: E.tensor_scalar(
                bh1[:, mt:mt + 1], pb[0][:, 0:1], vc("b1", mt), None, ALU.add),
                ["pb0", "vecs"], ["bh1"])
        LD(wsl[0][:, 0:DT * DH].rearrange("p (k m) -> p k m", k=DT),
           I["w1a"].rearrange("(k p) m -> p k m", p=128), ["i_w1a"], ["wsl0"])
        LD(wsl[1][:, 0:4 * NSLOT].rearrange("p (k m) -> p k m", k=4),
           I["w2s"].rearrange("(k p) m -> p k m", p=128), ["i_w2s"], ["wsl1"])
        for c in range(NC):
            if c == 1:
                OP("vector", lambda E: E.memset(carry2[:, :], 0.0), (), ["carry2"])
                for j in range(8):
                    LD(sm[:, 48:48 + DT], S["c2ao"][j, :].rearrange(
                        "(p c) -> p c", p=128), ["d_c2ao"], ["sm"])
                    OP("vector", lambda E, j=j: E.tensor_scalar(
                        sm[:, 56:56 + DT], ewTt[:, :], m8t[:, j:j + 1], im8t[:, j:j + 1],
                        ALU.mult, ALU.add), ["ewTt", "m8t", "im8t"], ["sm"])
                    OP("vector", lambda E: E.tensor_mul(carry2[:, :], carry2[:, :],
                                                        sm[:, 56:56 + DT]),
                       ["carry2", "sm"], ["carry2"])
                    OP("vector", lambda E, j=j: E.tensor_scalar(
                        sm[:, 48:48 + DT], sm[:, 48:48 + DT], m8t[:, j:j + 1], None,
                        ALU.mult), ["sm", "m8t"], ["sm"])
                    OP("vector", lambda E: E.tensor_add(carry2[:, :], carry2[:, :],
                                                        sm[:, 48:48 + DT]),
                       ["carry2", "sm"], ["carry2"])

            rslot = c % 2
            load_chunk(rslot, S["xn1"], "d_xn1", c)
            for mt in range(4):
                bank, bname = pm[mt % 2], f"pm{mt % 2}"
                mm_accum(bank[:, :], bname,
                         lambda k, mt=mt: wsl[0][:, k * DH + mt * 128:k * DH + mt * 128 + 128],
                         ["wsl0"],
                         lambda k, rs=rslot: rhv(rs, k), [f"rh{rslot}"], DT)
                OP("scalar", lambda E, mt=mt, bank=bank, bname=bname: E.activation(
                    hb[:, mt * 512:(mt + 1) * 512], bank[:, :], ACTF.Gelu,
                    bias=bh1[:, mt:mt + 1]), [bname, "bh1"], ["hb"])
            mm_accum(pm[2][0:NSLOT, :], "pm2",
                     lambda k: wsl[1][:, k * NSLOT:(k + 1) * NSLOT], ["wsl1"],
                     lambda k: hb[:, k * 512:(k + 1) * 512], ["hb"], 4)
            OP("vector", lambda E: E.tensor_scalar(
                lgc[:, :], pm[2][0:NSLOT, :], b2sb[:, 0:1], None, ALU.add),
                ["pm2", "b2sb"], ["lgc"])
            for tti in range(4):
                tt = c * 4 + tti
                OP("tensor", lambda E, tti=tti: E.transpose(
                    ptr[0][:, 0:NSLOT], lgc[:, tti * 128:(tti + 1) * 128],
                    identt[0:NSLOT, 0:NSLOT]), ["lgc", "identt"], ["ptr0"])
                OP("vector", lambda E: E.tensor_copy(sm[:, 128:128 + NSLOT],
                                                     ptr[0][:, 0:NSLOT]),
                   ["ptr0"], ["sm"])
                OP("vector", lambda E: E.tensor_copy(sm[:, 192:192 + NSLOT],
                                                     sm[:, 128:128 + NSLOT]),
                   ["sm"], ["sm"])
                OP("vector", lambda E: E.reduce_max(sm[:, 112:113],
                                                    sm[:, 192:192 + NSLOT],
                                                    axis=AX.X), ["sm"], ["sm"])
                OP("vector", lambda E: E.tensor_copy(sm[:, 113:114], sm[:, 112:113]),
                   ["sm"], ["sm"])
                for it in range(7):
                    OP("vector", lambda E: E.tensor_scalar(
                        mski[:, 0:NSLOT], sm[:, 192:192 + NSLOT], sm[:, 113:114],
                        None, ALU.is_ge), ["sm"], ["mski"])
                    OP("vector", lambda E: E.copy_predicated(
                        sm[:, 192:192 + NSLOT], mski[:, 0:NSLOT],
                        cn("neg30").broadcast_to([128, NSLOT])),
                        ["sm", "mski", "consts"], ["sm"])
                    OP("vector", lambda E: E.reduce_max(
                        sm[:, 113:114], sm[:, 192:192 + NSLOT], axis=AX.X),
                        ["sm"], ["sm"])
                OP("vector", lambda E: E.tensor_scalar(
                    sm[:, 0:NSLOT], sm[:, 128:128 + NSLOT], sm[:, 113:114], None,
                    ALU.is_ge), ["sm"], ["sm"])
                OP("vector", lambda E: E.tensor_mul(sm[:, 114:115], sm[:, 112:113],
                                                    tiv[:, 3:4]), ["sm", "tiv"],
                   ["sm"])
                OP("vector", lambda E: E.tensor_scalar(
                    sm[:, 114:115], sm[:, 114:115], -1.0, None, ALU.mult),
                    ["sm"], ["sm"])
                OP("scalar", lambda E: E.activation(
                    sm[:, 128:128 + NSLOT], sm[:, 128:128 + NSLOT], ACTF.Exp,
                    scale=tiv[:, 3:4], bias=sm[:, 114:115]), ["sm", "tiv"], ["sm"])
                OP("vector", lambda E: E.tensor_mul(
                    sm[:, 128:128 + NSLOT], sm[:, 128:128 + NSLOT], sm[:, 0:NSLOT]),
                    ["sm"], ["sm"])
                OP("vector", lambda E: E.reduce_sum(
                    sm[:, 115:116], sm[:, 128:128 + NSLOT], axis=AX.X), ["sm"], ["sm"])
                OP("vector", lambda E: E.reciprocal(sm[:, 116:117], sm[:, 115:116]),
                   ["sm"], ["sm"])
                OP("vector", lambda E, tt=tt: E.tensor_scalar(
                    alp[:, tt * NSLOT:(tt + 1) * NSLOT], sm[:, 128:128 + NSLOT],
                    sm[:, 116:117], None, ALU.mult), ["sm"], ["alp"])
                LD(ya[:, 0:D], S["xn1N"][tt * 128:(tt + 1) * 128, :],
                   [f"d_xn1N@{tt}"], ["ya"])
                MM(pb[0][0:NSLOT, :], "pb0", alp[:, tt * NSLOT:(tt + 1) * NSLOT],
                   ["alp"], ya[:, 0:512], ["ya"], tt == 0, tt == NT - 1)
                MM(pb[1][0:NSLOT, :], "pb1", alp[:, tt * NSLOT:(tt + 1) * NSLOT],
                   ["alp"], ya[:, 512:1024], ["ya"], tt == 0, tt == NT - 1)
                MM(pm[3][0:NSLOT, 0:1], "pm3", alp[:, tt * NSLOT:(tt + 1) * NSLOT],
                   ["alp"], oneskt[:, 0:1], ["oneskt"], tt == 0, tt == NT - 1)
        OP("vector", lambda E: E.tensor_copy(sistage[:, 0:512], pb[0][0:NSLOT, :]),
           ["pb0"], ["sistage"])
        OP("vector", lambda E: E.tensor_copy(sistage[:, 512:1024], pb[1][0:NSLOT, :]),
           ["pb1"], ["sistage"])
        OP("vector", lambda E: E.tensor_copy(sistage[:, 1024:1025],
                                             pm[3][0:NSLOT, 0:1]), ["pm3"], ["sistage"])

        STR(S["c2bi"][0, :].rearrange("(p c) -> p c", p=NSLOT), sistage[:, :],
            ["sistage"], ["d_c2bi"])
        P.op("gpsimd", lambda E: E.collective_compute(
            "AllGather", ALU.bypass, replica_groups=[list(range(8))],
            ins=[S["c2bi"].opt()], outs=[S["c2bo"].opt()]),
            ["d_c2bi"], ["d_c2bo"], force_inc=True)

        for mb in range(6):
            ws = mb % 2
            load_wblk(ws, I["WhhT"], "i_WhhT", DT, mb * 512, (mb + 1) * 512)
            for mt in range(4):
                m_g = mb * 4 + mt
                bank, bname = pm[mt % 2], f"pm{mt % 2}"
                mm_accum(bank[:, 0:NSLOT], bname,
                         lambda k, ws=ws, mt=mt:
                             wsl[ws][:, k * 512 + mt * 128:k * 512 + mt * 128 + 128],
                         [f"wsl{ws}"],
                         lambda k: sl[:, D + k * NSLOT:D + (k + 1) * NSLOT],
                         ["sl"], DT)
                OP("vector", lambda E, m_g=m_g, bank=bank, bname=bname:
                   E.tensor_scalar(ob[:, m_g * NSLOT:(m_g + 1) * NSLOT],
                                   bank[:, 0:NSLOT], vc("bhh", m_g), None,
                                   ALU.add), [bname, "vecs"], ["ob"])
        STR(S["gh"][:, :], ob[:, 0:(GRU // 128) * NSLOT], ["ob"], ["d_gh"])

        OP("vector", lambda E: E.memset(hprev[:, :], 0.0), (), ["hprev"])

        def wkv_chunk(c):
            cs = slice(c * 512, (c + 1) * 512)
            oslot = c % 2
            rwb = hb if c % 2 == 0 else sl
            rwn = "hb" if c % 2 == 0 else "sl"
            LD(rh[oslot][:, :DT * 512].rearrange("p (k t) -> p k t", k=DT),
               chunk_of(S["hh"], c), cname("d_hh", c), [f"rh{oslot}"])
            LD(rwb[:, 0:DT * 512].rearrange("p (k t) -> p k t", k=DT),
               chunk_of(S["kv"], c), cname("d_kv", c), [rwn])
            LD(ya[:, 0:DT * 512].rearrange("p (k t) -> p k t", k=DT),
               chunk_of(S["rr"], c), cname("d_rr", c), ["ya"])
            units = []
            for et in range(DT):
                def u(et=et, cs=cs, oslot=oslot, rwb=rwb, rwn=rwn):
                    kvt = rwb[:, et * 512:(et + 1) * 512]
                    hht = rh[oslot][:, et * 512:(et + 1) * 512]
                    rrt = ya[:, et * 512:(et + 1) * 512]
                    pslot = (et % 2) * 512 if T >= 1024 else 0
                    pt = sc[0][:, pslot:pslot + 512]
                    et_ = sc[3][:, pslot:pslot + 512]
                    OP("scalar", lambda E, pt=pt: E.activation(
                        pt, rampt[:, cs], ACTF.Exp,
                        scale=ne1t[:, et:et + 1]), ["rampt", "ne1t"], ["sc0"])
                    OP("vector", lambda E, pt=pt: E.tensor_scalar(
                        pt, pt, carry2[:, et:et + 1], None, ALU.mult),
                        ["sc0", "carry2"], ["sc0"])
                    OP("vector", lambda E, pslot=pslot: E.tensor_add(
                        sc[0][:, pslot + 1:pslot + 512],
                        sc[0][:, pslot + 1:pslot + 512],
                        hht[:, 0:511]), ["sc0", f"rh{oslot}"], ["sc0"])
                    OP("vector", lambda E, pslot=pslot: E.tensor_add(
                        sc[0][:, pslot:pslot + 1], sc[0][:, pslot:pslot + 1],
                        hprev[:, et:et + 1]), ["sc0", "hprev"], ["sc0"])
                    OP("gpsimd", lambda E: E.tensor_copy(
                        hprev[:, et:et + 1], hht[:, 511:512]),
                        [f"rh{oslot}"], ["hprev"])
                    OP("gpsimd", lambda E, et_=et_: E.tensor_tensor(
                        et_, kvt,
                        eut[:, et:et + 1].broadcast_to([128, 512]), ALU.mult),
                        [rwn, "eut"], ["sc3"])
                    OP("vector", lambda E, pt=pt, et_=et_: E.tensor_add(
                        pt, pt, et_), ["sc0", "sc3"], ["sc0"])
                    OP("gpsimd", lambda E, pt=pt: E.tensor_mul(
                        kvt, rrt, pt),
                        ["ya", "sc0", rwn], [rwn])
                units.append(u)
            return units

        def mixer_chunk(c):
            cs = slice(c * 512, (c + 1) * 512)
            rslot = c % 2
            rwb = hb if c % 2 == 0 else sl
            rwn = "hb" if c % 2 == 0 else "sl"
            units = []
            def u_load(rslot=rslot, c=c):
                LD(rh[rslot][:, :DT * 512].rearrange("p (k t) -> p k t", k=DT),
                   chunk_of(S["xn1"], c), cname("d_xn1", c), [f"rh{rslot}"])
            units.append(u_load)
            for mb in range(2):
                def u_w(mb=mb):
                    load_wblk(mb % 2, I["WoT1"], "i_WoT1", DT, mb * 512,
                              (mb + 1) * 512)
                units.append(u_w)
                for mt in range(4):
                    def u_mm(mb=mb, mt=mt, rslot=rslot, rwb=rwb, rwn=rwn):
                        m_g = mb * 4 + mt
                        bank, bname = pm[mt % 2], f"pm{mt % 2}"
                        mm_accum(bank[:, :], bname,
                                 lambda k, ws=mb % 2, mt=mt:
                                     wsl[ws][:, k * 512 + mt * 128:k * 512 + mt * 128 + 128],
                                 [f"wsl{mb % 2}"],
                                 lambda k: rwb[:, k * 512:(k + 1) * 512], [rwn], DT)
                        OP("vector", lambda E, m_g=m_g, bank=bank, bname=bname,
                           rs=rslot: E.scalar_tensor_tensor(
                               ob[:, m_g * 512:(m_g + 1) * 512], bank[:, :],
                               vc("ms", m_g), rhv(rs, m_g), ALU.mult, ALU.add),
                           [bname, "vecs", f"rh{rslot}"], ["ob"])
                    units.append(u_mm)
            def u_tail(c=c, cs=cs, rwb=rwb, rwn=rwn):
                store_chunk(S["hh"], "d_hh", c, ob, "ob")
                for k in range(DT):
                    half = (k % 2) * 512 if T >= 1024 else 0
                    OP("vector", lambda E, k=k, h=half: E.tensor_mul(
                        sc[1][:, h:h + 512], ob[:, k * 512:(k + 1) * 512],
                        ob[:, k * 512:(k + 1) * 512]), ["ob"], ["sc1"])
                    OP("tensor", lambda E, k=k, h=half: E.matmul(
                        pb[1][:, :], oneskt[:], sc[1][:, h:h + 512],
                        start=(k == 0), stop=(k == DT - 1)), ["sc1", "oneskt"],
                        ["pb1"])
                OP("vector", lambda E, cs=cs: E.tensor_scalar(
                    sc[2][:, cs], pb[1][:, :], cn("invD"), None, ALU.mult),
                    ["pb1", "consts"], ["sc2"])
                OP("scalar", lambda E, cs=cs: E.activation(
                    sc[2][:, cs], sc[2][:, cs], ACTF.Ln, bias=cn("eps6")),
                    ["sc2", "consts"], ["sc2"])
                OP("scalar", lambda E, cs=cs: E.activation(
                    sc[2][:, cs], sc[2][:, cs], ACTF.Exp, scale=-0.5),
                    ["sc2"], ["sc2"])
                for k in range(DT):
                    OP("vector", lambda E, k=k, cs=cs, rwb=rwb, rwn=rwn:
                       E.tensor_mul(rwb[:, k * 512:(k + 1) * 512],
                                    ob[:, k * 512:(k + 1) * 512], sc[2][:, cs]),
                       ["ob", "sc2"], [rwn])
                store_chunk(S["xr"], "d_xr", c, rwb, rwn)
            units.append(u_tail)
            return units

        prev_mix = None
        for c in range(NC):
            w_units = wkv_chunk(c)
            m_units = list(prev_mix) if prev_mix else []
            n = max(len(w_units), len(m_units))
            for i in range(n):
                if i < len(w_units):
                    w_units[i]()
                if i < len(m_units):
                    m_units[i]()
            prev_mix = mixer_chunk(c)
        for u in prev_mix:
            u()

        for c in range(NC):
            load_chunk(0, S["xr"], "d_xr", c)
            load_chunk(1, S["hh"], "d_hh", c)
            for q in range(4):
                for fb in range(2):
                    ws = fb
                    f0 = q * 1024 + fb * 512
                    load_wblk(ws, I["WfcT"], "i_WfcT", DT, f0, f0 + 512)
                    for mt in range(4):
                        hm = fb * 4 + mt
                        bank, bname = pm[mt % 2], f"pm{mt % 2}"
                        mm_accum(bank[:, :], bname,
                                 lambda k, ws=ws, mt=mt:
                                     wsl[ws][:, k * 512 + mt * 128:k * 512 + mt * 128 + 128],
                                 [f"wsl{ws}"],
                                 lambda k: rhv(0, k), ["rh0"], DT)
                        OP("vector", lambda E, hm=hm, bank=bank, bname=bname:
                           E.tensor_scalar(
                               hb[:, hm * 512:(hm + 1) * 512], bank[:, :], 0.5,
                               None, ALU.mult), [bname], ["hb"])
                        OP("vector", lambda E, hm=hm, bank=bank, bname=bname:
                           E.tensor_max(
                               hb[:, hm * 512:(hm + 1) * 512],
                               hb[:, hm * 512:(hm + 1) * 512], bank[:, :]),
                           ["hb", bname], ["hb"])
                        OP("vector", lambda E, hm=hm: E.tensor_mul(
                            hb[:, hm * 512:(hm + 1) * 512],
                            hb[:, hm * 512:(hm + 1) * 512],
                            hb[:, hm * 512:(hm + 1) * 512]), ["hb"], ["hb"])
                for mb in range(2):
                    ws = mb
                    LD(wsl[ws][:, 0:8 * 512].rearrange("p (k m) -> p k m", k=8),
                       I["WmlpT"].rearrange("(k p) m -> p k m", p=128)
                       [:, q * 8:q * 8 + 8, mb * 512:(mb + 1) * 512],
                       ["i_WmlpT"], [f"wsl{ws}"])
                    for mt in range(4):
                        m_g = mb * 4 + mt
                        bank, bname = pm[2 + (m_g % 2)], f"pm{2 + (m_g % 2)}"
                        mm_accum(bank[:, :], bname,
                                 lambda k, ws=ws, mt=mt:
                                     wsl[ws][:, k * 512 + mt * 128:k * 512 + mt * 128 + 128],
                                 [f"wsl{ws}"],
                                 lambda k: hb[:, k * 512:(k + 1) * 512], ["hb"], 8)
                        if q == 0:
                            OP("vector", lambda E, m_g=m_g, bank=bank, bname=bname:
                               E.tensor_copy(ob[:, m_g * 512:(m_g + 1) * 512],
                                             bank[:, :]), [bname], ["ob"])
                        else:
                            OP("vector", lambda E, m_g=m_g, bank=bank, bname=bname:
                               E.tensor_add(ob[:, m_g * 512:(m_g + 1) * 512],
                                            ob[:, m_g * 512:(m_g + 1) * 512],
                                            bank[:, :]), ["ob", bname], ["ob"])
            for mt in range(DT):
                OP("vector", lambda E, mt=mt: E.scalar_tensor_tensor(
                    ob[:, mt * 512:(mt + 1) * 512], ob[:, mt * 512:(mt + 1) * 512],
                    vc("mls", mt), rhv(1, mt), ALU.mult, ALU.add),
                    ["ob", "vecs", "rh1"], ["ob"])
            store_chunk(S["s1"], "d_s1", c, ob, "ob")

        OP("vector", lambda E: E.memset(sistage[:, :], 0.0), ("d_c2bi",), ["sistage"])
        for j in range(8):
            LD(sigstg[:, :], S["c2bo"][j, :]
               .rearrange("(p c) -> p c", p=NSLOT), ["d_c2bo"], ["sigstg"])
            OP("vector", lambda E, j=j: E.scalar_tensor_tensor(
                sistage[:, :], sigstg[:, :], mb8ct[:, j:j + 1],
                sistage[:, :], ALU.mult, ALU.add),
                ["sigstg", "mb8ct", "sistage"], ["sistage"])

        LD(sl[0:NSLOT, 0:D], I["slot0"][:, :], ["i_slot0"], ["sl"])
        for k in range(DT):
            OP("tensor", lambda E, k=k: E.transpose(
                ptr[k % 2][:, 0:NSLOT], sl[0:NSLOT, k * 128:(k + 1) * 128],
                identt[0:NSLOT, 0:NSLOT]), ["sl", "identt"], [f"ptr{k % 2}"])
            OP("vector", lambda E, k=k: E.tensor_copy(
                sl[:, D + k * NSLOT:D + (k + 1) * NSLOT], ptr[k % 2][:, 0:NSLOT]),
                [f"ptr{k % 2}"], ["sl"])
        OP("vector", lambda E: E.tensor_scalar(
            ss64[:, 0:1], sistage[:, 1024:1025], cn("eps8", NSLOT), None, ALU.add),
            ["sistage", "consts"], ["ss64"])
        OP("vector", lambda E: E.reciprocal(ss64[:, 1:2], ss64[:, 0:1]),
           ["ss64"], ["ss64"])
        OP("vector", lambda E: E.tensor_scalar(
            sl[0:NSLOT, 0:D], sistage[:, 0:D], ss64[:, 1:2], None, ALU.mult),
            ["sistage", "ss64"], ["sl"])
        for k in range(DT):
            OP("tensor", lambda E, k=k: E.transpose(
                ptr[k % 2][:, 0:NSLOT], sl[0:NSLOT, k * 128:(k + 1) * 128],
                identt[0:NSLOT, 0:NSLOT]), ["sl", "identt"], [f"ptr{k % 2}"])
            OP("vector", lambda E, k=k: E.tensor_copy(
                sl[:, 1536 + k * NSLOT:1536 + (k + 1) * NSLOT], ptr[k % 2][:, 0:NSLOT]),
                [f"ptr{k % 2}"], ["sl"])
        LD(hb[:, 1536:3072], S["gh"][:, :], ["d_gh"], ["hb"])
        for which, Wn, rbase, bias_nm, hb0 in [
            (0, "WihT", 1536, "bih", 0)]:
            for mb in range(6):
                ws = mb % 2
                load_wblk(ws, I[Wn], "i_" + Wn, DT, mb * 512, (mb + 1) * 512)
                for mt in range(4):
                    m_g = mb * 4 + mt
                    bank, bname = pm[mt % 2], f"pm{mt % 2}"
                    mm_accum(bank[:, 0:NSLOT], bname,
                             lambda k, ws=ws, mt=mt:
                                 wsl[ws][:, k * 512 + mt * 128:k * 512 + mt * 128 + 128],
                             [f"wsl{ws}"],
                             lambda k, rb=rbase: sl[:, rb + k * NSLOT:rb + (k + 1) * NSLOT],
                             ["sl"], DT)
                    OP("vector", lambda E, m_g=m_g, bank=bank, bname=bname,
                       bias_nm=bias_nm, hb0=hb0: E.tensor_scalar(
                           hb[:, hb0 + m_g * NSLOT:hb0 + (m_g + 1) * NSLOT],
                           bank[:, 0:NSLOT], vc(bias_nm, m_g), None, ALU.add),
                       [bname, "vecs"], ["hb"])
        OP("vector", lambda E: E.tensor_add(hb[:, 0:512], hb[:, 0:512],
                                            hb[:, 1536:2048]), ["hb"], ["hb"])
        OP("scalar", lambda E: E.activation(hb[:, 0:512], hb[:, 0:512], ACTF.Sigmoid),
           ["hb"], ["hb"])
        OP("vector", lambda E: E.tensor_add(hb[:, 512:1024], hb[:, 512:1024],
                                            hb[:, 2048:2560]), ["hb"], ["hb"])
        OP("scalar", lambda E: E.activation(hb[:, 512:1024], hb[:, 512:1024],
                                            ACTF.Sigmoid), ["hb"], ["hb"])
        OP("vector", lambda E: E.tensor_mul(hb[:, 2560:3072], hb[:, 0:512],
                                            hb[:, 2560:3072]), ["hb"], ["hb"])
        OP("vector", lambda E: E.tensor_add(hb[:, 1024:1536], hb[:, 1024:1536],
                                            hb[:, 2560:3072]), ["hb"], ["hb"])
        OP("scalar", lambda E: E.activation(hb[:, 1024:1536], hb[:, 1024:1536],
                                            ACTF.Tanh), ["hb"], ["hb"])
        OP("vector", lambda E: E.tensor_sub(hb[:, 1536:2048], sl[:, 1024:1536],
                                            hb[:, 1024:1536]), ["sl", "hb"], ["hb"])
        OP("vector", lambda E: E.tensor_mul(hb[:, 1536:2048], hb[:, 512:1024],
                                            hb[:, 1536:2048]), ["hb"], ["hb"])
        OP("vector", lambda E: E.tensor_add(sl[:, 2048:2560], hb[:, 1024:1536],
                                            hb[:, 1536:2048]), ["hb"], ["sl"])
        for Wn, M, bias_nm, dst0 in [("wq_s", DQ, "bq", 2560),
                                     ("wk_s", DQ, "bk", 2688),
                                     ("wv_s", D, "bv", 2816)]:
            for mb in range((M + 511) // 512):
                mcols = min(512, M - mb * 512)
                ws = mb % 2
                load_wblk(ws, I[Wn], "i_" + Wn, DT, mb * 512, mb * 512 + mcols)
                for mt in range(mcols // 128):
                    m_g = mb * 4 + mt
                    bank, bname = pm[mt % 2], f"pm{mt % 2}"
                    mm_accum(bank[:, 0:NSLOT], bname,
                             lambda k, ws=ws, mt=mt, mc=mcols:
                                 wsl[ws][:, k * mc + mt * 128:k * mc + mt * 128 + 128],
                             [f"wsl{ws}"],
                             lambda k: sl[:, 2048 + k * NSLOT:2048 + (k + 1) * NSLOT],
                             ["sl"], DT)
                    OP("vector", lambda E, m_g=m_g, bank=bank, bname=bname,
                       bias_nm=bias_nm, dst0=dst0: E.tensor_scalar(
                           sl[:, dst0 + m_g * NSLOT:dst0 + (m_g + 1) * NSLOT],
                           bank[:, 0:NSLOT], vc(bias_nm, m_g), None, ALU.add),
                       [bname, "vecs"], ["sl"])
        for k in range(2):
            MM(ptr[0][0:NSLOT, 0:NSLOT], "ptr0",
               sl[:, 2560 + k * NSLOT:2560 + (k + 1) * NSLOT], ["sl"],
               sl[:, 2688 + k * NSLOT:2688 + (k + 1) * NSLOT], ["sl"],
               k == 0, k == 1)
        OP("vector", lambda E: E.tensor_copy(sl[0:NSLOT, 3328:3392],
                                             ptr[0][0:NSLOT, 0:NSLOT]),
           ["ptr0"], ["sl"])
        OP("vector", lambda E: E.reduce_max(ss64[:, 2:3], sl[0:NSLOT, 3328:3392],
                                            axis=AX.X), ["sl"], ["ss64"])
        OP("vector", lambda E: E.tensor_mul(ss64[:, 3:4], ss64[:, 2:3],
                                            cn("nscq", NSLOT)),
           ["ss64", "consts"], ["ss64"])
        OP("scalar", lambda E: E.activation(
            sl[0:NSLOT, 3328:3392], sl[0:NSLOT, 3328:3392], ACTF.Exp,
            scale=cn("scq", NSLOT), bias=ss64[:, 3:4]),
            ["sl", "consts", "ss64"], ["sl"])
        OP("vector", lambda E: E.reduce_sum(ss64[:, 4:5], sl[0:NSLOT, 3328:3392],
                                            axis=AX.X), ["sl"], ["ss64"])
        OP("vector", lambda E: E.reciprocal(ss64[:, 5:6], ss64[:, 4:5]),
           ["ss64"], ["ss64"])
        OP("vector", lambda E: E.tensor_scalar(
            sl[0:NSLOT, 3328:3392], sl[0:NSLOT, 3328:3392], ss64[:, 5:6], None,
            ALU.mult), ["sl", "ss64"], ["sl"])
        OP("tensor", lambda E: E.transpose(
            ptr[1][0:NSLOT, 0:NSLOT], sl[0:NSLOT, 3328:3392],
            identt[0:NSLOT, 0:NSLOT]), ["sl", "identt"], ["ptr1"])
        OP("vector", lambda E: E.tensor_copy(sl[0:NSLOT, 3392:3456],
                                             ptr[1][0:NSLOT, 0:NSLOT]),
           ["ptr1"], ["sl"])
        for k in range(DT):
            OP("tensor", lambda E, k=k: E.transpose(
                ptr[k % 2][0:NSLOT, 0:128], sl[:, 2816 + k * NSLOT:2816 + (k + 1) * NSLOT],
                identt[:]), ["sl", "identt"], [f"ptr{k % 2}"])
            OP("vector", lambda E, k=k: E.tensor_copy(
                hb[0:NSLOT, k * 128:(k + 1) * 128], ptr[k % 2][0:NSLOT, 0:128]),
                [f"ptr{k % 2}"], ["hb"])
        for m in range(DT):
            MM(pm[0][:, 0:NSLOT], "pm0", hb[0:NSLOT, m * 128:(m + 1) * 128], ["hb"],
               sl[0:NSLOT, 3392:3456], ["sl"], True, True)
            OP("vector", lambda E, m=m: E.tensor_add(
                hb[:, 1024 + m * NSLOT:1024 + (m + 1) * NSLOT],
                sl[:, 2048 + m * NSLOT:2048 + (m + 1) * NSLOT], pm[0][:, 0:NSLOT]),
                ["sl", "pm0"], ["hb"])
        for k in range(DT):
            OP("tensor", lambda E, k=k: E.matmul(
                pb[0][:, 0:NSLOT], oneskt[:],
                hb[:, 1024 + k * NSLOT:1024 + (k + 1) * NSLOT],
                start=(k == 0), stop=(k == DT - 1)), ["hb", "oneskt"], ["pb0"])
        for k in range(DT):
            OP("vector", lambda E, k=k: E.tensor_mul(
                hb[:, 3584 + k % 2 * NSLOT:3584 + (k % 2 + 1) * NSLOT],
                hb[:, 1024 + k * NSLOT:1024 + (k + 1) * NSLOT],
                hb[:, 1024 + k * NSLOT:1024 + (k + 1) * NSLOT]), ["hb"], ["hb"])
            OP("tensor", lambda E, k=k: E.matmul(
                pb[1][:, 0:NSLOT], oneskt[:],
                hb[:, 3584 + k % 2 * NSLOT:3584 + (k % 2 + 1) * NSLOT],
                start=(k == 0), stop=(k == DT - 1)), ["hb", "oneskt"], ["pb1"])
        OP("vector", lambda E: E.tensor_scalar(
            hb[:, 3712:3776], pb[0][:, 0:NSLOT], cn("invD"), None, ALU.mult),
            ["pb0", "consts"], ["hb"])
        OP("vector", lambda E: E.tensor_scalar(
            hb[:, 3776:3840], pb[1][:, 0:NSLOT], cn("invD"), None, ALU.mult),
            ["pb1", "consts"], ["hb"])
        OP("vector", lambda E: E.tensor_mul(hb[:, 3840:3904], hb[:, 3712:3776],
                                            hb[:, 3712:3776]), ["hb"], ["hb"])
        OP("vector", lambda E: E.tensor_sub(hb[:, 3776:3840], hb[:, 3776:3840],
                                            hb[:, 3840:3904]),
           ["hb", "hb"], ["hb"])
        OP("scalar", lambda E: E.activation(hb[:, 3776:3840], hb[:, 3776:3840],
                                            ACTF.Ln, bias=cn("eps5")),
           ["hb", "consts"], ["hb"])
        OP("scalar", lambda E: E.activation(hb[:, 3776:3840], hb[:, 3776:3840],
                                            ACTF.Exp, scale=-0.5), ["hb"], ["hb"])
        for k in range(DT):
            OP("vector", lambda E, k=k: E.tensor_sub(
                hb[:, 1024 + k * NSLOT:1024 + (k + 1) * NSLOT],
                hb[:, 1024 + k * NSLOT:1024 + (k + 1) * NSLOT], hb[:, 3712:3776]),
                ["hb", "hb"], ["hb"])
            OP("vector", lambda E, k=k: E.tensor_mul(
                hb[:, 1024 + k * NSLOT:1024 + (k + 1) * NSLOT],
                hb[:, 1024 + k * NSLOT:1024 + (k + 1) * NSLOT], hb[:, 3776:3840]),
                ["hb", "hb"], ["hb"])
            OP("vector", lambda E, k=k: E.tensor_scalar(
                hb[:, 1024 + k * NSLOT:1024 + (k + 1) * NSLOT],
                hb[:, 1024 + k * NSLOT:1024 + (k + 1) * NSLOT],
                vc("lng", k), vc("lnb", k), ALU.mult, ALU.add), ["hb", "vecs"], ["hb"])
        for mb in range(2):
            ws = mb
            load_wblk(ws, I["wvp_s"], "i_wvp_s", DT, mb * 512, (mb + 1) * 512)
            for mt in range(4):
                m_g = mb * 4 + mt
                bank, bname = pm[mt % 2], f"pm{mt % 2}"
                mm_accum(bank[:, 0:NSLOT], bname,
                         lambda k, ws=ws, mt=mt:
                             wsl[ws][:, k * 512 + mt * 128:k * 512 + mt * 128 + 128],
                         [f"wsl{ws}"],
                         lambda k: hb[:, 1024 + k * NSLOT:1024 + (k + 1) * NSLOT],
                         ["hb"], DT)
                OP("vector", lambda E, m_g=m_g, bank=bank, bname=bname:
                   E.tensor_scalar(
                       sl[:, 2560 + m_g * NSLOT:2560 + (m_g + 1) * NSLOT],
                       bank[:, 0:NSLOT], vc("bvp", m_g), None, ALU.add),
                   [bname, "vecs"], ["sl"])
        for k in range(DT):
            OP("tensor", lambda E, k=k: E.transpose(
                ptr[k % 2][0:NSLOT, 0:128], sl[:, 2560 + k * NSLOT:2560 + (k + 1) * NSLOT],
                identt[:]), ["sl", "identt"], [f"ptr{k % 2}"])
            OP("vector", lambda E, k=k: E.tensor_copy(
                hb[0:NSLOT, 1536 + k * 128:1536 + (k + 1) * 128],
                ptr[k % 2][0:NSLOT, 0:128]), [f"ptr{k % 2}"], ["hb"])
        for tt in range(NT):
            OP("tensor", lambda E, tt=tt: E.transpose(
                ptr[0][0:NSLOT, 0:128], alp[:, tt * NSLOT:(tt + 1) * NSLOT],
                identt[:]), ["alp", "identt"], ["ptr0"])
            OP("vector", lambda E: E.tensor_copy(sl[0:NSLOT, 3456:3584],
                                                 ptr[0][0:NSLOT, 0:128]),
               ["ptr0"], ["sl"])
            for m in range(DT):
                MM(pm[1][:, 0:128], "pm1", hb[0:NSLOT, 1536 + m * 128:1536 + (m + 1) * 128],
                   ["hb"], sl[0:NSLOT, 3456:3584], ["sl"], True, True)
                OP("vector", lambda E, m=m: E.tensor_copy(
                    ob[:, m * 128:(m + 1) * 128], pm[1][:, 0:128]), ["pm1"], ["ob"])
            store_chunk(S["sv"], "d_sv", tt, ob, "ob", width=128)
        def evac_g1(mt, c, bank, bname):
            OP("scalar", lambda E, mt=mt, bank=bank, bname=bname: E.activation(
                ob[:, mt * 512:(mt + 1) * 512], bank[:, :], ACTF.Silu,
                bias=vc("bo1", mt)), [bname, "vecs"], ["ob"])
            if mt == DT - 1:
                store_chunk(S["g1"], "d_g1", c, ob, "ob")
        big_mm(I["wo_s"], "i_wo_s", D, D, S["sv"], "d_sv", evac_g1)

        for c in range(NC):
            load_chunk(0, S["s1"], "d_s1", c)
            load_chunk(1, S["g1"], "d_g1", c)
            LD(hb[:, 0:DT * 512].rearrange("p (k t) -> p k t", k=DT),
               chunk_of(S["x1"], c), cname("d_x1", c), ["hb"])
            for k in range(DT):
                OP("vector", lambda E, k=k: E.tensor_mul(
                    ob[:, k * 512:(k + 1) * 512], rhv(0, k), rhv(1, k)),
                    ["rh0", "rh1"], ["ob"])
                OP("vector", lambda E, k=k: E.tensor_add(
                    ob[:, k * 512:(k + 1) * 512], ob[:, k * 512:(k + 1) * 512],
                    hb[:, k * 512:(k + 1) * 512]), ["ob", "hb"], ["ob"])
            store_chunk(outT, "o_outT", c, ob, "ob")
        OP("vector", lambda E: E.engine_nop(), ["o_outT"], ())

        P.resolve()
        run = P.make_runner(sems)

        with nc.Block() as block:
            @block.tensor
            def _(eng):
                run("tensor", eng)

            @block.vector
            def _(eng):
                run("vector", eng)

            @block.scalar
            def _(eng):
                run("scalar", eng)

            @block.sync
            def _(eng):
                run("sync", eng)

            @block.gpsimd
            def _(eng):
                run("gpsimd", eng)

    return nc


def _vec_tiles(v, ntiles):
    return np.ascontiguousarray(v.reshape(ntiles, 128).T.astype(np.float32))


def _mat_tiles(v, ntiles, n2):
    return np.ascontiguousarray(
        v.reshape(ntiles, 128, n2).transpose(1, 0, 2).reshape(128, ntiles * n2)
        .astype(np.float32))


_CACHED = {}


def kernel(**inputs):
    B, Tfull, _ = inputs["L0_attn_x"].shape
    ncores = 8
    per_batch = ncores // B
    T = Tfull // per_batch

    key = T
    if key not in _CACHED:
        _CACHED[key] = build_program(T)
    nc = _CACHED[key]

    f32 = np.float32
    g = {k: np.asarray(v) for k, v in inputs.items()}

    common = {}
    common["WdT"] = np.ascontiguousarray(g["L0_attn_Wd"].T.astype(f32))
    common["WoT0"] = np.ascontiguousarray(g["L0_attn_Wo"].T.astype(f32))
    common["rwT"] = np.ascontiguousarray(g["L0_ffn_router_w"].T.astype(f32))
    common["moe_w1"] = np.ascontiguousarray(g["L0_ffn_w1"].astype(f32))
    common["moe_w2"] = np.ascontiguousarray(g["L0_ffn_w2"].astype(f32))
    common["WkT"] = np.ascontiguousarray(g["L1_attn_Wk"].T.astype(f32))
    common["WvT"] = np.ascontiguousarray(g["L1_attn_Wv"].T.astype(f32))
    common["WrT"] = np.ascontiguousarray(g["L1_attn_Wr"].T.astype(f32))
    common["WoT1"] = np.ascontiguousarray(g["L1_attn_Wo"].T.astype(f32))
    common["WfcT"] = np.ascontiguousarray(g["L1_attn_Wfc"].T.astype(f32))
    common["WmlpT"] = np.ascontiguousarray(g["L1_attn_Wmlp"].T.astype(f32))
    common["slot0"] = np.ascontiguousarray(g["L1_ffn_slot_init"].astype(f32))
    common["w1a"] = np.ascontiguousarray(g["L1_ffn_w1"][:D].astype(f32))
    common["w1b"] = np.ascontiguousarray(g["L1_ffn_w1"][D:].astype(f32))
    common["w2s"] = np.ascontiguousarray(g["L1_ffn_w2"].astype(f32))
    common["WihT"] = np.ascontiguousarray(g["L1_ffn_Wih"].T.astype(f32))
    common["WhhT"] = np.ascontiguousarray(g["L1_ffn_Whh"].T.astype(f32))
    common["wq_s"] = np.ascontiguousarray(g["L1_ffn_wq"].astype(f32))
    common["wk_s"] = np.ascontiguousarray(g["L1_ffn_wk"].astype(f32))
    common["wv_s"] = np.ascontiguousarray(g["L1_ffn_wv"].astype(f32))
    common["wvp_s"] = np.ascontiguousarray(g["L1_ffn_wvp"].astype(f32))
    common["wo_s"] = np.ascontiguousarray(g["L1_ffn_wo"].astype(f32))

    common["bd_t"] = _vec_tiles(g["L0_attn_bd"], DT)
    common["bo_t"] = _vec_tiles(g["L0_attn_bo"], DT)
    common["a_log_t"] = _mat_tiles(g["L0_attn_a_log"], DT, NST)
    common["bprm_t"] = _mat_tiles(g["L0_attn_b_param"], DT, NST)
    common["td_t"] = _vec_tiles(g["L1_attn_time_decay"], DT)
    common["tf_t"] = _vec_tiles(g["L1_attn_time_first"], DT)
    common["ms_t"] = _vec_tiles(g["L1_attn_mixer_scale"], DT)
    common["mls_t"] = _vec_tiles(g["L1_attn_mlp_scale"], DT)
    common["b1_t"] = _vec_tiles(g["L1_ffn_b1"], DH // 128)
    common["b2_c"] = np.ascontiguousarray(
        g["L1_ffn_b2"].reshape(NSLOT, 1).astype(f32))
    common["tau_c"] = np.ascontiguousarray(
        g["L1_ffn_tau"].reshape(1, 1).astype(f32))
    common["bih_t"] = _vec_tiles(g["L1_ffn_bih"], GRU // 128)
    common["bhh_t"] = _vec_tiles(g["L1_ffn_bhh"], GRU // 128)
    common["bq_t"] = _vec_tiles(g["L1_ffn_bq"], DQ // 128)
    common["bk_t"] = _vec_tiles(g["L1_ffn_bk"], DQ // 128)
    common["bv_t"] = _vec_tiles(g["L1_ffn_bv"], DT)
    common["lng_t"] = _vec_tiles(g["L1_ffn_ln_g"], DT)
    common["lnb_t"] = _vec_tiles(g["L1_ffn_ln_b"], DT)
    common["bvp_t"] = _vec_tiles(g["L1_ffn_bvp"], DT)
    common["bo1_t"] = _vec_tiles(g["L1_ffn_bo"], DT)

    common["ident"] = np.eye(128, dtype=f32)
    common["onesk"] = np.ones((128, 128), dtype=f32)
    sel8 = np.zeros((NEXP, 128 * NEXP), dtype=f32)
    for e in range(NEXP):
        sel8[e, e * 128:(e + 1) * 128] = 1.0
    common["sel8"] = sel8
    common["ramp0"] = np.broadcast_to(
        np.arange(T, dtype=f32)[None, :], (128, T)).copy()
    common["ramp256"] = np.broadcast_to(
        np.arange(256, dtype=f32)[None, :], (128, 256)).copy()

    x = g["L0_attn_x"].astype(f32)
    in_maps = []
    for cid in range(ncores):
        b = cid // per_batch
        r = cid % per_batch
        m = dict(common)
        xs = x[b, r * T:(r + 1) * T, :]
        m["x0T"] = np.ascontiguousarray(xs.T)
        m8 = np.zeros((128, 8), dtype=f32)
        mb8 = np.zeros((NSLOT, 8), dtype=f32)
        for j in range(8):
            if j // per_batch == b:
                mb8[:, j] = 1.0
                if j < cid:
                    m8[:, j] = 1.0
        m["m8"] = m8
        m["im8"] = 1.0 - m8
        m["mb8c"] = mb8
        in_maps.append(m)

    res = run_bass_kernel_spmd(nc, in_maps, core_ids=list(range(ncores)))
    out = np.empty((B, Tfull, D), dtype=f32)
    for cid in range(ncores):
        b = cid // per_batch
        r = cid % per_batch
        out[b, r * T:(r + 1) * T, :] = res.results[cid]["outT"].T
    return out
